# version 16
# baseline (speedup 1.0000x reference)
import numpy as np
from contextlib import ExitStack

import concourse.bass as bass
import concourse.mybir as mybir
from concourse.bass_utils import run_bass_kernel_spmd

F32 = mybir.dt.float32
BF16 = mybir.dt.bfloat16
AF = mybir.ActivationFunctionType
ALU = mybir.AluOpType
AX = mybir.AxisListType

P = 128
T = 2048
D = 1024
DB = D // P
NT = T // 512
N_CORES = 8
EPS_LN = 1e-5
EPS_RMS = 1e-6
NEXP = 14
INNER = 2048
IBL = INNER // P
DSTATE = 16
DTRANK = 64

ENGINES = ("tensor", "scalar", "vector", "pool", "sync")


class Prog:

    def __init__(self, nc, es):
        self.nc = nc
        self.es = es
        self.ops = {e: [] for e in ENGINES}
        self._sems = {}
        self.counts = {}
        self.scope = None
        self.ldsem = "ld"

    def sem(self, name):
        if name not in self._sems:
            self._sems[name] = self.es.enter_context(self.nc.semaphore(name))
            self.counts[name] = 0
        return self._sems[name]

    def emit(self, eng, f, sem=None, amt=1):
        if sem is not None:
            h = self.sem(sem)
            self.counts[sem] += amt

            def g(e, f=f, h=h, amt=amt):
                ins = f(e)
                ins.then_inc(h, amt)
                return ins
            self.ops[eng].append((self.scope, g))
            return self.counts[sem]
        self.ops[eng].append((self.scope, f))
        return None

    def tag(self, eng, sem, amt=1):
        h = self.sem(sem)
        self.counts[sem] += amt
        assert self.ops[eng], f"tag on empty stream {eng}"
        lbl, f = self.ops[eng].pop()

        def g(e, f=f, h=h, amt=amt):
            ins = f(e)
            assert ins is not None, "tagged op returned no instruction"
            ins.then_inc(h, amt)
        self.ops[eng].append((lbl, g))
        return self.counts[sem]

    def wait(self, eng, sem, val=None):
        h = self.sem(sem)
        v = self.counts[sem] if val is None else val

        def g(e, h=h, v=v):
            e.wait_ge(h, v)
        self.ops[eng].append((self.scope, g))

    def barrier(self, tag, engines=ENGINES):
        name = f"bar_{tag}"
        for eng in engines:
            self.emit(eng, lambda e: e.nop(), sem=name)
        tgt = self.counts[name]
        for eng in engines:
            self.wait(eng, name, tgt)

    def sb(self, name, shape, dtype=F32, es=None):
        return (es or self.es).enter_context(
            self.nc.sbuf_tensor(name, shape, dtype))

    def ps(self, name, shape=(P, 512)):
        return self.es.enter_context(
            self.nc.psum_tensor(name, list(shape), F32))

    def _replay(self, eng_name, e):
        nc = self.nc
        cur = None
        for lbl, f in self.ops[eng_name]:
            if lbl != (cur[0] if cur else None):
                if cur is not None:
                    nc.leave_named_scope(cur[0], cur[1], False)
                    cur = None
                if lbl is not None:
                    sid, _ = nc.enter_named_scope(lbl, False)
                    cur = (lbl, sid)
            f(e)
        if cur is not None:
            nc.leave_named_scope(cur[0], cur[1], False)

    def finalize(self):
        with self.nc.Block() as block:
            @block.tensor
            def _(e):
                self._replay("tensor", e)

            @block.scalar
            def _(e):
                self._replay("scalar", e)

            @block.vector
            def _(e):
                self._replay("vector", e)

            @block.gpsimd
            def _(e):
                self._replay("pool", e)

            @block.sync
            def _(e):
                self._replay("sync", e)



def _cols(v):
    return np.ascontiguousarray(v.reshape(-1, P).T.astype(np.float32))


def common_consts():
    c = {}
    c["ones1"] = np.ones((1, P), dtype=np.float32)
    c["ones_c"] = np.ones((P, 1), dtype=np.float32)
    c["ident"] = np.eye(P, dtype=np.float32)
    oh = np.zeros((NEXP, NEXP * P), dtype=np.float32)
    for e in range(NEXP):
        oh[e, e * P:(e + 1) * P] = 1.0
    c["onehot14"] = oh
    return c



def emit_ln(pr, xsb, out_bf, out_f32, sc, rms=False, gamma_col=None,
            eps=EPS_LN):
    rep, p0, p1 = sc["rep"], sc["p0"], sc["p1"]
    rowsB = sc["rowsB"]
    xb, sq = sc["xb"], sc["sq"]
    m_r = rowsB[0:1, 0:T]
    v_r = rowsB[0:1, T:2 * T]
    rstd_r = rowsB[0:1, 2 * T:3 * T]
    mrstd_r = rowsB[0:1, 3 * T:4 * T]
    msq_r = mrstd_r

    if xb is not None:
        for b in range(DB):
            pr.emit("vector", lambda e, b=b: e.tensor_copy(
                xb[:, b * T:(b + 1) * T], xsb[:, b * T:(b + 1) * T]))
        pr.tag("vector", "dve")
        pr.wait("scalar", "dve")
        pr.wait("tensor", "dve")
    stat_src = xb if xb is not None else xsb
    stat_ones = sc["ones_cb"] if xb is not None else sc["ones_cf"]

    slot_free_at = [pr.counts.get("pe", 0)] * 2
    for n in range(NT):
        for b in range(DB):
            pr.emit("tensor", lambda e, b=b, n=n: e.matmul(
                p0[0:1, :], stat_ones[:, 0:1],
                stat_src[:, b * T + n * 512: b * T + (n + 1) * 512],
                start=(b == 0), stop=(b == DB - 1)),
                sem="pe" if b == DB - 1 else None)
        for b in range(DB):
            s = b % 2
            pr.wait("scalar", "pe", slot_free_at[s])
            pr.emit("scalar", lambda e, b=b, n=n, s=s: e.activation(
                sq[:, s * 512:(s + 1) * 512],
                stat_src[:, b * T + n * 512: b * T + (n + 1) * 512],
                AF.Square), sem="act")
            pr.wait("tensor", "act")
            slot_free_at[s] = pr.emit("tensor", lambda e, s=s, b=b: e.matmul(
                p1[0:1, :], stat_ones[:, 0:1], sq[:, s * 512:(s + 1) * 512],
                start=(b == 0), stop=(b == DB - 1)), sem="pe")
        pr.wait("vector", "pe")
        pr.emit("vector", lambda e, n=n: e.tensor_copy(
            m_r[:, n * 512:(n + 1) * 512], p0[0:1, :]), sem="dve")
        pr.emit("vector", lambda e, n=n: e.tensor_copy(
            v_r[:, n * 512:(n + 1) * 512], p1[0:1, :]), sem="dve")
        pr.wait("tensor", "dve")

    def _vs():
        pr.tag("vector", "dve")
        pr.wait("vector", "dve")

    pr.wait("vector", "dve")
    if not rms:
        pr.emit("vector", lambda e: e.tensor_scalar(
            m_r, m_r, 1.0 / D, None, op0=ALU.mult))
        _vs()
        pr.emit("vector", lambda e: e.tensor_mul(msq_r, m_r, m_r))
        _vs()
        pr.emit("vector", lambda e: e.tensor_scalar(
            v_r, v_r, 1.0 / D, eps, op0=ALU.mult, op1=ALU.add))
        _vs()
        pr.emit("vector", lambda e: e.tensor_sub(
            v_r, v_r, msq_r), sem="dve")
    else:
        pr.emit("vector", lambda e: e.tensor_scalar(
            v_r, v_r, 1.0 / D, eps, op0=ALU.mult, op1=ALU.add), sem="dve")
    pr.wait("scalar", "dve")
    pr.emit("scalar", lambda e: e.activation(rstd_r, v_r, AF.Ln))
    pr.tag("scalar", "act")
    pr.wait("scalar", "act")
    pr.emit("scalar", lambda e: e.activation(
        rstd_r, rstd_r, AF.Exp, scale=-0.5), sem="act")
    if not rms:
        pr.wait("vector", "act")
        pr.emit("vector", lambda e: e.tensor_mul(
            mrstd_r, m_r, rstd_r), sem="dve")

    pr.wait("tensor", "act")
    if not rms:
        pr.wait("tensor", "dve")
    for n in range(NT):
        pr.emit("tensor", lambda e, n=n: e.matmul(
            p0[:, :], sc["ones_f"][0:1, :], rstd_r[:, n * 512:(n + 1) * 512],
            start=True, stop=True), sem="pe" if rms else None)
        if not rms:
            pr.emit("tensor", lambda e, n=n: e.matmul(
                p1[:, :], sc["ones_f"][0:1, :],
                mrstd_r[:, n * 512:(n + 1) * 512],
                start=True, stop=True), sem="pe")
        pr.wait("vector", "pe")
        pr.emit("vector", lambda e, n=n: e.tensor_copy(
            rep[:, n * 512:(n + 1) * 512], p0[:, :]), sem="dve")
        if not rms:
            pr.emit("vector", lambda e, n=n: e.tensor_copy(
                rep[:, T + n * 512:T + (n + 1) * 512], p1[:, :]), sem="dve")
        pr.wait("tensor", "dve")

    pr.wait("vector", "dve")
    rstd_rep = rep[:, 0:T]
    mrstd_rep = rep[:, T:2 * T]
    for b in range(DB):
        src = xsb[:, b * T:(b + 1) * T]
        if out_f32 is not None:
            dstf = out_f32[:, b * T:(b + 1) * T]
            if not rms:
                pr.emit("vector", lambda e, src=src, dstf=dstf: e.tensor_sub(
                    dstf, src, mrstd_rep))
                _vs()
                pr.emit("vector", lambda e, dstf=dstf: e.tensor_mul(
                    dstf, dstf, rstd_rep))
            else:
                pr.emit("vector", lambda e, src=src, dstf=dstf: e.tensor_mul(
                    dstf, src, rstd_rep))
            if out_bf is not None:
                dstb = out_bf[:, b * T:(b + 1) * T]
                if gamma_col is not None:
                    pr.emit("vector", lambda e, dstf=dstf, dstb=dstb, b=b:
                            e.tensor_scalar(dstb, dstf, gamma_col[:, b:b + 1],
                                            None, op0=ALU.mult))
                else:
                    pr.emit("vector", lambda e, dstf=dstf, dstb=dstb:
                            e.tensor_copy(dstb, dstf))
        else:
            dstb = out_bf[:, b * T:(b + 1) * T]
            if not rms:
                pr.emit("vector", lambda e, src=src, dstb=dstb: e.tensor_sub(
                    dstb, src, mrstd_rep))
                _vs()
                pr.emit("vector", lambda e, dstb=dstb: e.tensor_mul(
                    dstb, dstb, rstd_rep))
            else:
                pr.emit("vector", lambda e, src=src, dstb=dstb: e.tensor_mul(
                    dstb, src, rstd_rep))
            if gamma_col is not None:
                _vs()
                pr.emit("vector", lambda e, dstb=dstb, b=b: e.tensor_scalar(
                    dstb, dstb, gamma_col[:, b:b + 1], None, op0=ALU.mult))
    pr.tag("vector", "dve")



def emit_l0_ffn(pr, xnb, xsb, w, sc):
    s64, g64, relu, gb = sc["s64"], sc["g64"], sc["relu"], sc["gb"]
    tmp = sc["tmp"]
    p0, p1 = sc["p0"], sc["p1"]

    pr.wait("tensor", "dve")
    for n in range(NT):
        for b in range(DB):
            pr.emit("tensor", lambda e, b=b, n=n: e.matmul(
                p0[0:64, :], w["WinT"][:, b * 64:(b + 1) * 64],
                xnb[:, b * T + n * 512: b * T + (n + 1) * 512],
                start=(b == 0), stop=(b == DB - 1)),
                sem="pe" if b == DB - 1 else None)
        pr.wait("scalar", "pe")
        pr.emit("scalar", lambda e, n=n: e.activation(
            s64[0:64, n * 512:(n + 1) * 512], p0[0:64, :], AF.Identity,
            bias=w["binc"][0:64, 0:1]), sem="act")
        pr.wait("tensor", "act")

    dup1 = pr.ldsem + "_dup1"
    pr.wait("sync", "act")
    pr.emit("sync", lambda e: e.dma_start(out=s64[64:128, :], in_=s64[0:64, :]),
            sem=dup1, amt=16)
    pr.wait("scalar", dup1)
    pr.wait("vector", dup1)

    for hp in range(8):
        pr.emit("scalar", lambda e, hp=hp: e.activation(
            relu[:, :], s64[:, :], AF.Relu,
            scale=w["W1c"][:, hp:hp + 1], bias=w["b1c"][:, hp:hp + 1]),
            sem="act")
        pr.wait("vector", "act")
        if hp == 0:
            pr.emit("vector", lambda e: e.tensor_scalar(
                g64[:, :], relu[:, :], w["W2c"][:, 0:1], None, op0=ALU.mult),
                sem="dve")
        else:
            pr.emit("vector", lambda e, hp=hp: e.scalar_tensor_tensor(
                g64[:, :], relu[:, :], w["W2c"][:, hp:hp + 1], g64[:, :],
                op0=ALU.mult, op1=ALU.add), sem="dve")
        pr.wait("vector", "dve")
        pr.wait("scalar", "dve")
    dup2 = pr.ldsem + "_dup2"
    pr.wait("sync", "dve")
    pr.emit("sync", lambda e: e.dma_start(out=relu[0:64, :], in_=g64[64:128, :]),
            sem=dup2, amt=16)
    pr.wait("vector", dup2)
    pr.emit("vector", lambda e: e.tensor_add(
        g64[0:64, :], g64[0:64, :], relu[0:64, :]))
    pr.tag("vector", "dve")
    pr.wait("vector", "dve")
    pr.emit("vector", lambda e: e.tensor_scalar(
        gb[0:64, :], g64[0:64, :], w["b2c"][0:64, 0:1], None, op0=ALU.add),
        sem="dve")

    pr.wait("tensor", "dve")
    for b in range(DB):
        for n in range(NT):
            pr.emit("tensor", lambda e, b=b, n=n: e.matmul(
                p1[:, :], w["WoutT"][0:64, b * P:(b + 1) * P],
                gb[0:64, n * 512:(n + 1) * 512], start=True, stop=True),
                sem="pe")
            pr.wait("scalar", "pe")
            pr.emit("scalar", lambda e, b=b: e.activation(
                tmp[:, 0:512], p1[:, :], AF.Identity, scale=0.5,
                bias=w["bout_half"][:, b:b + 1]), sem="act")
            pr.wait("vector", "act")
            dst = xsb[:, b * T + n * 512: b * T + (n + 1) * 512]
            pr.emit("vector", lambda e, dst=dst: e.tensor_add(
                dst, dst, tmp[:, 0:512]), sem="dve")
            pr.wait("tensor", "dve")
            pr.wait("scalar", "dve")



def emit_router(pr, xsb, sc, r):
    rowsB = sc["rowsB"]
    m_r = rowsB[0:1, 0:T]
    rstd_r = rowsB[0:1, 2 * T:3 * T]
    p2, p3 = r["p2"], r["p3"]
    lgs, lgT = r["lgs"], r["lgT"]
    red, mt, cums = r["red"], r["mt"], r["cums"]

    pr.wait("tensor", "dve")
    for b in range(DB):
        pr.emit("tensor", lambda e, b=b: e.matmul(
            p2[0:NEXP, 0:1], r["rwT"][:, b * NEXP:(b + 1) * NEXP],
            sc["ones_cf"][:, 0:1], start=(b == 0), stop=(b == DB - 1)),
            sem="pe" if b == DB - 1 else None)
    pr.wait("vector", "pe")
    pr.emit("vector", lambda e: e.tensor_scalar(
        r["colsum_n"][:, 0:1], p2[0:NEXP, 0:1], -1.0, None, op0=ALU.mult),
        sem="dve")
    pr.wait("tensor", "dve")

    for n in range(NT):
        sl = slice(n * 512, (n + 1) * 512)
        for b in range(DB):
            pr.emit("tensor", lambda e, b=b, n=n: e.matmul(
                p2[0:NEXP, :], r["rwT"][:, b * NEXP:(b + 1) * NEXP],
                xsb[:, b * T + n * 512: b * T + (n + 1) * 512],
                start=(b == 0), stop=(b == DB - 1)))
        pr.emit("tensor", lambda e, n=n: e.matmul(
            p3[0:NEXP, :], r["ones_f"][0:1, 0:NEXP],
            m_r[:, n * 512:(n + 1) * 512], start=True, stop=True), sem="pe")
        pr.wait("vector", "pe")
        pr.emit("vector", lambda e, sl=sl: e.tensor_copy(
            lgs[0:NEXP, sl], p2[0:NEXP, :]))
        pr.tag("vector", "dve")
        pr.wait("vector", "dve")
        pr.emit("vector", lambda e, sl=sl: e.scalar_tensor_tensor(
            lgs[0:NEXP, sl], p3[0:NEXP, :], r["colsum_n"][0:NEXP, 0:1],
            lgs[0:NEXP, sl], op0=ALU.mult, op1=ALU.add), sem="dve")
        pr.wait("tensor", "dve")
        pr.emit("tensor", lambda e, n=n: e.matmul(
            p3[0:NEXP, :], r["ones_f"][0:1, 0:NEXP],
            rstd_r[:, n * 512:(n + 1) * 512], start=True, stop=True), sem="pe")
        pr.wait("vector", "pe")
        pr.wait("vector", "dve")
        pr.emit("vector", lambda e, sl=sl: e.tensor_tensor(
            lgs[0:NEXP, sl], lgs[0:NEXP, sl], p3[0:NEXP, :], op=ALU.mult),
            sem="dve")
        pr.wait("tensor", "dve")

    for i in range(16):
        pr.emit("tensor", lambda e, i=i: e.transpose(
            p2[:, 0:NEXP], lgs[0:NEXP, i * P:(i + 1) * P],
            r["identf"][0:NEXP, 0:NEXP]), sem="pe")
        pr.wait("vector", "pe")
        pr.emit("vector", lambda e, i=i: e.tensor_copy(
            lgT[:, i * NEXP:(i + 1) * NEXP], p2[:, 0:NEXP]), sem="dve")
        pr.wait("tensor", "dve")

    def vsync():
        pr.tag("vector", "dve")
        pr.wait("vector", "dve")

    for i in range(16):
        tile = lgT[:, i * NEXP:(i + 1) * NEXP]
        pr.emit("vector", lambda e, tile=tile: e.reduce_max(
            red[:, 0:1], tile, axis=AX.X))
        vsync()
        pr.emit("vector", lambda e, tile=tile: e.tensor_scalar(
            mt[:, :], tile, red[:, 0:1], None, op0=ALU.is_ge))
        vsync()
        pr.emit("vector", lambda e: e.tensor_tensor_scan(
            cums[:, :], r["ones14"][:, :], mt[:, :], 0.0,
            op0=ALU.mult, op1=ALU.add))
        vsync()
        pr.emit("vector", lambda e: e.tensor_scalar(
            cums[:, :], cums[:, :], 1.0, None, op0=ALU.is_le))
        vsync()
        pr.emit("vector", lambda e: e.tensor_tensor(
            mt[:, :], mt[:, :], cums[:, :], op=ALU.mult))
        vsync()
        pr.emit("vector", lambda e, tile=tile: e.tensor_scalar(
            cums[:, :], tile, red[:, 0:1], None, op0=ALU.subtract),
            sem="dve")
        pr.wait("scalar", "dve")
        pr.emit("scalar", lambda e: e.activation(
            cums[:, :], cums[:, :], AF.Exp), sem="act")
        pr.wait("vector", "act")
        pr.emit("vector", lambda e: e.reduce_sum(
            red[:, 1:2], cums[:, :], axis=AX.X))
        vsync()
        pr.emit("vector", lambda e: e.reciprocal(red[:, 2:3], red[:, 1:2]))
        vsync()
        pr.emit("vector", lambda e: e.tensor_scalar(
            red[:, 2:3], red[:, 2:3], 0.5, None, op0=ALU.mult))
        vsync()
        pr.emit("vector", lambda e: e.tensor_scalar(
            mt[:, :], mt[:, :], red[:, 2:3], None, op0=ALU.mult), sem="dve")
        pr.wait("tensor", "dve")
        pr.emit("tensor", lambda e: e.transpose(
            p2[0:NEXP, 0:P], mt[:, :], r["identf"][:, :]), sem="pe")
        pr.wait("vector", "pe")
        pr.emit("vector", lambda e, i=i: e.tensor_copy(
            r["msel"][0:NEXP, i * P:(i + 1) * P], p2[0:NEXP, 0:P]), sem="dve")
        pr.wait("tensor", "dve")
    pr.emit("vector", lambda e: e.tensor_copy(
        r["msel_b"][0:NEXP, :], r["msel"][0:NEXP, :]), sem="dve")


def emit_moe_experts(pr, xnb, xsb, m, dram, cache=None):
    w1s, w2s = m["w1s"], m["w2s"]
    w1f = m["w1f"]
    hid, t1 = m["hid"], m["t1"]
    ph = [m["ph0"], m["ph1"], m["p0"], m["p1"]]
    py = [m["py0"], m["py1"], m["p2b"]]

    last_w1_mm = 0
    last_w2_mm = 0
    cast_done = [0, 0]
    msel_slot_free = [0, 0]

    if cache == "in":
        pr.wait("sync", "cwb")
    pr.emit("sync", lambda e: e.dma_start(
        out=m["b1c"][:, :].rearrange("p (x k) -> p x k", k=16),
        in_=dram["d_b1c"].rearrange("x p k -> p x k")), sem="wld", amt=16)
    pr.emit("sync", lambda e: e.dma_start(
        out=m["b2c"][:, :].rearrange("p (x k) -> p x k", k=8),
        in_=dram["d_b2c"].rearrange("x p k -> p x k")), sem="wld", amt=16)
    pr.wait("scalar", "wld")
    pr.wait("vector", "wld")
    pr.wait("tensor", "wld")

    for ex in range(NEXP):
        chunks = ([("w1", kc, h) for kc in range(8) for h in range(2)] +
                  [("w2", kc, 0) for kc in range(16)])
        if cache == "in":
            pr.wait("sync", "pe", last_w1_mm)
            for kc in range(8):
                for h in range(2):
                    pr.emit("sync", lambda e, ex=ex, kc=kc, h=h: e.dma_start(
                        out=w1s[:, kc * T + h * 1024: kc * T + (h + 1) * 1024],
                        in_=dram["d_w1b"][ex, kc * P:(kc + 1) * P,
                                          h * 1024:(h + 1) * 1024]),
                        sem="fwld", amt=16)
            pr.wait("sync", "pe", last_w2_mm)
            for kc in range(16):
                pr.emit("sync", lambda e, ex=ex, kc=kc: e.dma_start(
                    out=w2s[:, kc * 1024:(kc + 1) * 1024],
                    in_=dram["d_w2b"][ex, kc * P:(kc + 1) * P, :]),
                    sem="fwld", amt=16)
            chunks = []
        for i, (which, kc, h) in enumerate(chunks):
            s = i % 2
            wsem = f"wld{s}"
            csem = f"wcast{s}"
            pr.wait("sync", csem, cast_done[s])
            if which == "w1":
                pr.emit("sync", lambda e, ex=ex, kc=kc, h=h, s=s: e.dma_start(
                    out=w1f[:, s * 1024:(s + 1) * 1024],
                    in_=dram["d_w1"][ex, kc * P:(kc + 1) * P,
                                     h * 1024:(h + 1) * 1024]),
                    sem=wsem, amt=16)
                dst = w1s[:, kc * T + h * 1024: kc * T + (h + 1) * 1024]
                gate = last_w1_mm
            else:
                pr.emit("sync", lambda e, ex=ex, kc=kc, s=s: e.dma_start(
                    out=w1f[:, s * 1024:(s + 1) * 1024],
                    in_=dram["d_w2"][ex, kc * P:(kc + 1) * P, :]),
                    sem=wsem, amt=16)
                dst = w2s[:, kc * 1024:(kc + 1) * 1024]
                gate = last_w2_mm
            ceng = "scalar" if s == 0 else "vector"
            pr.wait(ceng, wsem)
            pr.wait(ceng, "pe", gate)
            if ceng == "scalar":
                cast_done[s] = pr.emit(
                    ceng, lambda e, dst=dst, s=s: e.activation(
                        dst, w1f[:, s * 1024:(s + 1) * 1024], AF.Copy),
                    sem=csem)
            else:
                cast_done[s] = pr.emit(
                    ceng, lambda e, dst=dst, s=s: e.tensor_copy(
                        dst, w1f[:, s * 1024:(s + 1) * 1024]), sem=csem)
            if cache == "out":
                if which == "w1":
                    cdst = dram["d_w1b"][ex, kc * P:(kc + 1) * P,
                                         h * 1024:(h + 1) * 1024]
                else:
                    cdst = dram["d_w2b"][ex, kc * P:(kc + 1) * P, :]
                pr.wait("pool", csem)
                pr.emit("pool", lambda e, cdst=cdst, dst=dst: e.dma_start(
                    out=cdst, in_=dst), sem="cwb", amt=16)
        b1col = m["b1c"][:, ex * 16:(ex + 1) * 16]
        b2col = m["b2c"][:, ex * 8:(ex + 1) * 8]
        if cache == "in":
            pr.wait("tensor", "fwld")
        else:
            pr.wait("tensor", "wcast0")
            pr.wait("tensor", "wcast1")
        mslot = ex % 2
        mv = m["mselr"][:, mslot * T:(mslot + 1) * T]
        pr.wait("tensor", "yacc", msel_slot_free[mslot])
        for n4 in range(NT):
            pr.emit("tensor", lambda e, ex=ex, n4=n4: e.matmul(
                m["p3b"][:, :], m["ohb"][0:NEXP, ex * P:(ex + 1) * P],
                m["msel_b"][0:NEXP, n4 * 512:(n4 + 1) * 512],
                start=True, stop=True), sem="pe")
            pr.wait("scalar", "pe")
            pr.emit("scalar", lambda e, n4=n4, mv=mv: e.activation(
                mv[:, n4 * 512:(n4 + 1) * 512], m["p3b"][:, :],
                AF.Identity), sem="mselev")
            pr.wait("tensor", "mselev")

        w1_mm_mark = 0
        for tq in range(NT):
            for mm in range(16):
                s = mm % 4
                pr.wait("tensor", "hevac",
                        max(0, pr.counts.get("hevac", 0) - 3))
                for kc in range(8):
                    pr.emit("tensor",
                            lambda e, mm=mm, kc=kc, tq=tq, s=s: e.matmul(
                                ph[s][:, :],
                                w1s[:, kc * T + mm * P: kc * T + (mm + 1) * P],
                                xnb[:, kc * T + tq * 512:
                                    kc * T + (tq + 1) * 512],
                                start=(kc == 0), stop=(kc == 7)),
                            sem="pe" if kc == 7 else None)
                pr.wait("scalar", "pe")
                pr.emit("scalar", lambda e, mm=mm, s=s, b1col=b1col:
                        e.activation(
                    hid[:, mm * 512:(mm + 1) * 512], ph[s][:, :], AF.Gelu,
                    bias=b1col[:, mm:mm + 1]), sem="hevac")
            if tq == NT - 1:
                w1_mm_mark = pr.counts.get("pe", 0)
            pr.wait("tensor", "hevac")
            for mo in range(8):
                s = mo % 3
                pr.wait("tensor", "moev", max(0, pr.counts.get("moev", 0) - 2))
                for kc in range(16):
                    pr.emit("tensor", lambda e, mo=mo, kc=kc, s=s: e.matmul(
                        py[s][:, :],
                        w2s[:, kc * 1024 + mo * P: kc * 1024 + (mo + 1) * P],
                        hid[:, kc * 512:(kc + 1) * 512],
                        start=(kc == 0), stop=(kc == 15)),
                        sem="pe" if kc == 15 else None)
                pr.wait("scalar", "pe")
                pr.wait("scalar", "yacc",
                        max(0, pr.counts.get("yacc", 0) - 2))
                pr.emit("scalar", lambda e, mo=mo, s=s, b2col=b2col:
                        e.activation(
                    t1[:, s * 512:(s + 1) * 512], py[s][:, :], AF.Identity,
                    bias=b2col[:, mo:mo + 1]), sem="moev")
                pr.wait("vector", "moev")
                dst = xsb[:, mo * T + tq * 512: mo * T + (tq + 1) * 512]
                pr.emit("vector", lambda e, tq=tq, s=s, mv=mv: e.tensor_mul(
                    t1[:, s * 512:(s + 1) * 512], t1[:, s * 512:(s + 1) * 512],
                    mv[:, tq * 512:(tq + 1) * 512]))
                pr.tag("vector", "t1ch")
                pr.wait("vector", "t1ch")
                pr.emit("vector", lambda e, dst=dst, s=s: e.tensor_add(
                    dst, dst, t1[:, s * 512:(s + 1) * 512]), sem="yacc")
        last_w1_mm = w1_mm_mark
        last_w2_mm = pr.counts.get("pe", 0)
        msel_slot_free[ex % 2] = pr.counts.get("yacc", 0)



def emit_l0_attn(pr, xsb, xn, w, sc, chunks=NT, chunk_cb=None):
    p0, p1, p2, p3 = sc["p0"], sc["p1"], sc["p2"], sc["p3"]
    pB, pC = sc["pB"], sc["pC"]
    xnb_c, dtF = sc["xnb_c"], sc["dtF"]
    dtlo, Bsb, Csb = sc["dtlo"], sc["Bsb"], sc["Csb"]
    h_state = sc["h_state"]
    sp1, sp2 = sc["sp1"], sc["sp2"]
    a_t, u2_t, hs_t = sc["a_t"], sc["u2_t"], sc["hs_t"]

    def vsync():
        pr.tag("vector", "dve")
        pr.wait("vector", "dve")

    def ssync():
        pr.tag("scalar", "act")
        pr.wait("scalar", "act")

    pr.emit("vector", lambda e: e.memset(h_state[:, :], 0.0), sem="dve")
    sc["last_scan_dve"] = [0]

    import os as _os0
    _skip_all = int(_os0.environ.get("L0A_SKIP_ALL_FROM", "99"))
    for c in range(chunks):
        t0 = c * 512
        if int(_os0.environ.get("L0A_BARRIER", "1")):
            pr.barrier(f"l0a_chunk{c}_{pr.counts.get('dve', 0)}")
        if c >= _skip_all:
            continue
        pr.wait("vector", "dve")
        for b in range(DB):
            pr.emit("vector", lambda e, b=b, t0=t0: e.tensor_copy(
                xnb_c[:, b * 512:(b + 1) * 512],
                xn[:, b * T + t0: b * T + t0 + 512]))
        pr.tag("vector", "dve")

        pr.wait("tensor", "dve")
        for b in range(DB):
            pr.emit("tensor", lambda e, b=b: e.matmul(
                p0[0:64, :], w["WxdtT"][:, b * 64:(b + 1) * 64],
                xnb_c[:, b * 512:(b + 1) * 512],
                start=(b == 0), stop=(b == DB - 1)))
        for b in range(DB):
            pr.emit("tensor", lambda e, b=b: e.matmul(
                p1[0:16, :], w["WxBT"][:, b * 16:(b + 1) * 16],
                xnb_c[:, b * 512:(b + 1) * 512],
                start=(b == 0), stop=(b == DB - 1)))
        for b in range(DB):
            pr.emit("tensor", lambda e, b=b: e.matmul(
                p2[0:16, :], w["WxCT"][:, b * 16:(b + 1) * 16],
                xnb_c[:, b * 512:(b + 1) * 512],
                start=(b == 0), stop=(b == DB - 1)), sem="pe" if b == DB - 1 else None)
        pr.wait("scalar", "pe")
        pr.emit("scalar", lambda e: e.activation(dtlo[0:64, :], p0[0:64, :],
                                                 AF.Identity), sem="act")
        pr.wait("vector", "pe")
        pr.emit("vector", lambda e: e.tensor_copy(Bsb[0:16, :], p1[0:16, :]))
        pr.emit("vector", lambda e: e.tensor_copy(Csb[0:16, :], p2[0:16, :]),
                sem="dve")
        pr.wait("tensor", "dve")
        pr.wait("tensor", "act")

        for mc in range(DB):
            pr.emit("tensor", lambda e, mc=mc: e.matmul(
                p3[:, :], w["WdtT"][0:64, mc * P:(mc + 1) * P], dtlo[0:64, :],
                start=True, stop=True), sem="pe")
            pr.wait("scalar", "pe")
            dst = dtF[:, mc * 512:(mc + 1) * 512]
            pr.emit("scalar", lambda e, mc=mc: e.activation(
                sp1[:, :], p3[:, :], AF.Identity,
                bias=w["bdt_c"][:, mc:mc + 1]))
            ssync()
            pr.emit("scalar", lambda e, dst=dst: e.activation(
                dst, sp1[:, :], AF.Relu))
            ssync()
            pr.emit("scalar", lambda e: e.activation(sp2[:, :], sp1[:, :],
                                                     AF.Abs))
            ssync()
            pr.emit("scalar", lambda e: e.activation(sp2[:, :], sp2[:, :],
                                                     AF.Exp, scale=-1.0))
            ssync()
            pr.emit("scalar", lambda e: e.activation(sp2[:, :], sp2[:, :],
                                                     AF.Ln, bias=1.0),
                    sem="act")
            pr.wait("vector", "act")
            pr.emit("vector", lambda e, dst=dst: e.tensor_add(
                dst, dst, sp2[:, :]), sem="dve")
            pr.wait("tensor", "dve")
            pr.wait("scalar", "dve")

        import os as _os
        _skip_d = int(_os.environ.get("L0A_SKIP_DSKIP_FROM", "99"))
        pr.wait("vector", "dve")
        for b in range(DB if c < _skip_d else 0):
            xsl = xsb[:, b * T + t0: b * T + t0 + 512]
            nsl = xn[:, b * T + t0: b * T + t0 + 512]
            pr.emit("vector", lambda e, xsl=xsl, nsl=nsl, b=b:
                    e.scalar_tensor_tensor(xsl, nsl, w["D_c"][:, b:b + 1],
                                           xsl, op0=ALU.mult, op1=ALU.add))
        if c < _skip_d:
            vsync()
        for b in range(DB if c < _skip_d else 0):
            nsl = xn[:, b * T + t0: b * T + t0 + 512]
            pr.emit("vector", lambda e, nsl=nsl, b=b: e.tensor_mul(
                nsl, dtF[:, b * 512:(b + 1) * 512], nsl))
        if c < _skip_d:
            vsync()

        brow = sc["brow"]
        pr.wait("sync", "dve")
        import os
        _skip_n = int(os.environ.get("L0A_SKIP_NLOOP_FROM", "99"))
        _nbar = int(os.environ.get("L0A_NBARRIER", "0"))
        for n in range(DSTATE if c < _skip_n else 0):
            if _nbar:
                pr.barrier(f"l0a_nb_{c}_{n}_{pr.counts.get('dve', 0)}")
            s = n % 2
            pr.wait("sync", "brow_free", pr.counts.get("brow_free", 0) - 1
                    if pr.counts.get("brow_free", 0) >= 2 else 0)
            pr.emit("sync", lambda e, n=n, s=s: e.dma_start(
                out=brow[0:1, s * 1024: s * 1024 + 512], in_=Bsb[n:n + 1, :]),
                sem="browld", amt=16)
            pr.emit("sync", lambda e, n=n, s=s: e.dma_start(
                out=brow[0:1, s * 1024 + 512: s * 1024 + 1024],
                in_=Csb[n:n + 1, :]), sem="browld", amt=16)
            pr.wait("tensor", "dve")
            pr.wait("tensor", "bc_free", pr.counts.get("bc_free", 0))
            pr.wait("tensor", "browld")
            pr.emit("tensor", lambda e, s=s: e.matmul(
                pB[:, :], sc["ones_f"][0:1, :],
                brow[0:1, s * 1024: s * 1024 + 512],
                start=True, stop=True))
            pr.emit("tensor", lambda e, s=s: e.matmul(
                pC[:, :], sc["ones_f"][0:1, :],
                brow[0:1, s * 1024 + 512: s * 1024 + 1024],
                start=True, stop=True), sem="brow_free")
            pr.wait("vector", "brow_free")
            for b in range(DB):
                col = b * 16 + n
                usl = xn[:, b * T + t0: b * T + t0 + 512]
                xsl = xsb[:, b * T + t0: b * T + t0 + 512]
                pr.wait("scalar", "dve", sc["last_scan_dve"][0])
                pr.emit("scalar", lambda e, b=b, col=col: e.activation(
                    a_t[:, :], dtF[:, b * 512:(b + 1) * 512], AF.Exp,
                    scale=w["A_cols"][:, col:col + 1]), sem="act")
                pr.emit("vector", lambda e, usl=usl: e.tensor_tensor(
                    u2_t[:, :], usl, pB[:, :], op=ALU.mult))
                vsync()
                pr.wait("vector", "act")
                td = sc.get("tile_dump")
                if td is not None and td[:3] == (c, n, b):
                    bufs = td[3]
                    pr.emit("vector", lambda e: e.tensor_copy(
                        bufs["a"][:, :], a_t[:, :]))
                    pr.emit("vector", lambda e: e.tensor_copy(
                        bufs["u2"][:, :], u2_t[:, :]))
                    vsync()
                pr.emit("vector", lambda e, col=col: e.tensor_tensor_scan(
                    hs_t[:, :], a_t[:, :], u2_t[:, :],
                    h_state[:, col:col + 1], op0=ALU.mult, op1=ALU.add))
                vsync()
                sc["last_scan_dve"][0] = pr.counts["dve"]
                pr.emit("vector", lambda e, col=col: e.tensor_copy(
                    h_state[:, col:col + 1], hs_t[:, 511:512]))
                pr.emit("vector", lambda e: e.tensor_tensor(
                    u2_t[:, :], hs_t[:, :], pC[:, :], op=ALU.mult))
                vsync()
                pr.emit("vector", lambda e, xsl=xsl: e.tensor_add(
                    xsl, xsl, u2_t[:, :]), sem="bc_free" if b == DB - 1 else None)
                td = sc.get("tile_dump")
                if td is not None and td[:3] == (c, n, b):
                    bufs = td[3]
                    pr.emit("vector", lambda e: e.tensor_copy(
                        bufs["hs"][:, :], hs_t[:, :]))
                    pr.emit("vector", lambda e: e.tensor_copy(
                        bufs["pb"][:, :], pB[:, :]))
                    pr.emit("vector", lambda e: e.tensor_copy(
                        bufs["pc"][:, :], pC[:, :]), sem="dve")
        if chunk_cb is not None:
            chunk_cb(c)



def emit_l1_attn_e1(pr, znb, u_bf, w, sc):
    wstg = sc["wstg"]
    t2 = sc["t2"]
    pu, pg = sc["pu"], sc["pg"]
    wcin, wcg = sc["wcol_in"], sc["wcol_g"]
    pe_hist = []

    cast_done1 = [0, 0]
    for m in range(IBL):
        s = m % 2
        sl = slice(s * 8 * P, (s + 1) * 8 * P)
        slot_gate = pe_hist[m - 2] if m >= 2 else 0
        jobs = [(w["d_inwT"], wcin, 0), (w["d_gatewT"], wcg, 0),
                (w["d_inwT"], wcin, 1), (w["d_gatewT"], wcg, 1)]
        for j, (dsrc, dmat, half) in enumerate(jobs):
            q = j % 2
            wsem = f"wlde1_{q}"
            csem = f"wcste1_{q}"
            pr.wait("sync", csem, cast_done1[q])
            pr.emit("sync", lambda e, m=m, half=half, q=q, dsrc=dsrc:
                    e.dma_start(
                        out=wstg[:, q * 4 * P:(q + 1) * 4 * P],
                        in_=dsrc[half * 512:(half + 1) * 512,
                                 m * P:(m + 1) * P].rearrange(
                            "(c p) k -> p c k", p=P)), sem=wsem, amt=16)
            pr.wait("pool", wsem)
            pr.wait("pool", "pe", slot_gate)
            cast_done1[q] = pr.emit(
                "pool", lambda e, half=half, sl=sl, q=q, dmat=dmat:
                e.tensor_copy(
                    dmat[:, sl][:, half * 4 * P:(half + 1) * 4 * P],
                    wstg[:, q * 4 * P:(q + 1) * 4 * P]), sem=csem)
        pr.wait("tensor", "wcste1_0")
        pr.wait("tensor", "wcste1_1")
        for tq in range(NT):
            for kc in range(DB):
                pr.emit("tensor", lambda e, kc=kc, tq=tq, sl=sl: e.matmul(
                    pu[:, :], wcin[:, sl][:, kc * P:(kc + 1) * P],
                    znb[:, kc * T + tq * 512: kc * T + (tq + 1) * 512],
                    start=(kc == 0), stop=(kc == DB - 1)))
            for kc in range(DB):
                pr.emit("tensor", lambda e, kc=kc, tq=tq, sl=sl: e.matmul(
                    pg[:, :], wcg[:, sl][:, kc * P:(kc + 1) * P],
                    znb[:, kc * T + tq * 512: kc * T + (tq + 1) * 512],
                    start=(kc == 0), stop=(kc == DB - 1)),
                    sem="pe" if kc == DB - 1 else None)
            pr.wait("scalar", "pe")
            pr.emit("scalar", lambda e, m=m: e.activation(
                t2[:, :], pg[:, :], AF.Sigmoid,
                bias=w["gateb_c"][:, m:m + 1]), sem="act")
            pr.wait("vector", "act")
            dst = u_bf[:, m * T + tq * 512: m * T + (tq + 1) * 512]
            pr.emit("vector", lambda e, m=m: e.tensor_scalar(
                t2[:, :] if False else sc["t3"][:, :], pu[:, :],
                w["inb_c"][:, m:m + 1], None, op0=ALU.add))
            pr.tag("vector", "dve")
            pr.wait("vector", "dve")
            pr.emit("vector", lambda e, dst=dst: e.tensor_mul(
                dst, sc["t3"][:, :], t2[:, :]), sem="dve")
            pr.wait("tensor", "dve")
            pr.wait("scalar", "dve")
        pe_hist.append(pr.counts.get("pe", 0))


def emit_l1_attn_e2(pr, u_bf, xsb, w, sc):
    wstg = sc["wstg"]
    wb, wc, wd = sc["wb"], sc["wc"], sc["wd"]
    wo = sc["wo"]
    bq, cq, dq = sc["bq"], sc["cq"], sc["dq"]
    hsq = sc["hsq"]
    a_rep = sc["a_rep"]
    onesf_t = sc["onesf_t"]
    carry = sc["carry"]
    y_bf = sc["y_bf"]
    oacc = sc["oacc"]
    t1b = sc["t1b"]
    pb_, pc_, pd_, po_ = sc["pb_"], sc["pc_"], sc["pd_"], sc["po_"]

    def vsync():
        pr.tag("vector", "dve")
        pr.wait("vector", "dve")

    pr.emit("vector", lambda e: e.memset(oacc[:, :], 0.0))
    pr.emit("vector", lambda e: e.memset(carry[:, :], 0.0), sem="dve")
    pe_hist2 = []
    cast_done2 = [0, 0]

    for ib in range(IBL):
        slot_gate = pe_hist2[ib - 1] if ib >= 1 else 0
        jobs2 = ([(dn, dst, q, False) for dn, dst in
                  (("d_bwT", wb), ("d_cwT", wc), ("d_dwT", wd))
                  for q in range(4)] +
                 [("d_outwT", wo, q, True) for q in range(2)])
        for j, (dname, dst, q, is_out) in enumerate(jobs2):
            sl2 = j % 2
            wsem = f"wlde2_{sl2}"
            csem = f"wcste2_{sl2}"
            pr.wait("sync", csem, cast_done2[sl2])
            if is_out:
                pr.emit("sync", lambda e, ib=ib, q=q, sl2=sl2: e.dma_start(
                    out=wstg[:, sl2 * 4 * P:(sl2 + 1) * 4 * P],
                    in_=w["d_outwT"][ib * P:(ib + 1) * P,
                                     q * 512:(q + 1) * 512].rearrange(
                        "p (c k) -> p c k", k=P)), sem=wsem, amt=16)
                tgt = wo[:, q * 512:(q + 1) * 512]
            else:
                pr.emit("sync", lambda e, ib=ib, q=q, sl2=sl2, dname=dname:
                        e.dma_start(
                            out=wstg[:, sl2 * 4 * P:(sl2 + 1) * 4 * P],
                            in_=w[dname][q * 512:(q + 1) * 512,
                                         ib * P:(ib + 1) * P].rearrange(
                                "(c p) k -> p c k", p=P)), sem=wsem, amt=16)
                tgt = dst[:, q * 4 * P:(q + 1) * 4 * P]
            pr.wait("pool", wsem)
            pr.wait("pool", "pe", slot_gate)
            cast_done2[sl2] = pr.emit(
                "pool", lambda e, tgt=tgt, sl2=sl2: e.tensor_copy(
                    tgt, wstg[:, sl2 * 4 * P:(sl2 + 1) * 4 * P]), sem=csem)
        pr.wait("tensor", "wcste2_0")
        pr.wait("tensor", "wcste2_1")
        pr.wait("vector", "dve")
        pr.emit("vector", lambda e, ib=ib: e.tensor_scalar(
            a_rep[:, :], onesf_t[:, :], w["a_dec"][:, ib:ib + 1], None,
            op0=ALU.mult), sem="dve")

        for th in range(2):
            tsl = slice(th * 1024, (th + 1) * 1024)
            for nn in range(2):
                n0 = th * 1024 + nn * 512
                for kc in range(IBL):
                    pr.emit("tensor", lambda e, kc=kc, n0=n0, ib=ib: e.matmul(
                        pb_[:, :], wb[:, kc * P:(kc + 1) * P],
                        u_bf[:, kc * T + n0: kc * T + n0 + 512],
                        start=(kc == 0), stop=(kc == IBL - 1)))
                for kc in range(IBL):
                    pr.emit("tensor", lambda e, kc=kc, n0=n0, ib=ib: e.matmul(
                        pc_[:, :], wc[:, kc * P:(kc + 1) * P],
                        u_bf[:, kc * T + n0: kc * T + n0 + 512],
                        start=(kc == 0), stop=(kc == IBL - 1)))
                for kc in range(IBL):
                    pr.emit("tensor", lambda e, kc=kc, n0=n0, ib=ib: e.matmul(
                        pd_[:, :], wd[:, kc * P:(kc + 1) * P],
                        u_bf[:, kc * T + n0: kc * T + n0 + 512],
                        start=(kc == 0), stop=(kc == IBL - 1)),
                        sem="pe" if kc == IBL - 1 else None)
                pr.wait("vector", "pe")
                pr.wait("vector", "ybf", pr.counts.get("ybf", 0))
                nsl = slice(nn * 512, (nn + 1) * 512)
                pr.emit("vector", lambda e, nsl=nsl, ib=ib: e.tensor_scalar(
                    bq[:, nsl], pb_[:, :], w["bb_c"][:, ib:ib + 1], None,
                    op0=ALU.add))
                pr.emit("vector", lambda e, nsl=nsl, ib=ib: e.tensor_scalar(
                    cq[:, nsl], pc_[:, :], w["cb_c"][:, ib:ib + 1], None,
                    op0=ALU.add))
                pr.emit("vector", lambda e, nsl=nsl, ib=ib: e.tensor_scalar(
                    dq[:, nsl], pd_[:, :], w["db_c"][:, ib:ib + 1], None,
                    op0=ALU.add), sem="bcdev")
                pr.wait("tensor", "bcdev")
            pr.wait("vector", "bcdev")
            pr.emit("vector", lambda e, ib=ib: e.tensor_tensor_scan(
                hsq[:, :], a_rep[:, :], bq[:, :], carry[:, ib:ib + 1],
                op0=ALU.mult, op1=ALU.add))
            vsync()
            pr.emit("vector", lambda e, ib=ib: e.tensor_copy(
                carry[:, ib:ib + 1], hsq[:, 1023:1024]), sem="scn")
            pr.wait("pool", "scn")
            pr.emit("pool", lambda e: e.tensor_mul(cq[:, :], cq[:, :],
                                                   hsq[:, :]))
            pr.tag("pool", "ych")
            pr.wait("pool", "ych")
            pr.emit("pool", lambda e: e.tensor_add(cq[:, :], cq[:, :],
                                                   dq[:, :]))
            pr.tag("pool", "ych")
            pr.wait("pool", "ych")
            pr.emit("pool", lambda e: e.tensor_copy(y_bf[:, :], cq[:, :]),
                    sem="ybf")
            pr.wait("tensor", "ybf")
            for nn in range(2):
                for mo in range(DB):
                    s = mo % 2
                    po = po_[s]
                    pr.wait("tensor", "oev", max(0, pr.counts.get("oev", 0) - 1))
                    pr.emit("tensor", lambda e, mo=mo, nn=nn, s=s: e.matmul(
                        po_[s][:, :], wo[:, mo * P:(mo + 1) * P],
                        y_bf[:, nn * 512:(nn + 1) * 512],
                        start=True, stop=True), sem="pe")
                    pr.wait("scalar", "pe")
                    pr.emit("scalar", lambda e, s=s: e.activation(
                        t1b[:, :], po_[s][:, :], AF.Identity), sem="act")
                    pr.wait("vector", "act")
                    osl = slice(mo * T + th * 1024 + nn * 512,
                                mo * T + th * 1024 + nn * 512 + 512)
                    pr.emit("vector", lambda e, osl=osl: e.tensor_add(
                        oacc[:, osl], oacc[:, osl], t1b[:, :]), sem="oev")
                    pr.wait("scalar", "oev")
        pe_hist2.append(pr.counts.get("pe", 0))

    pr.wait("vector", "oev")
    for b in range(DB):
        pr.emit("vector", lambda e, b=b: e.scalar_tensor_tensor(
            xsb[:, b * T:(b + 1) * T], oacc[:, b * T:(b + 1) * T],
            w["outb_c"][:, b:b + 1], xsb[:, b * T:(b + 1) * T],
            op0=ALU.add, op1=ALU.add), sem="dve")


def build_l0_ffn_program():
    nc = bass.Bass("TRN2", target_bir_lowering=False)
    x_in = nc.dram_tensor("x_in", [D, T], F32, kind="ExternalInput").ap()
    x_out = nc.dram_tensor("x_out", [D, T], F32, kind="ExternalOutput").ap()
    d_WinT = nc.dram_tensor("WinT", [D, 64], F32, kind="ExternalInput").ap()
    d_bin = nc.dram_tensor("bin_", [64, 1], F32, kind="ExternalInput").ap()
    d_W1 = nc.dram_tensor("W1", [64, 16], F32, kind="ExternalInput").ap()
    d_b1 = nc.dram_tensor("b1", [64, 16], F32, kind="ExternalInput").ap()
    d_W2 = nc.dram_tensor("W2", [64, 16], F32, kind="ExternalInput").ap()
    d_b2 = nc.dram_tensor("b2", [64, 1], F32, kind="ExternalInput").ap()
    d_WoutT = nc.dram_tensor("WoutT", [64, D], F32, kind="ExternalInput").ap()
    d_bout = nc.dram_tensor("bout_c", [P, DB], F32, kind="ExternalInput").ap()
    d_ones = nc.dram_tensor("ones1", [1, P], F32, kind="ExternalInput").ap()
    d_onesc = nc.dram_tensor("ones_c", [P, 1], F32, kind="ExternalInput").ap()

    es = ExitStack()
    with es:
        pr = Prog(nc, es)
        xsb = pr.sb("xsb", [P, DB * T])
        xb = pr.sb("xb", [P, DB * T], BF16)
        sq = pr.sb("sq", [P, 2 * 512], BF16)
        rowsB = pr.sb("rowsB", [1, 4 * T])
        rep = pr.sb("rep", [P, 2 * T])
        s64 = pr.sb("s64", [P, T])
        g64 = pr.sb("g64", [P, T])
        relu = pr.sb("relu", [P, T])
        gb = pr.sb("gb", [64, T], BF16)
        tmp = pr.sb("tmp", [P, 512])
        ones_b = pr.sb("ones_b", [1, P], BF16)
        ones_f = pr.sb("ones_f", [1, P])
        ones_cf = pr.sb("ones_cf", [P, 1])
        ones_cb = pr.sb("ones_cb", [P, 1], BF16)
        WinT_f = pr.sb("WinT_f", [P, DB * 64])
        WinT = pr.sb("WinT_s", [P, DB * 64], BF16)
        WoutT_f = pr.sb("WoutT_f", [64, D])
        WoutT = pr.sb("WoutT_s", [64, D], BF16)
        W1c = pr.sb("W1c", [P, 8])
        b1c = pr.sb("b1c", [P, 8])
        W2c = pr.sb("W2c", [P, 8])
        binc = pr.sb("binc", [64, 1])
        b2c = pr.sb("b2c", [64, 1])
        bout_c = pr.sb("bout_c_s", [P, DB])
        bout_half = pr.sb("bout_half", [P, DB])
        p0 = pr.ps("p0")
        p1 = pr.ps("p1")

        def ld(dst, src):
            pr.emit("sync", lambda e, dst=dst, src=src: e.dma_start(
                out=dst, in_=src), sem=pr.ldsem, amt=16)

        for b in range(DB):
            ld(xsb[:, b * T:(b + 1) * T], x_in[b * P:(b + 1) * P, :])
        for b in range(DB):
            ld(WinT_f[:, b * 64:(b + 1) * 64], d_WinT[b * P:(b + 1) * P, :])
        ld(WoutT_f[:, :], d_WoutT[:, :])
        ld(W1c[0:64, :], d_W1[:, 0:8])
        ld(W1c[64:128, :], d_W1[:, 8:16])
        ld(b1c[0:64, :], d_b1[:, 0:8])
        ld(b1c[64:128, :], d_b1[:, 8:16])
        ld(W2c[0:64, :], d_W2[:, 0:8])
        ld(W2c[64:128, :], d_W2[:, 8:16])
        ld(binc[:, :], d_bin[:, :])
        ld(b2c[:, :], d_b2[:, :])
        ld(bout_c[:, :], d_bout[:, :])
        ld(ones_f[:, :], d_ones[:, :])
        ld(ones_cf[:, :], d_onesc[:, :])

        pr.wait("vector", pr.ldsem)
        pr.emit("vector", lambda e: e.tensor_copy(WinT[:, :], WinT_f[:, :]))
        pr.emit("vector", lambda e: e.tensor_copy(
            WoutT[0:64, :], WoutT_f[0:64, :]))
        pr.emit("vector", lambda e: e.tensor_copy(ones_b[:, :], ones_f[:, :]))
        pr.emit("vector", lambda e: e.tensor_copy(ones_cb[:, :], ones_cf[:, :]))
        pr.emit("vector", lambda e: e.tensor_scalar(
            bout_half[:, :], bout_c[:, :], 0.5, None, op0=ALU.mult), sem="dve")
        pr.wait("scalar", pr.ldsem)
        pr.wait("tensor", pr.ldsem)
        pr.wait("tensor", "dve")
        pr.wait("scalar", "dve")

        sc = dict(xb=xb, sq=sq, rowsB=rowsB, rep=rep, p0=p0, p1=p1,
                  ones_b=ones_b, ones_f=ones_f, ones_cb=ones_cb,
                  ones_cf=ones_cf, s64=s64, g64=g64,
                  relu=relu, gb=gb, tmp=tmp)
        w = dict(WinT=WinT, W1c=W1c, b1c=b1c, W2c=W2c, binc=binc, b2c=b2c,
                 WoutT=WoutT, bout_half=bout_half)
        emit_ln(pr, xsb, xb, None, sc)
        emit_l0_ffn(pr, xb, xsb, w, sc)

        pr.wait("sync", "dve")
        for b in range(DB):
            pr.emit("sync", lambda e, b=b: e.dma_start(
                out=x_out[b * P:(b + 1) * P, :],
                in_=xsb[:, b * T:(b + 1) * T]), sem="st", amt=16)
        pr.wait("sync", "st")
        pr.finalize()
    return nc


def prep_l0_ffn_inputs(inputs):
    c = common_consts()
    return {
        "WinT": np.ascontiguousarray(inputs["L0_ffn_Win"].astype(np.float32)),
        "bin_": np.ascontiguousarray(
            inputs["L0_ffn_bin_"].astype(np.float32).reshape(64, 1)),
        "W1": np.ascontiguousarray(inputs["L0_ffn_W1"].astype(np.float32)),
        "b1": np.ascontiguousarray(inputs["L0_ffn_b1"].astype(np.float32)),
        "W2": np.ascontiguousarray(inputs["L0_ffn_W2"].astype(np.float32)),
        "b2": np.ascontiguousarray(
            inputs["L0_ffn_b2"].astype(np.float32).reshape(64, 1)),
        "WoutT": np.ascontiguousarray(
            inputs["L0_ffn_Wout"].astype(np.float32)),
        "bout_c": _cols(inputs["L0_ffn_bout"]),
        "ones1": c["ones1"],
        "ones_c": c["ones_c"],
    }


def build_moe_program():
    nc = bass.Bass("TRN2", target_bir_lowering=False)
    x_in = nc.dram_tensor("x_in", [D, T], F32, kind="ExternalInput").ap()
    x_out = nc.dram_tensor("x_out", [D, T], F32, kind="ExternalOutput").ap()
    d_rwT = nc.dram_tensor("rwT", [D, NEXP], F32, kind="ExternalInput").ap()
    d_w1 = nc.dram_tensor("w1", [NEXP, D, INNER], F32,
                          kind="ExternalInput").ap()
    d_w2 = nc.dram_tensor("w2", [NEXP, INNER, D], F32,
                          kind="ExternalInput").ap()
    d_b1c = nc.dram_tensor("b1c", [NEXP, P, 16], F32,
                           kind="ExternalInput").ap()
    d_b2c = nc.dram_tensor("b2c", [NEXP, P, 8], F32,
                           kind="ExternalInput").ap()
    d_ones = nc.dram_tensor("ones1", [1, P], F32, kind="ExternalInput").ap()
    d_onesc = nc.dram_tensor("ones_c", [P, 1], F32, kind="ExternalInput").ap()
    d_ident = nc.dram_tensor("ident", [P, P], F32, kind="ExternalInput").ap()
    d_oh = nc.dram_tensor("onehot14", [NEXP, NEXP * P], F32,
                          kind="ExternalInput").ap()

    es = ExitStack()
    with es:
        pr = Prog(nc, es)
        xsb = pr.sb("xsb", [P, DB * T])
        xb = pr.sb("xb", [P, DB * T], BF16)
        ohb = pr.sb("ohb", [NEXP, NEXP * P], BF16)
        ones_b = pr.sb("ones_b", [1, P], BF16)
        ones_f = pr.sb("ones_f", [1, P])
        ones_cf = pr.sb("ones_cf", [P, 1])
        ones_cb = pr.sb("ones_cb", [P, 1], BF16)
        identf = pr.sb("identf", [P, P])
        msel_b = pr.sb("msel_b", [NEXP, T], BF16)
        ones14 = pr.sb("ones14", [P, NEXP])
        p0 = pr.ps("p0")
        p1 = pr.ps("p1")
        p2 = pr.ps("p2")
        p3 = pr.ps("p3")
        ph0 = pr.ps("ph0")
        ph1 = pr.ps("ph1")
        py0 = pr.ps("py0")
        py1 = pr.ps("py1")

        es2 = es.enter_context(ExitStack())
        sq = pr.sb("sq", [P, 2 * 512], BF16, es=es2)
        rowsB = pr.sb("rowsB", [1, 4 * T], es=es2)
        rep = pr.sb("rep", [P, 2 * T], es=es2)
        rwT = pr.sb("rwT_s", [P, DB * NEXP], es=es2)
        colsum_n = pr.sb("colsum_n", [NEXP, 1], es=es2)
        lgs = pr.sb("lgs", [NEXP, T], es=es2)
        lgT = pr.sb("lgT", [P, 16 * NEXP], es=es2)
        red = pr.sb("red", [P, 8], es=es2)
        mt = pr.sb("mt", [P, NEXP], es=es2)
        cums = pr.sb("cums", [P, NEXP], es=es2)
        msel = pr.sb("msel", [NEXP, T], es=es2)

        def ld(dst, src):
            pr.emit("sync", lambda e, dst=dst, src=src: e.dma_start(
                out=dst, in_=src), sem=pr.ldsem, amt=16)

        for b in range(DB):
            ld(xsb[:, b * T:(b + 1) * T], x_in[b * P:(b + 1) * P, :])
        for b in range(DB):
            ld(rwT[:, b * NEXP:(b + 1) * NEXP], d_rwT[b * P:(b + 1) * P, :])
        ld(ones_f[:, :], d_ones[:, :])
        ld(ones_cf[:, :], d_onesc[:, :])
        ld(identf[:, :], d_ident[:, :])
        ld(rep[0:NEXP, 0:NEXP * P], d_oh[:, :])

        pr.wait("vector", pr.ldsem)
        pr.emit("vector", lambda e: e.tensor_copy(ones_b[:, :], ones_f[:, :]))
        pr.emit("vector", lambda e: e.tensor_copy(ones_cb[:, :], ones_cf[:, :]))
        pr.emit("vector", lambda e: e.tensor_copy(
            ohb[:, :], rep[0:NEXP, 0:NEXP * P]))
        pr.emit("vector", lambda e: e.memset(ones14[:, :], 1.0), sem="dve")
        pr.wait("scalar", pr.ldsem)
        pr.wait("tensor", pr.ldsem)
        pr.wait("tensor", "dve")
        pr.wait("scalar", "dve")

        sc = dict(xb=xb, sq=sq, rowsB=rowsB, rep=rep, p0=p0, p1=p1,
                  ones_b=ones_b, ones_f=ones_f, ones_cb=ones_cb,
                  ones_cf=ones_cf)
        emit_ln(pr, xsb, xb, None, sc)
        r = dict(rwT=rwT, colsum_n=colsum_n, lgs=lgs, lgT=lgT, red=red, mt=mt,
                 cums=cums, ones14=ones14, msel=msel, msel_b=msel_b,
                 identf=identf, ones_f=ones_f, p2=p2, p3=p3)
        emit_router(pr, xsb, sc, r)
        pr.barrier("router_done")
        es2.close()

        w1s = pr.sb("w1s", [P, 8 * T], BF16)
        w2s = pr.sb("w2s", [P, 16 * 1024], BF16)
        w1f = pr.sb("w1f", [P, 2 * 1024])
        b1c = pr.sb("b1c_s", [P, NEXP * 16])
        b2c = pr.sb("b2c_s", [P, NEXP * 8])
        mselr = pr.sb("mselr", [P, 2 * T], BF16)
        hid = pr.sb("hid", [P, 16 * 512], BF16)
        t1 = pr.sb("t1", [P, 4 * 512], BF16)
        mm = dict(w1s=w1s, w2s=w2s, w1f=w1f, b1c=b1c, b2c=b2c,
                  mselr=mselr, hid=hid, t1=t1, msel_b=msel_b, ohb=ohb,
                  ph0=ph0, ph1=ph1, py0=py0, py1=py1, p2=p2,
                  p0=p0, p1=p1, p2b=p2, p3b=p3)
        dram = dict(d_w1=d_w1, d_w2=d_w2, d_b1c=d_b1c, d_b2c=d_b2c)
        emit_moe_experts(pr, xb, xsb, mm, dram)

        pr.wait("sync", "dve")
        pr.wait("sync", "yacc")
        for b in range(DB):
            pr.emit("sync", lambda e, b=b: e.dma_start(
                out=x_out[b * P:(b + 1) * P, :],
                in_=xsb[:, b * T:(b + 1) * T]), sem="st", amt=16)
        pr.wait("sync", "st")
        pr.finalize()
    return nc



def build_l0_attn_program():
    nc = bass.Bass("TRN2", target_bir_lowering=False)
    x_in = nc.dram_tensor("x_in", [D, T], F32, kind="ExternalInput").ap()
    x_out = nc.dram_tensor("x_out", [D, T], F32, kind="ExternalOutput").ap()
    d_WxT = nc.dram_tensor("WxT", [D, 96], F32, kind="ExternalInput").ap()
    d_WdtT = nc.dram_tensor("WdtT", [64, D], F32, kind="ExternalInput").ap()
    d_bdt = nc.dram_tensor("bdt_c", [P, DB], F32, kind="ExternalInput").ap()
    d_Alog = nc.dram_tensor("Alog_c", [P, DB * 16], F32, kind="ExternalInput").ap()
    d_Dc = nc.dram_tensor("D_c", [P, DB], F32, kind="ExternalInput").ap()
    d_ones = nc.dram_tensor("ones1", [1, P], F32, kind="ExternalInput").ap()
    d_onesc = nc.dram_tensor("ones_c", [P, 1], F32, kind="ExternalInput").ap()

    es = ExitStack()
    with es:
        pr = Prog(nc, es)
        xsb = pr.sb("xsb", [P, DB * T])
        xn = pr.sb("xn", [P, DB * T])
        ones_b = pr.sb("ones_b", [1, P], BF16)
        ones_f = pr.sb("ones_f", [1, P])
        ones_cf = pr.sb("ones_cf", [P, 1])
        ones_cb = pr.sb("ones_cb", [P, 1], BF16)
        WxdtT = pr.sb("WxdtT", [P, DB * 64], BF16)
        WxBT = pr.sb("WxBT", [P, DB * 16], BF16)
        WxCT = pr.sb("WxCT", [P, DB * 16], BF16)
        WdtT = pr.sb("WdtT_s", [64, D], BF16)
        bdt_c = pr.sb("bdt_c_s", [P, DB])
        A_cols = pr.sb("A_cols", [P, DB * 16])
        D_c = pr.sb("D_c_s", [P, DB])
        p0 = pr.ps("p0")
        p1 = pr.ps("p1")
        p2 = pr.ps("p2")
        p3 = pr.ps("p3")
        pB = pr.ps("pB")
        pC = pr.ps("pC")

        es2 = es.enter_context(ExitStack())
        sq = pr.sb("sq", [P, 2 * 512], F32, es=es2)
        rowsB = pr.sb("rowsB", [1, 4 * T], es=es2)
        rep = pr.sb("rep", [P, 2 * T], es=es2)
        wstage = pr.sb("wstage", [P, DB * 96], F32, es=es2)
        wstage2 = pr.sb("wstage2", [64, D], F32, es=es2)

        def ld(dst, src):
            pr.emit("sync", lambda e, dst=dst, src=src: e.dma_start(
                out=dst, in_=src), sem=pr.ldsem, amt=16)

        for b in range(DB):
            ld(xsb[:, b * T:(b + 1) * T], x_in[b * P:(b + 1) * P, :])
        for b in range(DB):
            ld(wstage[:, b * 96:(b + 1) * 96], d_WxT[b * P:(b + 1) * P, :])
        ld(wstage2[:, :], d_WdtT[:, :])
        ld(bdt_c[:, :], d_bdt[:, :])
        ld(A_cols[:, :], d_Alog[:, :])
        ld(D_c[:, :], d_Dc[:, :])
        ld(ones_f[:, :], d_ones[:, :])
        ld(ones_cf[:, :], d_onesc[:, :])

        pr.wait("vector", pr.ldsem)
        pr.emit("vector", lambda e: e.tensor_copy(ones_b[:, :], ones_f[:, :]))
        pr.emit("vector", lambda e: e.tensor_copy(ones_cb[:, :], ones_cf[:, :]))
        for b in range(DB):
            pr.emit("vector", lambda e, b=b: e.tensor_copy(
                WxdtT[:, b * 64:(b + 1) * 64],
                wstage[:, b * 96: b * 96 + 64]))
            pr.emit("vector", lambda e, b=b: e.tensor_copy(
                WxBT[:, b * 16:(b + 1) * 16],
                wstage[:, b * 96 + 64: b * 96 + 80]))
            pr.emit("vector", lambda e, b=b: e.tensor_copy(
                WxCT[:, b * 16:(b + 1) * 16],
                wstage[:, b * 96 + 80: b * 96 + 96]))
        pr.emit("vector", lambda e: e.tensor_copy(WdtT[0:64, :], wstage2[0:64, :]))
        pr.tag("vector", "dve")
        pr.wait("scalar", pr.ldsem)
        pr.emit("scalar", lambda e: e.activation(A_cols[:, :], A_cols[:, :],
                                                 AF.Exp), sem="act")
        pr.wait("vector", "act")
        pr.emit("vector", lambda e: e.tensor_scalar(
            A_cols[:, :], A_cols[:, :], -1.0, None, op0=ALU.mult), sem="dve")
        pr.wait("tensor", pr.ldsem)
        pr.wait("tensor", "dve")
        pr.wait("scalar", "dve")

        sc = dict(xb=None, sq=sq, rowsB=rowsB, rep=rep, p0=p0, p1=p1,
                  ones_b=ones_b, ones_f=ones_f, ones_cb=ones_cb,
                  ones_cf=ones_cf)
        emit_ln(pr, xsb, None, xn, sc)
        pr.barrier("ln_done")
        es2.close()

        xnb_c = pr.sb("xnb_c", [P, DB * 512], BF16)
        dtF = pr.sb("dtF", [P, DB * 512])
        dtlo = pr.sb("dtlo", [64, 512], BF16)
        Bsb = pr.sb("Bsb", [16, 512])
        Csb = pr.sb("Csb", [16, 512])
        h_state = pr.sb("h_state", [P, P])
        sp1 = pr.sb("sp1", [P, 512])
        sp2 = pr.sb("sp2", [P, 512])
        a_t = pr.sb("a_t", [P, 512])
        u2_t = pr.sb("u2_t", [P, 512])
        hs_t = pr.sb("hs_t", [P, 512])
        brow = pr.sb("brow", [1, 4 * 512])
        w = dict(WxdtT=WxdtT, WxBT=WxBT, WxCT=WxCT, WdtT=WdtT, bdt_c=bdt_c,
                 A_cols=A_cols, D_c=D_c)
        sc2 = dict(xnb_c=xnb_c, dtF=dtF, dtlo=dtlo, Bsb=Bsb, Csb=Csb,
                   h_state=h_state, sp1=sp1, sp2=sp2, a_t=a_t, u2_t=u2_t,
                   hs_t=hs_t, brow=brow, ones_f=ones_f,
                   p0=p0, p1=p1, p2=p2, p3=p3, pB=pB, pC=pC)
        emit_l0_attn(pr, xsb, xn, w, sc2)

        pr.wait("sync", "dve")
        for b in range(DB):
            pr.emit("sync", lambda e, b=b: e.dma_start(
                out=x_out[b * P:(b + 1) * P, :],
                in_=xsb[:, b * T:(b + 1) * T]), sem="st", amt=16)
        pr.wait("sync", "st")
        pr.finalize()
    return nc


def prep_l0_attn_inputs(inputs):
    c = common_consts()
    Alog = inputs["L0_attn_A_log"].astype(np.float32)
    Alog_c = np.ascontiguousarray(
        Alog.reshape(DB, P, 16).transpose(1, 0, 2).reshape(P, DB * 16))
    return {
        "WxT": np.ascontiguousarray(
            inputs["L0_attn_W_xproj"].astype(np.float32).T),
        "WdtT": np.ascontiguousarray(
            inputs["L0_attn_W_dt"].astype(np.float32).T),
        "bdt_c": _cols(inputs["L0_attn_b_dt"]),
        "Alog_c": Alog_c,
        "D_c": _cols(inputs["L0_attn_D_param"]),
        "ones1": c["ones1"],
        "ones_c": c["ones_c"],
    }



def emit_ln_zres(pr, xsb, znb, gamma_col, sc):
    rep, p0, p1 = sc["rep"], sc["p0"], sc["p1"]
    rowsB = sc["rowsB"]
    sq = sc["sq"]
    tmpb = sc["tmpb"]
    m_r = rowsB[0:1, 0:T]
    v_r = rowsB[0:1, T:2 * T]
    rstd_r = rowsB[0:1, 2 * T:3 * T]
    aux_r = rowsB[0:1, 3 * T:4 * T]

    pr.wait("tensor", "dve")
    pr.wait("scalar", "dve")
    slot_free_at = [pr.counts.get("pe", 0)] * 2
    for n in range(NT):
        for b in range(DB):
            pr.emit("tensor", lambda e, b=b, n=n: e.matmul(
                p0[0:1, :], sc["ones_cf"][:, 0:1],
                xsb[:, b * T + n * 512: b * T + (n + 1) * 512],
                start=(b == 0), stop=(b == DB - 1)),
                sem="pe" if b == DB - 1 else None)
        for b in range(DB):
            s = b % 2
            pr.wait("scalar", "pe", slot_free_at[s])
            pr.emit("scalar", lambda e, b=b, n=n, s=s: e.activation(
                sq[:, s * 512:(s + 1) * 512],
                xsb[:, b * T + n * 512: b * T + (n + 1) * 512],
                AF.Square), sem="act")
            pr.wait("tensor", "act")
            slot_free_at[s] = pr.emit("tensor", lambda e, s=s, b=b: e.matmul(
                p1[0:1, :], sc["ones_cf"][:, 0:1], sq[:, s * 512:(s + 1) * 512],
                start=(b == 0), stop=(b == DB - 1)), sem="pe")
        pr.wait("vector", "pe")
        pr.emit("vector", lambda e, n=n: e.tensor_copy(
            m_r[:, n * 512:(n + 1) * 512], p0[0:1, :]))
        pr.emit("vector", lambda e, n=n: e.tensor_copy(
            v_r[:, n * 512:(n + 1) * 512], p1[0:1, :]), sem="dve")
        pr.wait("tensor", "dve")

    def vs():
        pr.tag("vector", "dve")
        pr.wait("vector", "dve")

    pr.emit("vector", lambda e: e.tensor_scalar(
        m_r, m_r, 1.0 / D, None, op0=ALU.mult))
    vs()
    pr.emit("vector", lambda e: e.tensor_mul(aux_r, m_r, m_r))
    vs()
    pr.emit("vector", lambda e: e.tensor_scalar(
        v_r, v_r, 1.0 / D, EPS_LN, op0=ALU.mult, op1=ALU.add))
    vs()
    pr.emit("vector", lambda e: e.tensor_sub(v_r, v_r, aux_r))
    vs()
    pr.emit("vector", lambda e: e.tensor_scalar(
        aux_r, v_r, -EPS_LN, None, op0=ALU.add))
    vs()
    pr.emit("vector", lambda e: e.reciprocal(rstd_r, v_r))
    vs()
    pr.emit("vector", lambda e: e.tensor_mul(aux_r, aux_r, rstd_r))
    vs()
    pr.emit("vector", lambda e: e.tensor_scalar(
        aux_r, aux_r, EPS_RMS, None, op0=ALU.add), sem="dve")
    pr.wait("scalar", "dve")
    pr.emit("scalar", lambda e: e.activation(rstd_r, v_r, AF.Ln))
    pr.tag("scalar", "act")
    pr.wait("scalar", "act")
    pr.emit("scalar", lambda e: e.activation(
        rstd_r, rstd_r, AF.Exp, scale=-0.5))
    pr.tag("scalar", "act")
    pr.wait("scalar", "act")
    pr.emit("scalar", lambda e: e.activation(aux_r, aux_r, AF.Ln))
    pr.tag("scalar", "act")
    pr.wait("scalar", "act")
    pr.emit("scalar", lambda e: e.activation(
        aux_r, aux_r, AF.Exp, scale=-0.5), sem="act")

    pr.wait("tensor", "act")
    for n in range(NT):
        pr.emit("tensor", lambda e, n=n: e.matmul(
            p0[:, :], sc["ones_f"][0:1, :], m_r[:, n * 512:(n + 1) * 512],
            start=True, stop=True))
        pr.emit("tensor", lambda e, n=n: e.matmul(
            p1[:, :], sc["ones_f"][0:1, :], rstd_r[:, n * 512:(n + 1) * 512],
            start=True, stop=True), sem="pe")
        pr.wait("vector", "pe")
        pr.emit("vector", lambda e, n=n: e.tensor_copy(
            rep[:, n * 512:(n + 1) * 512], p0[:, :]))
        pr.emit("vector", lambda e, n=n: e.tensor_copy(
            rep[:, T + n * 512:T + (n + 1) * 512], p1[:, :]), sem="dve")
        pr.wait("tensor", "dve")
        pr.emit("tensor", lambda e, n=n: e.matmul(
            p0[:, :], sc["ones_f"][0:1, :], aux_r[:, n * 512:(n + 1) * 512],
            start=True, stop=True), sem="pe")
        pr.wait("vector", "pe")
        pr.emit("vector", lambda e, n=n: e.tensor_copy(
            rep[:, 2 * T + n * 512:2 * T + (n + 1) * 512], p0[:, :]),
            sem="dve")
        pr.wait("tensor", "dve")

    m_rep = rep[:, 0:T]
    rstd_rep = rep[:, T:2 * T]
    r2_rep = rep[:, 2 * T:3 * T]
    for b in range(DB):
        xb_sl = xsb[:, b * T:(b + 1) * T]
        zn_sl = znb[:, b * T:(b + 1) * T]
        pr.emit("vector", lambda e, xb_sl=xb_sl: e.tensor_sub(
            tmpb[:, :], xb_sl, m_rep))
        vs()
        pr.emit("vector", lambda e: e.tensor_mul(
            tmpb[:, :], tmpb[:, :], rstd_rep))
        vs()
        pr.emit("vector", lambda e, xb_sl=xb_sl: e.tensor_add(
            xb_sl, xb_sl, tmpb[:, :]))
        pr.emit("vector", lambda e, zn_sl=zn_sl: e.tensor_mul(
            zn_sl, tmpb[:, :], r2_rep))
        vs()
        pr.emit("vector", lambda e, zn_sl=zn_sl, b=b: e.tensor_scalar(
            zn_sl, zn_sl, sc["normw_c"][:, b:b + 1], None, op0=ALU.mult))
    pr.tag("vector", "dve")


def build_l1_attn_program():
    nc = bass.Bass("TRN2", target_bir_lowering=False)
    x_in = nc.dram_tensor("x_in", [D, T], F32, kind="ExternalInput").ap()
    x_out = nc.dram_tensor("x_out", [D, T], F32, kind="ExternalOutput").ap()
    d_inwT = nc.dram_tensor("inwT", [D, INNER], F32, kind="ExternalInput").ap()
    d_gatewT = nc.dram_tensor("gatewT", [D, INNER], F32, kind="ExternalInput").ap()
    d_bwT = nc.dram_tensor("bwT", [INNER, INNER], F32, kind="ExternalInput").ap()
    d_cwT = nc.dram_tensor("cwT", [INNER, INNER], F32, kind="ExternalInput").ap()
    d_dwT = nc.dram_tensor("dwT", [INNER, INNER], F32, kind="ExternalInput").ap()
    d_outwT = nc.dram_tensor("outwT", [INNER, D], F32, kind="ExternalInput").ap()
    d_inb = nc.dram_tensor("inb_c", [P, IBL], F32, kind="ExternalInput").ap()
    d_gateb = nc.dram_tensor("gateb_c", [P, IBL], F32, kind="ExternalInput").ap()
    d_bb = nc.dram_tensor("bb_c", [P, IBL], F32, kind="ExternalInput").ap()
    d_cb = nc.dram_tensor("cb_c", [P, IBL], F32, kind="ExternalInput").ap()
    d_db = nc.dram_tensor("db_c", [P, IBL], F32, kind="ExternalInput").ap()
    d_outb = nc.dram_tensor("outb_c", [P, DB], F32, kind="ExternalInput").ap()
    d_alog = nc.dram_tensor("alog_c", [P, IBL], F32, kind="ExternalInput").ap()
    d_normw = nc.dram_tensor("normw_c", [P, DB], F32, kind="ExternalInput").ap()
    d_ones = nc.dram_tensor("ones1", [1, P], F32, kind="ExternalInput").ap()
    d_onesc = nc.dram_tensor("ones_c", [P, 1], F32, kind="ExternalInput").ap()

    es = ExitStack()
    with es:
        pr = Prog(nc, es)
        xsb = pr.sb("xsb", [P, DB * T])
        znb = pr.sb("znb", [P, DB * T], BF16)
        ones_f = pr.sb("ones_f", [1, P])
        ones_cf = pr.sb("ones_cf", [P, 1])
        inb_c = pr.sb("inb_c_s", [P, IBL])
        gateb_c = pr.sb("gateb_c_s", [P, IBL])
        bb_c = pr.sb("bb_c_s", [P, IBL])
        cb_c = pr.sb("cb_c_s", [P, IBL])
        db_c = pr.sb("db_c_s", [P, IBL])
        outb_c = pr.sb("outb_c_s", [P, DB])
        a_dec = pr.sb("a_dec", [P, IBL])
        normw_c = pr.sb("normw_c_s", [P, DB])
        asp = pr.sb("asp", [P, 2 * IBL])
        p0 = pr.ps("p0")
        p1 = pr.ps("p1")
        p2 = pr.ps("p2")
        p3 = pr.ps("p3")
        p4 = pr.ps("p4")

        def ld(dst, srcd):
            pr.emit("sync", lambda e, dst=dst, srcd=srcd: e.dma_start(
                out=dst, in_=srcd), sem=pr.ldsem, amt=16)

        for b in range(DB):
            ld(xsb[:, b * T:(b + 1) * T], x_in[b * P:(b + 1) * P, :])
        ld(inb_c[:, :], d_inb[:, :])
        ld(gateb_c[:, :], d_gateb[:, :])
        ld(bb_c[:, :], d_bb[:, :])
        ld(cb_c[:, :], d_cb[:, :])
        ld(db_c[:, :], d_db[:, :])
        ld(outb_c[:, :], d_outb[:, :])
        ld(a_dec[:, :], d_alog[:, :])
        ld(normw_c[:, :], d_normw[:, :])
        ld(ones_f[:, :], d_ones[:, :])
        ld(ones_cf[:, :], d_onesc[:, :])

        def ss():
            pr.tag("scalar", "act")
            pr.wait("scalar", "act")

        pr.wait("scalar", pr.ldsem)
        pr.emit("scalar", lambda e: e.activation(asp[:, 0:IBL], a_dec[:, :],
                                                 AF.Relu))
        ss()
        pr.emit("scalar", lambda e: e.activation(asp[:, IBL:2 * IBL],
                                                 a_dec[:, :], AF.Abs))
        ss()
        pr.emit("scalar", lambda e: e.activation(
            asp[:, IBL:2 * IBL], asp[:, IBL:2 * IBL], AF.Exp, scale=-1.0))
        ss()
        pr.emit("scalar", lambda e: e.activation(
            asp[:, IBL:2 * IBL], asp[:, IBL:2 * IBL], AF.Ln, bias=1.0),
            sem="act")
        pr.wait("vector", "act")
        pr.emit("vector", lambda e: e.tensor_add(
            asp[:, 0:IBL], asp[:, 0:IBL], asp[:, IBL:2 * IBL]), sem="dve")
        pr.wait("scalar", "dve")
        pr.emit("scalar", lambda e: e.activation(
            a_dec[:, :], asp[:, 0:IBL], AF.Exp, scale=-1.0), sem="act")
        pr.wait("vector", pr.ldsem)
        pr.wait("tensor", pr.ldsem)
        pr.wait("vector", "act")
        pr.wait("tensor", "act")

        es2 = es.enter_context(ExitStack())
        sq = pr.sb("sq", [P, 2 * 512], F32, es=es2)
        rowsB = pr.sb("rowsB", [1, 4 * T], es=es2)
        rep = pr.sb("rep", [P, 3 * T], es=es2)
        tmpb = pr.sb("tmpb", [P, T], es=es2)
        sc = dict(sq=sq, rowsB=rowsB, rep=rep, tmpb=tmpb, ones_cf=ones_cf,
                  ones_f=ones_f, normw_c=normw_c, p0=p0, p1=p1)
        emit_ln_zres(pr, xsb, znb, normw_c, sc)
        pr.barrier("zres_done")
        es2.close()

        u_bf = pr.sb("u_bf", [P, IBL * T], BF16)
        es3 = es.enter_context(ExitStack())
        wstg1 = pr.sb("wstg1", [P, 2 * 4 * P], F32, es=es3)
        wcol_in = pr.sb("wcol_in", [P, 2 * DB * P], BF16, es=es3)
        wcol_g = pr.sb("wcol_g", [P, 2 * DB * P], BF16, es=es3)
        t2 = pr.sb("t2", [P, 512], F32, es=es3)
        t3 = pr.sb("t3", [P, 512], F32, es=es3)
        w1 = dict(d_inwT=d_inwT, d_gatewT=d_gatewT, inb_c=inb_c,
                  gateb_c=gateb_c)
        sc1 = dict(wstg=wstg1, wcol_in=wcol_in, wcol_g=wcol_g, t2=t2, t3=t3,
                   pu=p0, pg=p1)
        emit_l1_attn_e1(pr, znb, u_bf, w1, sc1)
        pr.barrier("e1_done")
        es3.close()

        wstg = pr.sb("wstg", [P, 2 * 4 * P], F32)
        wb = pr.sb("wb", [P, IBL * P], BF16)
        wc = pr.sb("wc", [P, IBL * P], BF16)
        wd = pr.sb("wd", [P, IBL * P], BF16)
        wo = pr.sb("wo", [P, 1024], BF16)
        bq = pr.sb("bq", [P, 1024])
        cq = pr.sb("cq", [P, 1024])
        dq = pr.sb("dq", [P, 1024])
        hsq = pr.sb("hsq", [P, 1024])
        a_rep = pr.sb("a_rep", [P, 1024])
        onesf_t = pr.sb("onesf_t", [P, 1024])
        carry = pr.sb("carry", [P, IBL])
        y_bf = pr.sb("y_bf", [P, 1024], BF16)
        t1b = pr.sb("t1b", [P, 512], BF16)
        pr.emit("vector", lambda e: e.memset(onesf_t[:, :], 1.0), sem="dve")
        w2 = dict(d_bwT=d_bwT, d_cwT=d_cwT, d_dwT=d_dwT, d_outwT=d_outwT,
                  bb_c=bb_c, cb_c=cb_c, db_c=db_c, outb_c=outb_c,
                  a_dec=a_dec)
        sc2 = dict(wstg=wstg, wb=wb, wc=wc, wd=wd, wo=wo, bq=bq, cq=cq,
                   dq=dq, hsq=hsq, a_rep=a_rep, onesf_t=onesf_t, carry=carry,
                   y_bf=y_bf, oacc=znb, t1b=t1b,
                   pb_=p0, pc_=p1, pd_=p2, po_=[p3, p4])
        emit_l1_attn_e2(pr, u_bf, xsb, w2, sc2)

        pr.wait("sync", "dve")
        for b in range(DB):
            pr.emit("sync", lambda e, b=b: e.dma_start(
                out=x_out[b * P:(b + 1) * P, :],
                in_=xsb[:, b * T:(b + 1) * T]), sem="st", amt=16)
        pr.wait("sync", "st")
        pr.finalize()
    return nc


def prep_l1_attn_inputs(inputs):
    c = common_consts()
    f = np.float32
    return {
        "inwT": np.ascontiguousarray(inputs["L1_attn_in_w"].astype(f).T),
        "gatewT": np.ascontiguousarray(inputs["L1_attn_gate_w"].astype(f).T),
        "bwT": np.ascontiguousarray(inputs["L1_attn_b_w"].astype(f).T),
        "cwT": np.ascontiguousarray(inputs["L1_attn_c_w"].astype(f).T),
        "dwT": np.ascontiguousarray(inputs["L1_attn_d_w"].astype(f).T),
        "outwT": np.ascontiguousarray(inputs["L1_attn_out_w"].astype(f).T),
        "inb_c": _cols(inputs["L1_attn_in_b"]),
        "gateb_c": _cols(inputs["L1_attn_gate_b"]),
        "bb_c": _cols(inputs["L1_attn_b_b"]),
        "cb_c": _cols(inputs["L1_attn_c_b"]),
        "db_c": _cols(inputs["L1_attn_d_b"]),
        "outb_c": _cols(inputs["L1_attn_out_b"]),
        "alog_c": _cols(inputs["L1_attn_a_log"]),
        "normw_c": _cols(inputs["L1_attn_norm_w"]),
        "ones1": c["ones1"],
        "ones_c": c["ones_c"],
    }


def prep_moe_inputs(inputs):
    c = common_consts()
    b1 = inputs["L1_ffn_b1"].astype(np.float32)
    b2 = inputs["L1_ffn_b2"].astype(np.float32)
    return {
        "rwT": np.ascontiguousarray(
            inputs["L1_ffn_router_w"].astype(np.float32)),
        "w1": np.ascontiguousarray(inputs["L1_ffn_w1"].astype(np.float32)),
        "w2": np.ascontiguousarray(inputs["L1_ffn_w2"].astype(np.float32)),
        "b1c": np.ascontiguousarray(
            np.stack([_cols(b1[e]) for e in range(NEXP)])),
        "b2c": np.ascontiguousarray(
            np.stack([_cols(b2[e]) for e in range(NEXP)])),
        "ones1": c["ones1"],
        "ones_c": c["ones_c"],
        "ident": c["ident"],
        "onehot14": c["onehot14"],
    }




def build_fused_program():
    nc = bass.Bass("TRN2", target_bir_lowering=False)
    f = F32
    x_in = nc.dram_tensor("x_in", [D, T], f, kind="ExternalInput").ap()
    x_out = nc.dram_tensor("x_out", [D, T], f, kind="ExternalOutput").ap()
    dd = {}
    def din(name, shape):
        dd[name] = nc.dram_tensor(name, list(shape), f, kind="ExternalInput").ap()
    din("WinT", [D, 64]); din("bin_", [64, 1]); din("W1", [64, 16])
    din("b1", [64, 16]); din("W2", [64, 16]); din("b2", [64, 1])
    din("WoutT", [64, D]); din("bout_c", [P, DB])
    din("WxT", [D, 96]); din("WdtT", [64, D]); din("bdt_c", [P, DB])
    din("Alog_c", [P, DB * 16]); din("D_c", [P, DB])
    din("rwT", [D, NEXP]); din("w1", [NEXP, D, INNER]); din("w2", [NEXP, INNER, D])
    din("b1c", [NEXP, P, 16]); din("b2c", [NEXP, P, 8]); din("onehot14", [NEXP, NEXP * P])
    din("inwT", [D, INNER]); din("gatewT", [D, INNER]); din("bwT", [INNER, INNER])
    din("cwT", [INNER, INNER]); din("dwT", [INNER, INNER]); din("outwT", [INNER, D])
    din("inb_c", [P, IBL]); din("gateb_c", [P, IBL]); din("bb_c", [P, IBL])
    din("cb_c", [P, IBL]); din("db_c", [P, IBL]); din("outb_c", [P, DB])
    din("alog_c", [P, IBL]); din("normw_c", [P, DB])
    din("ones1", [1, P]); din("ones_c", [P, 1]); din("ident", [P, P])
    d_w1b = nc.dram_tensor("w1bcache", [NEXP, D, INNER], BF16,
                           kind="Internal").ap()
    d_w2b = nc.dram_tensor("w2bcache", [NEXP, INNER, D], BF16,
                           kind="Internal").ap()

    es = ExitStack()
    with es:
        pr = Prog(nc, es)
        xsb = pr.sb("xsb", [P, DB * T])
        ones_b = pr.sb("ones_b", [1, P], BF16)
        ones_f = pr.sb("ones_f", [1, P])
        ones_cf = pr.sb("ones_cf", [P, 1])
        ones_cb = pr.sb("ones_cb", [P, 1], BF16)
        identf = pr.sb("identf", [P, P])
        junk = pr.sb("junk", [P, 8])
        psum = [pr.ps(f"pp{i}") for i in range(8)]

        def ld(dst, srcd, es_=None):
            pr.emit("sync", lambda e, dst=dst, srcd=srcd: e.dma_start(
                out=dst, in_=srcd), sem=pr.ldsem, amt=16)

        for b in range(DB):
            ld(xsb[:, b * T:(b + 1) * T], x_in[b * P:(b + 1) * P, :])
        ld(ones_f[:, :], dd["ones1"][:, :])
        ld(ones_cf[:, :], dd["ones_c"][:, :])
        ld(identf[:, :], dd["ident"][:, :])
        pr.wait("vector", pr.ldsem)
        pr.emit("vector", lambda e: e.memset(junk[:, :], 0.0))
        pr.emit("vector", lambda e: e.tensor_copy(ones_b[:, :], ones_f[:, :]))
        pr.emit("vector", lambda e: e.tensor_copy(ones_cb[:, :], ones_cf[:, :]),
                sem="dve")
        pr.wait("scalar", pr.ldsem)
        pr.wait("tensor", pr.ldsem)
        pr.wait("tensor", "dve")
        pr.wait("scalar", "dve")

        def endphase(tag):
            pr.wait("vector", pr.ldsem)
            pr.emit("vector", lambda e: e.tensor_copy(junk[:, 0:4],
                                                      junk[:, 4:8]))
            pr.tag("vector", "dve")
            pr.barrier(tag)

        def phase_l0_ffn(tag):
            s2 = es.enter_context(ExitStack())
            xb = pr.sb(f"xb_{tag}", [P, DB * T], BF16, es=s2)
            sq = pr.sb(f"sq_{tag}", [P, 2 * 512], BF16, es=s2)
            rowsB = pr.sb(f"rowsB_{tag}", [1, 4 * T], es=s2)
            rep = pr.sb(f"rep_{tag}", [P, 2 * T], es=s2)
            s64 = pr.sb(f"s64_{tag}", [P, T], es=s2)
            g64 = pr.sb(f"g64_{tag}", [P, T], es=s2)
            relu = pr.sb(f"relu_{tag}", [P, T], es=s2)
            gb = pr.sb(f"gb_{tag}", [64, T], BF16, es=s2)
            tmp = pr.sb(f"tmp_{tag}", [P, 512], es=s2)
            WinT_f = pr.sb(f"WinT_f_{tag}", [P, DB * 64], es=s2)
            WinT = pr.sb(f"WinT_s_{tag}", [P, DB * 64], BF16, es=s2)
            WoutT_f = pr.sb(f"WoutT_f_{tag}", [64, D], es=s2)
            WoutT = pr.sb(f"WoutT_s_{tag}", [64, D], BF16, es=s2)
            W1c = pr.sb(f"W1c_{tag}", [P, 8], es=s2)
            b1c = pr.sb(f"b1c_{tag}", [P, 8], es=s2)
            W2c = pr.sb(f"W2c_{tag}", [P, 8], es=s2)
            binc = pr.sb(f"binc_{tag}", [64, 1], es=s2)
            b2c = pr.sb(f"b2c_{tag}", [64, 1], es=s2)
            bout_c = pr.sb(f"bout_c_{tag}", [P, DB], es=s2)
            bout_half = pr.sb(f"bout_half_{tag}", [P, DB], es=s2)
            for b in range(DB):
                ld(xb[:, b * 64:(b + 1) * 64] if False else
                   WinT_f[:, b * 64:(b + 1) * 64],
                   dd["WinT"][b * P:(b + 1) * P, :])
            ld(WoutT_f[:, :], dd["WoutT"][:, :])
            ld(W1c[0:64, :], dd["W1"][:, 0:8]); ld(W1c[64:128, :], dd["W1"][:, 8:16])
            ld(b1c[0:64, :], dd["b1"][:, 0:8]); ld(b1c[64:128, :], dd["b1"][:, 8:16])
            ld(W2c[0:64, :], dd["W2"][:, 0:8]); ld(W2c[64:128, :], dd["W2"][:, 8:16])
            ld(binc[:, :], dd["bin_"][:, :]); ld(b2c[:, :], dd["b2"][:, :])
            ld(bout_c[:, :], dd["bout_c"][:, :])
            pr.wait("vector", pr.ldsem)
            pr.emit("vector", lambda e: e.tensor_copy(WinT[:, :], WinT_f[:, :]))
            pr.emit("vector", lambda e: e.tensor_copy(WoutT[0:64, :],
                                                      WoutT_f[0:64, :]))
            pr.emit("vector", lambda e: e.tensor_scalar(
                bout_half[:, :], bout_c[:, :], 0.5, None, op0=ALU.mult),
                sem="dve")
            pr.wait("tensor", "dve")
            pr.wait("scalar", "dve")
            sc = dict(xb=xb, sq=sq, rowsB=rowsB, rep=rep, p0=psum[0],
                      p1=psum[1], ones_b=ones_b, ones_f=ones_f,
                      ones_cb=ones_cb, ones_cf=ones_cf, s64=s64, g64=g64,
                      relu=relu, gb=gb, tmp=tmp)
            w = dict(WinT=WinT, W1c=W1c, b1c=b1c, W2c=W2c, binc=binc,
                     b2c=b2c, WoutT=WoutT, bout_half=bout_half)
            emit_ln(pr, xsb, xb, None, sc)
            emit_l0_ffn(pr, xb, xsb, w, sc)
            endphase(tag)
            s2.close()

        def phase_l0_attn():
            s2 = es.enter_context(ExitStack())
            xn = pr.sb("xn_B", [P, DB * T], es=s2)
            WxdtT = pr.sb("WxdtT_B", [P, DB * 64], BF16, es=s2)
            WxBT = pr.sb("WxBT_B", [P, DB * 16], BF16, es=s2)
            WxCT = pr.sb("WxCT_B", [P, DB * 16], BF16, es=s2)
            WdtT = pr.sb("WdtT_B", [64, D], BF16, es=s2)
            bdt_c = pr.sb("bdt_c_B", [P, DB], es=s2)
            A_cols = pr.sb("A_cols_B", [P, DB * 16], es=s2)
            D_c = pr.sb("D_c_B", [P, DB], es=s2)
            s3 = es.enter_context(ExitStack())
            sq = pr.sb("sq_B", [P, 2 * 512], F32, es=s3)
            rowsB = pr.sb("rowsB_B", [1, 4 * T], es=s3)
            rep = pr.sb("rep_B", [P, 2 * T], es=s3)
            wstage = pr.sb("wstage_B", [P, DB * 96], F32, es=s3)
            wstage2 = pr.sb("wstage2_B", [64, D], F32, es=s3)
            for b in range(DB):
                ld(wstage[:, b * 96:(b + 1) * 96], dd["WxT"][b * P:(b + 1) * P, :])
            ld(wstage2[:, :], dd["WdtT"][:, :])
            ld(bdt_c[:, :], dd["bdt_c"][:, :])
            ld(A_cols[:, :], dd["Alog_c"][:, :])
            ld(D_c[:, :], dd["D_c"][:, :])
            pr.wait("vector", pr.ldsem)
            for b in range(DB):
                pr.emit("vector", lambda e, b=b: e.tensor_copy(
                    WxdtT[:, b * 64:(b + 1) * 64], wstage[:, b * 96: b * 96 + 64]))
                pr.emit("vector", lambda e, b=b: e.tensor_copy(
                    WxBT[:, b * 16:(b + 1) * 16], wstage[:, b * 96 + 64: b * 96 + 80]))
                pr.emit("vector", lambda e, b=b: e.tensor_copy(
                    WxCT[:, b * 16:(b + 1) * 16], wstage[:, b * 96 + 80: b * 96 + 96]))
            pr.emit("vector", lambda e: e.tensor_copy(WdtT[0:64, :], wstage2[0:64, :]))
            pr.tag("vector", "dve")
            pr.wait("scalar", pr.ldsem)
            pr.wait("scalar", "dve")
            pr.emit("scalar", lambda e: e.activation(A_cols[:, :], A_cols[:, :],
                                                     AF.Exp), sem="act")
            pr.wait("vector", "act")
            pr.emit("vector", lambda e: e.tensor_scalar(
                A_cols[:, :], A_cols[:, :], -1.0, None, op0=ALU.mult), sem="dve")
            pr.wait("tensor", "dve")
            pr.wait("scalar", "dve")
            sc = dict(xb=None, sq=sq, rowsB=rowsB, rep=rep, p0=psum[0],
                      p1=psum[1], ones_b=ones_b, ones_f=ones_f,
                      ones_cb=ones_cb, ones_cf=ones_cf)
            emit_ln(pr, xsb, None, xn, sc)
            pr.barrier("B_ln_done")
            s3.close()
            xnb_c = pr.sb("xnb_c_B", [P, DB * 512], BF16, es=s2)
            dtF = pr.sb("dtF_B", [P, DB * 512], es=s2)
            dtlo = pr.sb("dtlo_B", [64, 512], BF16, es=s2)
            Bsb = pr.sb("Bsb_B", [16, 512], es=s2)
            Csb = pr.sb("Csb_B", [16, 512], es=s2)
            h_state = pr.sb("h_state_B", [P, P], es=s2)
            sp1 = pr.sb("sp1_B", [P, 512], es=s2)
            sp2 = pr.sb("sp2_B", [P, 512], es=s2)
            a_t = pr.sb("a_t_B", [P, 512], es=s2)
            u2_t = pr.sb("u2_t_B", [P, 512], es=s2)
            hs_t = pr.sb("hs_t_B", [P, 512], es=s2)
            brow = pr.sb("brow_B", [1, 4 * 512], es=s2)
            w = dict(WxdtT=WxdtT, WxBT=WxBT, WxCT=WxCT, WdtT=WdtT,
                     bdt_c=bdt_c, A_cols=A_cols, D_c=D_c)
            sc2 = dict(xnb_c=xnb_c, dtF=dtF, dtlo=dtlo, Bsb=Bsb, Csb=Csb,
                       h_state=h_state, sp1=sp1, sp2=sp2, a_t=a_t, u2_t=u2_t,
                       hs_t=hs_t, brow=brow, ones_f=ones_f,
                       p0=psum[0], p1=psum[1], p2=psum[2], p3=psum[3],
                       pB=psum[4], pC=psum[5])
            emit_l0_attn(pr, xsb, xn, w, sc2)
            endphase("B_done")
            s2.close()

        def phase_moe(tag):
            s2 = es.enter_context(ExitStack())
            xb = pr.sb(f"xb_{tag}", [P, DB * T], BF16, es=s2)
            ohb = pr.sb(f"ohb_{tag}", [NEXP, NEXP * P], BF16, es=s2)
            msel_b = pr.sb(f"msel_b_{tag}", [NEXP, T], BF16, es=s2)
            ones14 = pr.sb(f"ones14_{tag}", [P, NEXP], es=s2)
            s3 = es.enter_context(ExitStack())
            sq = pr.sb(f"sq_{tag}", [P, 2 * 512], BF16, es=s3)
            rowsB = pr.sb(f"rowsB_{tag}", [1, 4 * T], es=s3)
            rep = pr.sb(f"rep_{tag}", [P, 2 * T], es=s3)
            rwT = pr.sb(f"rwT_{tag}", [P, DB * NEXP], es=s3)
            colsum_n = pr.sb(f"colsum_{tag}", [NEXP, 1], es=s3)
            lgs = pr.sb(f"lgs_{tag}", [NEXP, T], es=s3)
            lgT = pr.sb(f"lgT_{tag}", [P, 16 * NEXP], es=s3)
            red = pr.sb(f"red_{tag}", [P, 8], es=s3)
            mt = pr.sb(f"mt_{tag}", [P, NEXP], es=s3)
            cums = pr.sb(f"cums_{tag}", [P, NEXP], es=s3)
            msel = pr.sb(f"msel_{tag}", [NEXP, T], es=s3)
            for b in range(DB):
                ld(rwT[:, b * NEXP:(b + 1) * NEXP], dd["rwT"][b * P:(b + 1) * P, :])
            ld(rep[0:NEXP, 0:NEXP * P], dd["onehot14"][:, :])
            pr.wait("vector", pr.ldsem)
            pr.emit("vector", lambda e: e.tensor_copy(
                ohb[:, :], rep[0:NEXP, 0:NEXP * P]))
            pr.emit("vector", lambda e: e.memset(ones14[:, :], 1.0), sem="dve")
            pr.wait("tensor", "dve")
            pr.wait("scalar", "dve")
            sc = dict(xb=xb, sq=sq, rowsB=rowsB, rep=rep, p0=psum[0],
                      p1=psum[1], ones_b=ones_b, ones_f=ones_f,
                      ones_cb=ones_cb, ones_cf=ones_cf)
            emit_ln(pr, xsb, xb, None, sc)
            r = dict(rwT=rwT, colsum_n=colsum_n, lgs=lgs, lgT=lgT, red=red,
                     mt=mt, cums=cums, ones14=ones14, msel=msel,
                     msel_b=msel_b, identf=identf, ones_f=ones_f,
                     p2=psum[2], p3=psum[3])
            emit_router(pr, xsb, sc, r)
            pr.barrier(f"router_done_{tag}")
            s3.close()
            w1s = pr.sb(f"w1s_{tag}", [P, 8 * T], BF16, es=s2)
            w2s = pr.sb(f"w2s_{tag}", [P, 16 * 1024], BF16, es=s2)
            w1f = pr.sb(f"w1f_{tag}", [P, 2 * 1024], es=s2)
            b1c = pr.sb(f"b1cs_{tag}", [P, NEXP * 16], es=s2)
            b2c = pr.sb(f"b2cs_{tag}", [P, NEXP * 8], es=s2)
            mselr = pr.sb(f"mselr_{tag}", [P, 2 * T], BF16, es=s2)
            hid = pr.sb(f"hid_{tag}", [P, 16 * 512], BF16, es=s2)
            t1 = pr.sb(f"t1_{tag}", [P, 4 * 512], BF16, es=s2)
            mm = dict(w1s=w1s, w2s=w2s, w1f=w1f, b1c=b1c, b2c=b2c,
                      mselr=mselr, hid=hid, t1=t1, msel_b=msel_b, ohb=ohb,
                      ph0=psum[4], ph1=psum[5], py0=psum[6], py1=psum[7],
                      p2=psum[2], p0=psum[0], p1=psum[1], p2b=psum[2],
                      p3b=psum[3])
            dram = dict(d_w1=dd["w1"], d_w2=dd["w2"], d_b1c=dd["b1c"],
                        d_b2c=dd["b2c"], d_w1b=d_w1b, d_w2b=d_w2b)
            emit_moe_experts(pr, xb, xsb, mm, dram,
                             cache="out" if tag == "Dx" else "in")
            pr.wait("vector", "yacc")
            endphase(tag)
            s2.close()

        def phase_l1_attn():
            s2 = es.enter_context(ExitStack())
            znb = pr.sb("znb_E", [P, DB * T], BF16, es=s2)
            inb_c = pr.sb("inb_c_E", [P, IBL], es=s2)
            gateb_c = pr.sb("gateb_c_E", [P, IBL], es=s2)
            bb_c = pr.sb("bb_c_E", [P, IBL], es=s2)
            cb_c = pr.sb("cb_c_E", [P, IBL], es=s2)
            db_c = pr.sb("db_c_E", [P, IBL], es=s2)
            outb_c = pr.sb("outb_c_E", [P, DB], es=s2)
            a_dec = pr.sb("a_dec_E", [P, IBL], es=s2)
            normw_c = pr.sb("normw_c_E", [P, DB], es=s2)
            asp = pr.sb("asp_E", [P, 2 * IBL], es=s2)
            ld(inb_c[:, :], dd["inb_c"][:, :])
            ld(gateb_c[:, :], dd["gateb_c"][:, :])
            ld(bb_c[:, :], dd["bb_c"][:, :])
            ld(cb_c[:, :], dd["cb_c"][:, :])
            ld(db_c[:, :], dd["db_c"][:, :])
            ld(outb_c[:, :], dd["outb_c"][:, :])
            ld(a_dec[:, :], dd["alog_c"][:, :])
            ld(normw_c[:, :], dd["normw_c"][:, :])

            def ss():
                pr.tag("scalar", "act")
                pr.wait("scalar", "act")

            pr.wait("scalar", pr.ldsem)
            pr.emit("scalar", lambda e: e.activation(asp[:, 0:IBL], a_dec[:, :],
                                                     AF.Relu))
            ss()
            pr.emit("scalar", lambda e: e.activation(asp[:, IBL:2 * IBL],
                                                     a_dec[:, :], AF.Abs))
            ss()
            pr.emit("scalar", lambda e: e.activation(
                asp[:, IBL:2 * IBL], asp[:, IBL:2 * IBL], AF.Exp, scale=-1.0))
            ss()
            pr.emit("scalar", lambda e: e.activation(
                asp[:, IBL:2 * IBL], asp[:, IBL:2 * IBL], AF.Ln, bias=1.0),
                sem="act")
            pr.wait("vector", "act")
            pr.emit("vector", lambda e: e.tensor_add(
                asp[:, 0:IBL], asp[:, 0:IBL], asp[:, IBL:2 * IBL]), sem="dve")
            pr.wait("scalar", "dve")
            pr.emit("scalar", lambda e: e.activation(
                a_dec[:, :], asp[:, 0:IBL], AF.Exp, scale=-1.0), sem="act")
            pr.wait("vector", pr.ldsem)
            pr.wait("tensor", pr.ldsem)
            pr.wait("vector", "act")
            pr.wait("tensor", "act")

            s3 = es.enter_context(ExitStack())
            sq = pr.sb("sq_E", [P, 2 * 512], F32, es=s3)
            rowsB = pr.sb("rowsB_E", [1, 4 * T], es=s3)
            rep = pr.sb("rep_E", [P, 3 * T], es=s3)
            tmpb = pr.sb("tmpb_E", [P, T], es=s3)
            sc = dict(sq=sq, rowsB=rowsB, rep=rep, tmpb=tmpb, ones_cf=ones_cf,
                      ones_f=ones_f, normw_c=normw_c, p0=psum[0], p1=psum[1])
            emit_ln_zres(pr, xsb, znb, normw_c, sc)
            pr.barrier("E_zres_done")
            s3.close()
            u_bf = pr.sb("u_bf_E", [P, IBL * T], BF16, es=s2)
            s4 = es.enter_context(ExitStack())
            wstg1 = pr.sb("wstg1_E", [P, 2 * 4 * P], F32, es=s4)
            wcol_in = pr.sb("wcol_in_E", [P, 2 * DB * P], BF16, es=s4)
            wcol_g = pr.sb("wcol_g_E", [P, 2 * DB * P], BF16, es=s4)
            t2 = pr.sb("t2_E", [P, 512], F32, es=s4)
            t3 = pr.sb("t3_E", [P, 512], F32, es=s4)
            w1 = dict(d_inwT=dd["inwT"], d_gatewT=dd["gatewT"], inb_c=inb_c,
                      gateb_c=gateb_c)
            sc1 = dict(wstg=wstg1, wcol_in=wcol_in, wcol_g=wcol_g, t2=t2,
                       t3=t3, pu=psum[0], pg=psum[1])
            emit_l1_attn_e1(pr, znb, u_bf, w1, sc1)
            pr.barrier("E_e1_done")
            s4.close()
            wstg = pr.sb("wstg_E", [P, 2 * 4 * P], es=s2)
            wb = pr.sb("wb_E", [P, IBL * P], BF16, es=s2)
            wc = pr.sb("wc_E", [P, IBL * P], BF16, es=s2)
            wd = pr.sb("wd_E", [P, IBL * P], BF16, es=s2)
            wo = pr.sb("wo_E", [P, 1024], BF16, es=s2)
            bq = pr.sb("bq_E", [P, 1024], es=s2)
            cq = pr.sb("cq_E", [P, 1024], es=s2)
            dq = pr.sb("dq_E", [P, 1024], es=s2)
            hsq = pr.sb("hsq_E", [P, 1024], es=s2)
            a_rep = pr.sb("a_rep_E", [P, 1024], es=s2)
            onesf_t = pr.sb("onesf_t_E", [P, 1024], es=s2)
            carry = pr.sb("carry_E", [P, IBL], es=s2)
            y_bf = pr.sb("y_bf_E", [P, 1024], BF16, es=s2)
            t1b = pr.sb("t1b_E", [P, 512], BF16, es=s2)
            pr.emit("vector", lambda e: e.memset(onesf_t[:, :], 1.0), sem="dve")
            w2 = dict(d_bwT=dd["bwT"], d_cwT=dd["cwT"], d_dwT=dd["dwT"],
                      d_outwT=dd["outwT"], bb_c=bb_c, cb_c=cb_c, db_c=db_c,
                      outb_c=outb_c, a_dec=a_dec)
            sc2 = dict(wstg=wstg, wb=wb, wc=wc, wd=wd, wo=wo, bq=bq, cq=cq,
                       dq=dq, hsq=hsq, a_rep=a_rep, onesf_t=onesf_t,
                       carry=carry, y_bf=y_bf, oacc=znb, t1b=t1b,
                       pb_=psum[0], pc_=psum[1], pd_=psum[2],
                       po_=[psum[3], psum[4]])
            emit_l1_attn_e2(pr, u_bf, xsb, w2, sc2)
            endphase("E_done")
            s2.close()

        pr.scope = "phA"; pr.ldsem = "ld_A"; phase_l0_ffn("A")
        pr.scope = "phB"; pr.ldsem = "ld_B"; phase_l0_attn()
        pr.scope = "phC"; pr.ldsem = "ld_C"; phase_l0_ffn("C")
        pr.scope = "phD"; pr.ldsem = "ld_D"; phase_moe("Dx")
        pr.scope = "phE"; pr.ldsem = "ld_E"; phase_l1_attn()
        pr.scope = "phF"; pr.ldsem = "ld_F"; phase_moe("Fx")
        pr.scope = None

        pr.wait("sync", "dve")
        for b in range(DB):
            pr.emit("sync", lambda e, b=b: e.dma_start(
                out=x_out[b * P:(b + 1) * P, :],
                in_=xsb[:, b * T:(b + 1) * T]), sem="st", amt=16)
        pr.wait("sync", "st")
        pr.finalize()
    return nc


def prep_fused_inputs(inputs):
    m = {}
    m.update(prep_l0_ffn_inputs(inputs))
    m.update(prep_l0_attn_inputs(inputs))
    m.update(prep_moe_inputs(inputs))
    m.update(prep_l1_attn_inputs(inputs))
    return m


def kernel(**inputs):
    x = inputs["L0_attn_x"]
    subs = globals().get("_SUBLAYERS", "FUSED")
    xt = [np.ascontiguousarray(x[c].T.astype(np.float32))
          for c in range(N_CORES)]
    if subs == "FUSED":
        nc = build_fused_program()
        base = prep_fused_inputs(inputs)
        in_maps = []
        for c in range(N_CORES):
            m = dict(base)
            m["x_in"] = xt[c]
            in_maps.append(m)
        res = run_bass_kernel_spmd(nc, in_maps, core_ids=list(range(N_CORES)))
        out = np.stack([res.results[c]["x_out"].T for c in range(N_CORES)])
        return out.astype(np.float32)
    for s in subs:
        if s in "AC":
            nc = build_l0_ffn_program()
            base = prep_l0_ffn_inputs(inputs)
        elif s == "B":
            nc = build_l0_attn_program()
            base = prep_l0_attn_inputs(inputs)
        elif s == "E":
            nc = build_l1_attn_program()
            base = prep_l1_attn_inputs(inputs)
        elif s in "DF":
            nc = build_moe_program()
            base = prep_moe_inputs(inputs)
        else:
            raise ValueError(s)
        in_maps = []
        for c in range(N_CORES):
            m = dict(base)
            m["x_in"] = xt[c]
            in_maps.append(m)
        res = run_bass_kernel_spmd(nc, in_maps, core_ids=list(range(N_CORES)))
        xt = [np.ascontiguousarray(res.results[c]["x_out"])
              for c in range(N_CORES)]
    out = np.stack([xt[c].T for c in range(N_CORES)])
    return out.astype(np.float32)


# version 17
# speedup vs baseline: 1.0330x; 1.0330x over previous
import numpy as np
from contextlib import ExitStack

import concourse.bass as bass
import concourse.mybir as mybir
from concourse.bass_utils import run_bass_kernel_spmd

F32 = mybir.dt.float32
BF16 = mybir.dt.bfloat16
AF = mybir.ActivationFunctionType
ALU = mybir.AluOpType
AX = mybir.AxisListType

P = 128
T = 2048
D = 1024
DB = D // P
NT = T // 512
N_CORES = 8
EPS_LN = 1e-5
EPS_RMS = 1e-6
NEXP = 14
INNER = 2048
IBL = INNER // P
DSTATE = 16
DTRANK = 64

ENGINES = ("tensor", "scalar", "vector", "pool", "sync")


class Prog:

    def __init__(self, nc, es):
        self.nc = nc
        self.es = es
        self.ops = {e: [] for e in ENGINES}
        self._sems = {}
        self.counts = {}
        self.scope = None
        self.ldsem = "ld"

    def sem(self, name):
        if name not in self._sems:
            self._sems[name] = self.es.enter_context(self.nc.semaphore(name))
            self.counts[name] = 0
        return self._sems[name]

    def emit(self, eng, f, sem=None, amt=1):
        if sem is not None:
            h = self.sem(sem)
            self.counts[sem] += amt

            def g(e, f=f, h=h, amt=amt):
                ins = f(e)
                ins.then_inc(h, amt)
                return ins
            self.ops[eng].append((self.scope, g))
            return self.counts[sem]
        self.ops[eng].append((self.scope, f))
        return None

    def tag(self, eng, sem, amt=1):
        h = self.sem(sem)
        self.counts[sem] += amt
        assert self.ops[eng], f"tag on empty stream {eng}"
        lbl, f = self.ops[eng].pop()

        def g(e, f=f, h=h, amt=amt):
            ins = f(e)
            assert ins is not None, "tagged op returned no instruction"
            ins.then_inc(h, amt)
        self.ops[eng].append((lbl, g))
        return self.counts[sem]

    def wait(self, eng, sem, val=None):
        h = self.sem(sem)
        v = self.counts[sem] if val is None else val

        def g(e, h=h, v=v):
            e.wait_ge(h, v)
        self.ops[eng].append((self.scope, g))

    def barrier(self, tag, engines=ENGINES):
        name = f"bar_{tag}"
        for eng in engines:
            self.emit(eng, lambda e: e.nop(), sem=name)
        tgt = self.counts[name]
        for eng in engines:
            self.wait(eng, name, tgt)

    def sb(self, name, shape, dtype=F32, es=None):
        return (es or self.es).enter_context(
            self.nc.sbuf_tensor(name, shape, dtype))

    def ps(self, name, shape=(P, 512)):
        return self.es.enter_context(
            self.nc.psum_tensor(name, list(shape), F32))

    def _replay(self, eng_name, e):
        nc = self.nc
        cur = None
        for lbl, f in self.ops[eng_name]:
            if lbl != (cur[0] if cur else None):
                if cur is not None:
                    nc.leave_named_scope(cur[0], cur[1], False)
                    cur = None
                if lbl is not None:
                    sid, _ = nc.enter_named_scope(lbl, False)
                    cur = (lbl, sid)
            f(e)
        if cur is not None:
            nc.leave_named_scope(cur[0], cur[1], False)

    def finalize(self):
        with self.nc.Block() as block:
            @block.tensor
            def _(e):
                self._replay("tensor", e)

            @block.scalar
            def _(e):
                self._replay("scalar", e)

            @block.vector
            def _(e):
                self._replay("vector", e)

            @block.gpsimd
            def _(e):
                self._replay("pool", e)

            @block.sync
            def _(e):
                self._replay("sync", e)



def _cols(v):
    return np.ascontiguousarray(v.reshape(-1, P).T.astype(np.float32))


def common_consts():
    c = {}
    c["ones1"] = np.ones((1, P), dtype=np.float32)
    c["ones_c"] = np.ones((P, 1), dtype=np.float32)
    c["ident"] = np.eye(P, dtype=np.float32)
    oh = np.zeros((NEXP, NEXP * P), dtype=np.float32)
    for e in range(NEXP):
        oh[e, e * P:(e + 1) * P] = 1.0
    c["onehot14"] = oh
    return c



def emit_ln(pr, xsb, out_bf, out_f32, sc, rms=False, gamma_col=None,
            eps=EPS_LN):
    rep, p0, p1 = sc["rep"], sc["p0"], sc["p1"]
    rowsB = sc["rowsB"]
    xb, sq = sc["xb"], sc["sq"]
    m_r = rowsB[0:1, 0:T]
    v_r = rowsB[0:1, T:2 * T]
    rstd_r = rowsB[0:1, 2 * T:3 * T]
    mrstd_r = rowsB[0:1, 3 * T:4 * T]
    msq_r = mrstd_r

    if xb is not None:
        for b in range(DB):
            pr.emit("vector", lambda e, b=b: e.tensor_copy(
                xb[:, b * T:(b + 1) * T], xsb[:, b * T:(b + 1) * T]))
        pr.tag("vector", "dve")
        pr.wait("scalar", "dve")
        pr.wait("tensor", "dve")
    stat_src = xb if xb is not None else xsb
    stat_ones = sc["ones_cb"] if xb is not None else sc["ones_cf"]

    slot_free_at = [pr.counts.get("pe", 0)] * 2
    for n in range(NT):
        for b in range(DB):
            pr.emit("tensor", lambda e, b=b, n=n: e.matmul(
                p0[0:1, :], stat_ones[:, 0:1],
                stat_src[:, b * T + n * 512: b * T + (n + 1) * 512],
                start=(b == 0), stop=(b == DB - 1)),
                sem="pe" if b == DB - 1 else None)
        for b in range(DB):
            s = b % 2
            pr.wait("scalar", "pe", slot_free_at[s])
            pr.emit("scalar", lambda e, b=b, n=n, s=s: e.activation(
                sq[:, s * 512:(s + 1) * 512],
                stat_src[:, b * T + n * 512: b * T + (n + 1) * 512],
                AF.Square), sem="act")
            pr.wait("tensor", "act")
            slot_free_at[s] = pr.emit("tensor", lambda e, s=s, b=b: e.matmul(
                p1[0:1, :], stat_ones[:, 0:1], sq[:, s * 512:(s + 1) * 512],
                start=(b == 0), stop=(b == DB - 1)), sem="pe")
        pr.wait("vector", "pe")
        pr.emit("vector", lambda e, n=n: e.tensor_copy(
            m_r[:, n * 512:(n + 1) * 512], p0[0:1, :]), sem="dve")
        pr.emit("vector", lambda e, n=n: e.tensor_copy(
            v_r[:, n * 512:(n + 1) * 512], p1[0:1, :]), sem="dve")
        pr.wait("tensor", "dve")

    def _vs():
        pr.tag("vector", "dve")
        pr.wait("vector", "dve")

    pr.wait("vector", "dve")
    if not rms:
        pr.emit("vector", lambda e: e.tensor_scalar(
            m_r, m_r, 1.0 / D, None, op0=ALU.mult))
        _vs()
        pr.emit("vector", lambda e: e.tensor_mul(msq_r, m_r, m_r))
        _vs()
        pr.emit("vector", lambda e: e.tensor_scalar(
            v_r, v_r, 1.0 / D, eps, op0=ALU.mult, op1=ALU.add))
        _vs()
        pr.emit("vector", lambda e: e.tensor_sub(
            v_r, v_r, msq_r), sem="dve")
    else:
        pr.emit("vector", lambda e: e.tensor_scalar(
            v_r, v_r, 1.0 / D, eps, op0=ALU.mult, op1=ALU.add), sem="dve")
    pr.wait("scalar", "dve")
    pr.emit("scalar", lambda e: e.activation(rstd_r, v_r, AF.Ln))
    pr.tag("scalar", "act")
    pr.wait("scalar", "act")
    pr.emit("scalar", lambda e: e.activation(
        rstd_r, rstd_r, AF.Exp, scale=-0.5), sem="act")
    if not rms:
        pr.wait("vector", "act")
        pr.emit("vector", lambda e: e.tensor_mul(
            mrstd_r, m_r, rstd_r), sem="dve")

    pr.wait("tensor", "act")
    if not rms:
        pr.wait("tensor", "dve")
    for n in range(NT):
        pr.emit("tensor", lambda e, n=n: e.matmul(
            p0[:, :], sc["ones_f"][0:1, :], rstd_r[:, n * 512:(n + 1) * 512],
            start=True, stop=True), sem="pe" if rms else None)
        if not rms:
            pr.emit("tensor", lambda e, n=n: e.matmul(
                p1[:, :], sc["ones_f"][0:1, :],
                mrstd_r[:, n * 512:(n + 1) * 512],
                start=True, stop=True), sem="pe")
        pr.wait("vector", "pe")
        pr.emit("vector", lambda e, n=n: e.tensor_copy(
            rep[:, n * 512:(n + 1) * 512], p0[:, :]), sem="dve")
        if not rms:
            pr.emit("vector", lambda e, n=n: e.tensor_copy(
                rep[:, T + n * 512:T + (n + 1) * 512], p1[:, :]), sem="dve")
        pr.wait("tensor", "dve")

    pr.wait("vector", "dve")
    rstd_rep = rep[:, 0:T]
    mrstd_rep = rep[:, T:2 * T]
    for b in range(DB):
        src = xsb[:, b * T:(b + 1) * T]
        if out_f32 is not None:
            dstf = out_f32[:, b * T:(b + 1) * T]
            if not rms:
                pr.emit("vector", lambda e, src=src, dstf=dstf: e.tensor_sub(
                    dstf, src, mrstd_rep))
                _vs()
                pr.emit("vector", lambda e, dstf=dstf: e.tensor_mul(
                    dstf, dstf, rstd_rep))
            else:
                pr.emit("vector", lambda e, src=src, dstf=dstf: e.tensor_mul(
                    dstf, src, rstd_rep))
            if out_bf is not None:
                dstb = out_bf[:, b * T:(b + 1) * T]
                if gamma_col is not None:
                    pr.emit("vector", lambda e, dstf=dstf, dstb=dstb, b=b:
                            e.tensor_scalar(dstb, dstf, gamma_col[:, b:b + 1],
                                            None, op0=ALU.mult))
                else:
                    pr.emit("vector", lambda e, dstf=dstf, dstb=dstb:
                            e.tensor_copy(dstb, dstf))
        else:
            dstb = out_bf[:, b * T:(b + 1) * T]
            if not rms:
                pr.emit("vector", lambda e, src=src, dstb=dstb: e.tensor_sub(
                    dstb, src, mrstd_rep))
                _vs()
                pr.emit("vector", lambda e, dstb=dstb: e.tensor_mul(
                    dstb, dstb, rstd_rep))
            else:
                pr.emit("vector", lambda e, src=src, dstb=dstb: e.tensor_mul(
                    dstb, src, rstd_rep))
            if gamma_col is not None:
                _vs()
                pr.emit("vector", lambda e, dstb=dstb, b=b: e.tensor_scalar(
                    dstb, dstb, gamma_col[:, b:b + 1], None, op0=ALU.mult))
    pr.tag("vector", "dve")



def emit_l0_ffn(pr, xnb, xsb, w, sc):
    s64, g64, relu, gb = sc["s64"], sc["g64"], sc["relu"], sc["gb"]
    tmp = sc["tmp"]
    p0, p1 = sc["p0"], sc["p1"]

    pr.wait("tensor", "dve")
    for n in range(NT):
        for b in range(DB):
            pr.emit("tensor", lambda e, b=b, n=n: e.matmul(
                p0[0:64, :], w["WinT"][:, b * 64:(b + 1) * 64],
                xnb[:, b * T + n * 512: b * T + (n + 1) * 512],
                start=(b == 0), stop=(b == DB - 1)),
                sem="pe" if b == DB - 1 else None)
        pr.wait("scalar", "pe")
        pr.emit("scalar", lambda e, n=n: e.activation(
            s64[0:64, n * 512:(n + 1) * 512], p0[0:64, :], AF.Identity,
            bias=w["binc"][0:64, 0:1]), sem="act")
        pr.wait("tensor", "act")

    dup1 = pr.ldsem + "_dup1"
    pr.wait("sync", "act")
    pr.emit("sync", lambda e: e.dma_start(out=s64[64:128, :], in_=s64[0:64, :]),
            sem=dup1, amt=16)
    pr.wait("scalar", dup1)
    pr.wait("vector", dup1)

    for hp in range(8):
        pr.emit("scalar", lambda e, hp=hp: e.activation(
            relu[:, :], s64[:, :], AF.Relu,
            scale=w["W1c"][:, hp:hp + 1], bias=w["b1c"][:, hp:hp + 1]),
            sem="act")
        pr.wait("vector", "act")
        if hp == 0:
            pr.emit("vector", lambda e: e.tensor_scalar(
                g64[:, :], relu[:, :], w["W2c"][:, 0:1], None, op0=ALU.mult),
                sem="dve")
        else:
            pr.emit("vector", lambda e, hp=hp: e.scalar_tensor_tensor(
                g64[:, :], relu[:, :], w["W2c"][:, hp:hp + 1], g64[:, :],
                op0=ALU.mult, op1=ALU.add), sem="dve")
        pr.wait("vector", "dve")
        pr.wait("scalar", "dve")
    dup2 = pr.ldsem + "_dup2"
    pr.wait("sync", "dve")
    pr.emit("sync", lambda e: e.dma_start(out=relu[0:64, :], in_=g64[64:128, :]),
            sem=dup2, amt=16)
    pr.wait("vector", dup2)
    pr.emit("vector", lambda e: e.tensor_add(
        g64[0:64, :], g64[0:64, :], relu[0:64, :]))
    pr.tag("vector", "dve")
    pr.wait("vector", "dve")
    pr.emit("vector", lambda e: e.tensor_scalar(
        gb[0:64, :], g64[0:64, :], w["b2c"][0:64, 0:1], None, op0=ALU.add),
        sem="dve")

    pr.wait("tensor", "dve")
    for b in range(DB):
        for n in range(NT):
            pr.emit("tensor", lambda e, b=b, n=n: e.matmul(
                p1[:, :], w["WoutT"][0:64, b * P:(b + 1) * P],
                gb[0:64, n * 512:(n + 1) * 512], start=True, stop=True),
                sem="pe")
            pr.wait("scalar", "pe")
            pr.emit("scalar", lambda e, b=b: e.activation(
                tmp[:, 0:512], p1[:, :], AF.Identity, scale=0.5,
                bias=w["bout_half"][:, b:b + 1]), sem="act")
            pr.wait("vector", "act")
            dst = xsb[:, b * T + n * 512: b * T + (n + 1) * 512]
            pr.emit("vector", lambda e, dst=dst: e.tensor_add(
                dst, dst, tmp[:, 0:512]), sem="dve")
            pr.wait("tensor", "dve")
            pr.wait("scalar", "dve")



def emit_router(pr, xsb, sc, r):
    rowsB = sc["rowsB"]
    m_r = rowsB[0:1, 0:T]
    rstd_r = rowsB[0:1, 2 * T:3 * T]
    p2, p3 = r["p2"], r["p3"]
    lgs, lgT = r["lgs"], r["lgT"]
    red, mt, cums = r["red"], r["mt"], r["cums"]

    pr.wait("tensor", "dve")
    for b in range(DB):
        pr.emit("tensor", lambda e, b=b: e.matmul(
            p2[0:NEXP, 0:1], r["rwT"][:, b * NEXP:(b + 1) * NEXP],
            sc["ones_cf"][:, 0:1], start=(b == 0), stop=(b == DB - 1)),
            sem="pe" if b == DB - 1 else None)
    pr.wait("vector", "pe")
    pr.emit("vector", lambda e: e.tensor_scalar(
        r["colsum_n"][:, 0:1], p2[0:NEXP, 0:1], -1.0, None, op0=ALU.mult),
        sem="dve")
    pr.wait("tensor", "dve")

    for n in range(NT):
        sl = slice(n * 512, (n + 1) * 512)
        for b in range(DB):
            pr.emit("tensor", lambda e, b=b, n=n: e.matmul(
                p2[0:NEXP, :], r["rwT"][:, b * NEXP:(b + 1) * NEXP],
                xsb[:, b * T + n * 512: b * T + (n + 1) * 512],
                start=(b == 0), stop=(b == DB - 1)))
        pr.emit("tensor", lambda e, n=n: e.matmul(
            p3[0:NEXP, :], r["ones_f"][0:1, 0:NEXP],
            m_r[:, n * 512:(n + 1) * 512], start=True, stop=True), sem="pe")
        pr.wait("vector", "pe")
        pr.emit("vector", lambda e, sl=sl: e.tensor_copy(
            lgs[0:NEXP, sl], p2[0:NEXP, :]))
        pr.tag("vector", "dve")
        pr.wait("vector", "dve")
        pr.emit("vector", lambda e, sl=sl: e.scalar_tensor_tensor(
            lgs[0:NEXP, sl], p3[0:NEXP, :], r["colsum_n"][0:NEXP, 0:1],
            lgs[0:NEXP, sl], op0=ALU.mult, op1=ALU.add), sem="dve")
        pr.wait("tensor", "dve")
        pr.emit("tensor", lambda e, n=n: e.matmul(
            p3[0:NEXP, :], r["ones_f"][0:1, 0:NEXP],
            rstd_r[:, n * 512:(n + 1) * 512], start=True, stop=True), sem="pe")
        pr.wait("vector", "pe")
        pr.wait("vector", "dve")
        pr.emit("vector", lambda e, sl=sl: e.tensor_tensor(
            lgs[0:NEXP, sl], lgs[0:NEXP, sl], p3[0:NEXP, :], op=ALU.mult),
            sem="dve")
        pr.wait("tensor", "dve")

    for i in range(16):
        pr.emit("tensor", lambda e, i=i: e.transpose(
            p2[:, 0:NEXP], lgs[0:NEXP, i * P:(i + 1) * P],
            r["identf"][0:NEXP, 0:NEXP]), sem="pe")
        pr.wait("vector", "pe")
        pr.emit("vector", lambda e, i=i: e.tensor_copy(
            lgT[:, i * NEXP:(i + 1) * NEXP], p2[:, 0:NEXP]), sem="dve")
        pr.wait("tensor", "dve")

    def vsync():
        pr.tag("vector", "dve")
        pr.wait("vector", "dve")

    for i in range(16):
        tile = lgT[:, i * NEXP:(i + 1) * NEXP]
        pr.emit("vector", lambda e, tile=tile: e.reduce_max(
            red[:, 0:1], tile, axis=AX.X))
        vsync()
        pr.emit("vector", lambda e, tile=tile: e.tensor_scalar(
            mt[:, :], tile, red[:, 0:1], None, op0=ALU.is_ge))
        vsync()
        pr.emit("vector", lambda e: e.tensor_tensor_scan(
            cums[:, :], r["ones14"][:, :], mt[:, :], 0.0,
            op0=ALU.mult, op1=ALU.add))
        vsync()
        pr.emit("vector", lambda e: e.tensor_scalar(
            cums[:, :], cums[:, :], 1.0, None, op0=ALU.is_le))
        vsync()
        pr.emit("vector", lambda e: e.tensor_tensor(
            mt[:, :], mt[:, :], cums[:, :], op=ALU.mult))
        vsync()
        pr.emit("vector", lambda e, tile=tile: e.tensor_scalar(
            cums[:, :], tile, red[:, 0:1], None, op0=ALU.subtract),
            sem="dve")
        pr.wait("scalar", "dve")
        pr.emit("scalar", lambda e: e.activation(
            cums[:, :], cums[:, :], AF.Exp), sem="act")
        pr.wait("vector", "act")
        pr.emit("vector", lambda e: e.reduce_sum(
            red[:, 1:2], cums[:, :], axis=AX.X))
        vsync()
        pr.emit("vector", lambda e: e.reciprocal(red[:, 2:3], red[:, 1:2]))
        vsync()
        pr.emit("vector", lambda e: e.tensor_scalar(
            red[:, 2:3], red[:, 2:3], 0.5, None, op0=ALU.mult))
        vsync()
        pr.emit("vector", lambda e: e.tensor_scalar(
            mt[:, :], mt[:, :], red[:, 2:3], None, op0=ALU.mult), sem="dve")
        pr.wait("tensor", "dve")
        pr.emit("tensor", lambda e: e.transpose(
            p2[0:NEXP, 0:P], mt[:, :], r["identf"][:, :]), sem="pe")
        pr.wait("vector", "pe")
        pr.emit("vector", lambda e, i=i: e.tensor_copy(
            r["msel"][0:NEXP, i * P:(i + 1) * P], p2[0:NEXP, 0:P]), sem="dve")
        pr.wait("tensor", "dve")
    pr.emit("vector", lambda e: e.tensor_copy(
        r["msel_b"][0:NEXP, :], r["msel"][0:NEXP, :]), sem="dve")


def emit_moe_experts(pr, xnb, xsb, m, dram, cache=None):
    w1s, w2s = m["w1s"], m["w2s"]
    w1f = m["w1f"]
    hid, t1 = m["hid"], m["t1"]
    ph = [m["ph0"], m["ph1"], m["p0"], m["p1"]]
    py = [m["py0"], m["py1"], m["p2b"]]

    last_w1_mm = 0
    last_w2_mm = 0
    cast_done = [0, 0]
    msel_slot_free = [0, 0]

    if cache == "in":
        pr.wait("sync", "cwb")
    pr.emit("sync", lambda e: e.dma_start(
        out=m["b1c"][:, :].rearrange("p (x k) -> p x k", k=16),
        in_=dram["d_b1c"].rearrange("x p k -> p x k")), sem="wld", amt=16)
    pr.emit("sync", lambda e: e.dma_start(
        out=m["b2c"][:, :].rearrange("p (x k) -> p x k", k=8),
        in_=dram["d_b2c"].rearrange("x p k -> p x k")), sem="wld", amt=16)
    pr.wait("scalar", "wld")
    pr.wait("vector", "wld")
    pr.wait("tensor", "wld")

    for ex in range(NEXP):
        chunks = ([("w1", kc, h) for kc in range(8) for h in range(2)] +
                  [("w2", kc, 0) for kc in range(16)])
        if cache == "in":
            pr.wait("sync", "pe", last_w1_mm)
            for kc in range(8):
                for h in range(2):
                    pr.emit("sync", lambda e, ex=ex, kc=kc, h=h: e.dma_start(
                        out=w1s[:, kc * T + h * 1024: kc * T + (h + 1) * 1024],
                        in_=dram["d_w1b"][ex, kc * P:(kc + 1) * P,
                                          h * 1024:(h + 1) * 1024]),
                        sem="fwld", amt=16)
            pr.wait("sync", "pe", last_w2_mm)
            for kc in range(16):
                pr.emit("sync", lambda e, ex=ex, kc=kc: e.dma_start(
                    out=w2s[:, kc * 1024:(kc + 1) * 1024],
                    in_=dram["d_w2b"][ex, kc * P:(kc + 1) * P, :]),
                    sem="fwld", amt=16)
            chunks = []
        for i, (which, kc, h) in enumerate(chunks):
            s = i % 2
            wsem = f"wld{s}"
            csem = f"wcast{s}"
            pr.wait("sync", csem, cast_done[s])
            if which == "w1":
                pr.emit("sync", lambda e, ex=ex, kc=kc, h=h, s=s: e.dma_start(
                    out=w1f[:, s * 1024:(s + 1) * 1024],
                    in_=dram["d_w1"][ex, kc * P:(kc + 1) * P,
                                     h * 1024:(h + 1) * 1024]),
                    sem=wsem, amt=16)
                dst = w1s[:, kc * T + h * 1024: kc * T + (h + 1) * 1024]
                gate = last_w1_mm
            else:
                pr.emit("sync", lambda e, ex=ex, kc=kc, s=s: e.dma_start(
                    out=w1f[:, s * 1024:(s + 1) * 1024],
                    in_=dram["d_w2"][ex, kc * P:(kc + 1) * P, :]),
                    sem=wsem, amt=16)
                dst = w2s[:, kc * 1024:(kc + 1) * 1024]
                gate = last_w2_mm
            ceng = "scalar" if s == 0 else "vector"
            pr.wait(ceng, wsem)
            pr.wait(ceng, "pe", gate)
            if ceng == "scalar":
                cast_done[s] = pr.emit(
                    ceng, lambda e, dst=dst, s=s: e.activation(
                        dst, w1f[:, s * 1024:(s + 1) * 1024], AF.Copy),
                    sem=csem)
            else:
                cast_done[s] = pr.emit(
                    ceng, lambda e, dst=dst, s=s: e.tensor_copy(
                        dst, w1f[:, s * 1024:(s + 1) * 1024]), sem=csem)
            if cache == "out":
                if which == "w1":
                    cdst = dram["d_w1b"][ex, kc * P:(kc + 1) * P,
                                         h * 1024:(h + 1) * 1024]
                else:
                    cdst = dram["d_w2b"][ex, kc * P:(kc + 1) * P, :]
                pr.wait("pool", csem)
                pr.emit("pool", lambda e, cdst=cdst, dst=dst: e.dma_start(
                    out=cdst, in_=dst), sem="cwb", amt=16)
        b1col = m["b1c"][:, ex * 16:(ex + 1) * 16]
        b2col = m["b2c"][:, ex * 8:(ex + 1) * 8]
        if cache == "in":
            pr.wait("tensor", "fwld")
        else:
            pr.wait("tensor", "wcast0")
            pr.wait("tensor", "wcast1")
        mslot = ex % 2
        mv = m["mselr"][:, mslot * T:(mslot + 1) * T]
        pr.wait("tensor", "yacc", msel_slot_free[mslot])
        for n4 in range(NT):
            pr.emit("tensor", lambda e, ex=ex, n4=n4: e.matmul(
                m["p3b"][:, :], m["ohb"][0:NEXP, ex * P:(ex + 1) * P],
                m["msel_b"][0:NEXP, n4 * 512:(n4 + 1) * 512],
                start=True, stop=True), sem="pe")
            pr.wait("scalar", "pe")
            pr.emit("scalar", lambda e, n4=n4, mv=mv: e.activation(
                mv[:, n4 * 512:(n4 + 1) * 512], m["p3b"][:, :],
                AF.Identity), sem="mselev")
            pr.wait("tensor", "mselev")

        w1_mm_mark = 0
        for tq in range(NT):
            for mm in range(16):
                s = mm % 4
                pr.wait("tensor", "hevac",
                        max(0, pr.counts.get("hevac", 0) - 3))
                for kc in range(8):
                    pr.emit("tensor",
                            lambda e, mm=mm, kc=kc, tq=tq, s=s: e.matmul(
                                ph[s][:, :],
                                w1s[:, kc * T + mm * P: kc * T + (mm + 1) * P],
                                xnb[:, kc * T + tq * 512:
                                    kc * T + (tq + 1) * 512],
                                start=(kc == 0), stop=(kc == 7)),
                            sem="pe" if kc == 7 else None)
                pr.wait("scalar", "pe")
                pr.emit("scalar", lambda e, mm=mm, s=s, b1col=b1col:
                        e.activation(
                    hid[:, mm * 512:(mm + 1) * 512], ph[s][:, :], AF.Gelu,
                    bias=b1col[:, mm:mm + 1]), sem="hevac")
            if tq == NT - 1:
                w1_mm_mark = pr.counts.get("pe", 0)
            pr.wait("tensor", "hevac")
            for mo in range(8):
                s = mo % 3
                pr.wait("tensor", "moev", max(0, pr.counts.get("moev", 0) - 2))
                for kc in range(16):
                    pr.emit("tensor", lambda e, mo=mo, kc=kc, s=s: e.matmul(
                        py[s][:, :],
                        w2s[:, kc * 1024 + mo * P: kc * 1024 + (mo + 1) * P],
                        hid[:, kc * 512:(kc + 1) * 512],
                        start=(kc == 0), stop=(kc == 15)),
                        sem="pe" if kc == 15 else None)
                pr.wait("scalar", "pe")
                pr.wait("scalar", "yacc",
                        max(0, pr.counts.get("yacc", 0) - 2))
                pr.emit("scalar", lambda e, mo=mo, s=s, b2col=b2col:
                        e.activation(
                    t1[:, s * 512:(s + 1) * 512], py[s][:, :], AF.Identity,
                    bias=b2col[:, mo:mo + 1]), sem="moev")
                pr.wait("vector", "moev")
                dst = xsb[:, mo * T + tq * 512: mo * T + (tq + 1) * 512]
                pr.emit("vector", lambda e, tq=tq, s=s, mv=mv: e.tensor_mul(
                    t1[:, s * 512:(s + 1) * 512], t1[:, s * 512:(s + 1) * 512],
                    mv[:, tq * 512:(tq + 1) * 512]))
                pr.tag("vector", "t1ch")
                pr.wait("vector", "t1ch")
                pr.emit("vector", lambda e, dst=dst, s=s: e.tensor_add(
                    dst, dst, t1[:, s * 512:(s + 1) * 512]), sem="yacc")
        last_w1_mm = w1_mm_mark
        last_w2_mm = pr.counts.get("pe", 0)
        msel_slot_free[ex % 2] = pr.counts.get("yacc", 0)



def emit_l0_attn(pr, xsb, xn, w, sc, chunks=NT, chunk_cb=None):
    p0, p1, p2, p3 = sc["p0"], sc["p1"], sc["p2"], sc["p3"]
    pB, pC = sc["pB"], sc["pC"]
    xnb_c, dtF = sc["xnb_c"], sc["dtF"]
    dtlo, Bsb, Csb = sc["dtlo"], sc["Bsb"], sc["Csb"]
    h_state = sc["h_state"]
    sp1, sp2 = sc["sp1"], sc["sp2"]
    a_t, u2_t, hs_t = sc["a_t"], sc["u2_t"], sc["hs_t"]

    def vsync():
        pr.tag("vector", "dve")
        pr.wait("vector", "dve")

    def ssync():
        pr.tag("scalar", "act")
        pr.wait("scalar", "act")

    pr.emit("vector", lambda e: e.memset(h_state[:, :], 0.0), sem="dve")
    sc["last_scan_dve"] = [0]

    import os as _os0
    _skip_all = int(_os0.environ.get("L0A_SKIP_ALL_FROM", "99"))
    for c in range(chunks):
        t0 = c * 512
        if int(_os0.environ.get("L0A_BARRIER", "1")):
            pr.barrier(f"l0a_chunk{c}_{pr.counts.get('dve', 0)}")
        if c >= _skip_all:
            continue
        pr.wait("vector", "dve")
        for b in range(DB):
            pr.emit("vector", lambda e, b=b, t0=t0: e.tensor_copy(
                xnb_c[:, b * 512:(b + 1) * 512],
                xn[:, b * T + t0: b * T + t0 + 512]))
        pr.tag("vector", "dve")

        pr.wait("tensor", "dve")
        for b in range(DB):
            pr.emit("tensor", lambda e, b=b: e.matmul(
                p0[0:64, :], w["WxdtT"][:, b * 64:(b + 1) * 64],
                xnb_c[:, b * 512:(b + 1) * 512],
                start=(b == 0), stop=(b == DB - 1)))
        for b in range(DB):
            pr.emit("tensor", lambda e, b=b: e.matmul(
                p1[0:16, :], w["WxBT"][:, b * 16:(b + 1) * 16],
                xnb_c[:, b * 512:(b + 1) * 512],
                start=(b == 0), stop=(b == DB - 1)))
        for b in range(DB):
            pr.emit("tensor", lambda e, b=b: e.matmul(
                p2[0:16, :], w["WxCT"][:, b * 16:(b + 1) * 16],
                xnb_c[:, b * 512:(b + 1) * 512],
                start=(b == 0), stop=(b == DB - 1)), sem="pe" if b == DB - 1 else None)
        pr.wait("scalar", "pe")
        pr.emit("scalar", lambda e: e.activation(dtlo[0:64, :], p0[0:64, :],
                                                 AF.Identity), sem="act")
        pr.wait("vector", "pe")
        pr.emit("vector", lambda e: e.tensor_copy(Bsb[0:16, :], p1[0:16, :]))
        pr.emit("vector", lambda e: e.tensor_copy(Csb[0:16, :], p2[0:16, :]),
                sem="dve")
        pr.wait("tensor", "dve")
        pr.wait("tensor", "act")

        for mc in range(DB):
            pr.emit("tensor", lambda e, mc=mc: e.matmul(
                p3[:, :], w["WdtT"][0:64, mc * P:(mc + 1) * P], dtlo[0:64, :],
                start=True, stop=True), sem="pe")
            pr.wait("scalar", "pe")
            dst = dtF[:, mc * 512:(mc + 1) * 512]
            pr.emit("scalar", lambda e, mc=mc: e.activation(
                sp1[:, :], p3[:, :], AF.Identity,
                bias=w["bdt_c"][:, mc:mc + 1]))
            ssync()
            pr.emit("scalar", lambda e, dst=dst: e.activation(
                dst, sp1[:, :], AF.Relu))
            ssync()
            pr.emit("scalar", lambda e: e.activation(sp2[:, :], sp1[:, :],
                                                     AF.Abs))
            ssync()
            pr.emit("scalar", lambda e: e.activation(sp2[:, :], sp2[:, :],
                                                     AF.Exp, scale=-1.0))
            ssync()
            pr.emit("scalar", lambda e: e.activation(sp2[:, :], sp2[:, :],
                                                     AF.Ln, bias=1.0),
                    sem="act")
            pr.wait("vector", "act")
            pr.emit("vector", lambda e, dst=dst: e.tensor_add(
                dst, dst, sp2[:, :]), sem="dve")
            pr.wait("tensor", "dve")
            pr.wait("scalar", "dve")

        import os as _os
        _skip_d = int(_os.environ.get("L0A_SKIP_DSKIP_FROM", "99"))
        pr.wait("vector", "dve")
        for b in range(DB if c < _skip_d else 0):
            xsl = xsb[:, b * T + t0: b * T + t0 + 512]
            nsl = xn[:, b * T + t0: b * T + t0 + 512]
            pr.emit("vector", lambda e, xsl=xsl, nsl=nsl, b=b:
                    e.scalar_tensor_tensor(xsl, nsl, w["D_c"][:, b:b + 1],
                                           xsl, op0=ALU.mult, op1=ALU.add))
        if c < _skip_d:
            vsync()
        for b in range(DB if c < _skip_d else 0):
            nsl = xn[:, b * T + t0: b * T + t0 + 512]
            pr.emit("vector", lambda e, nsl=nsl, b=b: e.tensor_mul(
                nsl, dtF[:, b * 512:(b + 1) * 512], nsl))
        if c < _skip_d:
            vsync()

        brow = sc["brow"]
        pr.wait("sync", "dve")
        import os
        _skip_n = int(os.environ.get("L0A_SKIP_NLOOP_FROM", "99"))
        _nbar = int(os.environ.get("L0A_NBARRIER", "0"))
        for n in range(DSTATE if c < _skip_n else 0):
            if _nbar:
                pr.barrier(f"l0a_nb_{c}_{n}_{pr.counts.get('dve', 0)}")
            s = n % 2
            bsem = f"browld{s}"
            pr.wait("sync", "brow_free", pr.counts.get("brow_free", 0) - 1
                    if pr.counts.get("brow_free", 0) >= 2 else 0)
            pr.emit("sync", lambda e, n=n, s=s: e.dma_start(
                out=brow[0:1, s * 1024: s * 1024 + 512], in_=Bsb[n:n + 1, :]),
                sem=bsem, amt=16)
            pr.emit("sync", lambda e, n=n, s=s: e.dma_start(
                out=brow[0:1, s * 1024 + 512: s * 1024 + 1024],
                in_=Csb[n:n + 1, :]), sem=bsem, amt=16)
            pr.wait("tensor", "dve")
            pr.wait("tensor", "bc_free", pr.counts.get("bc_free", 0))
            pr.wait("tensor", bsem)
            pr.emit("tensor", lambda e, s=s: e.matmul(
                pB[:, :], sc["ones_f"][0:1, :],
                brow[0:1, s * 1024: s * 1024 + 512],
                start=True, stop=True))
            pr.emit("tensor", lambda e, s=s: e.matmul(
                pC[:, :], sc["ones_f"][0:1, :],
                brow[0:1, s * 1024 + 512: s * 1024 + 1024],
                start=True, stop=True), sem="brow_free")
            pr.wait("vector", "brow_free")
            for b in range(DB):
                col = b * 16 + n
                usl = xn[:, b * T + t0: b * T + t0 + 512]
                xsl = xsb[:, b * T + t0: b * T + t0 + 512]
                pr.wait("scalar", "dve", sc["last_scan_dve"][0])
                pr.emit("scalar", lambda e, b=b, col=col: e.activation(
                    a_t[:, :], dtF[:, b * 512:(b + 1) * 512], AF.Exp,
                    scale=w["A_cols"][:, col:col + 1]), sem="act")
                pr.emit("vector", lambda e, usl=usl: e.tensor_tensor(
                    u2_t[:, :], usl, pB[:, :], op=ALU.mult))
                vsync()
                pr.wait("vector", "act")
                td = sc.get("tile_dump")
                if td is not None and td[:3] == (c, n, b):
                    bufs = td[3]
                    pr.emit("vector", lambda e: e.tensor_copy(
                        bufs["a"][:, :], a_t[:, :]))
                    pr.emit("vector", lambda e: e.tensor_copy(
                        bufs["u2"][:, :], u2_t[:, :]))
                    vsync()
                pr.emit("vector", lambda e, col=col: e.tensor_tensor_scan(
                    hs_t[:, :], a_t[:, :], u2_t[:, :],
                    h_state[:, col:col + 1], op0=ALU.mult, op1=ALU.add))
                vsync()
                sc["last_scan_dve"][0] = pr.counts["dve"]
                pr.emit("vector", lambda e, col=col: e.tensor_copy(
                    h_state[:, col:col + 1], hs_t[:, 511:512]))
                pr.emit("vector", lambda e: e.tensor_tensor(
                    u2_t[:, :], hs_t[:, :], pC[:, :], op=ALU.mult))
                vsync()
                pr.emit("vector", lambda e, xsl=xsl: e.tensor_add(
                    xsl, xsl, u2_t[:, :]), sem="bc_free" if b == DB - 1 else None)
                td = sc.get("tile_dump")
                if td is not None and td[:3] == (c, n, b):
                    bufs = td[3]
                    pr.emit("vector", lambda e: e.tensor_copy(
                        bufs["hs"][:, :], hs_t[:, :]))
                    pr.emit("vector", lambda e: e.tensor_copy(
                        bufs["pb"][:, :], pB[:, :]))
                    pr.emit("vector", lambda e: e.tensor_copy(
                        bufs["pc"][:, :], pC[:, :]), sem="dve")
        if chunk_cb is not None:
            chunk_cb(c)



def emit_l1_attn_e1(pr, znb, u_bf, w, sc):
    wstg = sc["wstg"]
    t2 = sc["t2"]
    pu, pg = sc["pu"], sc["pg"]
    wcin, wcg = sc["wcol_in"], sc["wcol_g"]
    pe_hist = []

    cast_done1 = [0, 0]
    for m in range(IBL):
        s = m % 2
        sl = slice(s * 8 * P, (s + 1) * 8 * P)
        slot_gate = pe_hist[m - 2] if m >= 2 else 0
        jobs = [(w["d_inwT"], wcin, 0), (w["d_gatewT"], wcg, 0),
                (w["d_inwT"], wcin, 1), (w["d_gatewT"], wcg, 1)]
        for j, (dsrc, dmat, half) in enumerate(jobs):
            q = j % 2
            wsem = f"wlde1_{q}"
            csem = f"wcste1_{q}"
            pr.wait("sync", csem, cast_done1[q])
            pr.emit("sync", lambda e, m=m, half=half, q=q, dsrc=dsrc:
                    e.dma_start(
                        out=wstg[:, q * 4 * P:(q + 1) * 4 * P],
                        in_=dsrc[half * 512:(half + 1) * 512,
                                 m * P:(m + 1) * P].rearrange(
                            "(c p) k -> p c k", p=P)), sem=wsem, amt=16)
            pr.wait("pool", wsem)
            pr.wait("pool", "pe", slot_gate)
            cast_done1[q] = pr.emit(
                "pool", lambda e, half=half, sl=sl, q=q, dmat=dmat:
                e.tensor_copy(
                    dmat[:, sl][:, half * 4 * P:(half + 1) * 4 * P],
                    wstg[:, q * 4 * P:(q + 1) * 4 * P]), sem=csem)
        pr.wait("tensor", "wcste1_0")
        pr.wait("tensor", "wcste1_1")
        for tq in range(NT):
            for kc in range(DB):
                pr.emit("tensor", lambda e, kc=kc, tq=tq, sl=sl: e.matmul(
                    pu[:, :], wcin[:, sl][:, kc * P:(kc + 1) * P],
                    znb[:, kc * T + tq * 512: kc * T + (tq + 1) * 512],
                    start=(kc == 0), stop=(kc == DB - 1)))
            for kc in range(DB):
                pr.emit("tensor", lambda e, kc=kc, tq=tq, sl=sl: e.matmul(
                    pg[:, :], wcg[:, sl][:, kc * P:(kc + 1) * P],
                    znb[:, kc * T + tq * 512: kc * T + (tq + 1) * 512],
                    start=(kc == 0), stop=(kc == DB - 1)),
                    sem="pe" if kc == DB - 1 else None)
            pr.wait("scalar", "pe")
            pr.emit("scalar", lambda e, m=m: e.activation(
                t2[:, :], pg[:, :], AF.Sigmoid,
                bias=w["gateb_c"][:, m:m + 1]), sem="act")
            pr.wait("vector", "act")
            dst = u_bf[:, m * T + tq * 512: m * T + (tq + 1) * 512]
            pr.emit("vector", lambda e, m=m: e.tensor_scalar(
                t2[:, :] if False else sc["t3"][:, :], pu[:, :],
                w["inb_c"][:, m:m + 1], None, op0=ALU.add))
            pr.tag("vector", "dve")
            pr.wait("vector", "dve")
            pr.emit("vector", lambda e, dst=dst: e.tensor_mul(
                dst, sc["t3"][:, :], t2[:, :]), sem="dve")
            pr.wait("tensor", "dve")
            pr.wait("scalar", "dve")
        pe_hist.append(pr.counts.get("pe", 0))


def emit_l1_attn_e2(pr, u_bf, xsb, w, sc):
    wstg = sc["wstg"]
    wb, wc, wd = sc["wb"], sc["wc"], sc["wd"]
    wo = sc["wo"]
    bq, cq, dq = sc["bq"], sc["cq"], sc["dq"]
    hsq = sc["hsq"]
    a_rep = sc["a_rep"]
    onesf_t = sc["onesf_t"]
    carry = sc["carry"]
    y_bf = sc["y_bf"]
    oacc = sc["oacc"]
    t1b = sc["t1b"]
    pb_, pc_, pd_, po_ = sc["pb_"], sc["pc_"], sc["pd_"], sc["po_"]

    def vsync():
        pr.tag("vector", "dve")
        pr.wait("vector", "dve")

    pr.emit("vector", lambda e: e.memset(oacc[:, :], 0.0))
    pr.emit("vector", lambda e: e.memset(carry[:, :], 0.0), sem="dve")
    pe_hist2 = []
    cast_done2 = [0, 0]

    for ib in range(IBL):
        slot_gate = pe_hist2[ib - 1] if ib >= 1 else 0
        jobs2 = ([(dn, dst, q, False) for dn, dst in
                  (("d_bwT", wb), ("d_cwT", wc), ("d_dwT", wd))
                  for q in range(4)] +
                 [("d_outwT", wo, q, True) for q in range(2)])
        for j, (dname, dst, q, is_out) in enumerate(jobs2):
            sl2 = j % 2
            wsem = f"wlde2_{sl2}"
            csem = f"wcste2_{sl2}"
            pr.wait("sync", csem, cast_done2[sl2])
            if is_out:
                pr.emit("sync", lambda e, ib=ib, q=q, sl2=sl2: e.dma_start(
                    out=wstg[:, sl2 * 4 * P:(sl2 + 1) * 4 * P],
                    in_=w["d_outwT"][ib * P:(ib + 1) * P,
                                     q * 512:(q + 1) * 512].rearrange(
                        "p (c k) -> p c k", k=P)), sem=wsem, amt=16)
                tgt = wo[:, q * 512:(q + 1) * 512]
            else:
                pr.emit("sync", lambda e, ib=ib, q=q, sl2=sl2, dname=dname:
                        e.dma_start(
                            out=wstg[:, sl2 * 4 * P:(sl2 + 1) * 4 * P],
                            in_=w[dname][q * 512:(q + 1) * 512,
                                         ib * P:(ib + 1) * P].rearrange(
                                "(c p) k -> p c k", p=P)), sem=wsem, amt=16)
                tgt = dst[:, q * 4 * P:(q + 1) * 4 * P]
            pr.wait("pool", wsem)
            pr.wait("pool", "pe", slot_gate)
            cast_done2[sl2] = pr.emit(
                "pool", lambda e, tgt=tgt, sl2=sl2: e.tensor_copy(
                    tgt, wstg[:, sl2 * 4 * P:(sl2 + 1) * 4 * P]), sem=csem)
        pr.wait("tensor", "wcste2_0")
        pr.wait("tensor", "wcste2_1")
        pr.wait("vector", "dve")
        pr.emit("vector", lambda e, ib=ib: e.tensor_scalar(
            a_rep[:, :], onesf_t[:, :], w["a_dec"][:, ib:ib + 1], None,
            op0=ALU.mult), sem="dve")

        for th in range(2):
            tsl = slice(th * 1024, (th + 1) * 1024)
            for nn in range(2):
                n0 = th * 1024 + nn * 512
                for kc in range(IBL):
                    pr.emit("tensor", lambda e, kc=kc, n0=n0, ib=ib: e.matmul(
                        pb_[:, :], wb[:, kc * P:(kc + 1) * P],
                        u_bf[:, kc * T + n0: kc * T + n0 + 512],
                        start=(kc == 0), stop=(kc == IBL - 1)))
                for kc in range(IBL):
                    pr.emit("tensor", lambda e, kc=kc, n0=n0, ib=ib: e.matmul(
                        pc_[:, :], wc[:, kc * P:(kc + 1) * P],
                        u_bf[:, kc * T + n0: kc * T + n0 + 512],
                        start=(kc == 0), stop=(kc == IBL - 1)))
                for kc in range(IBL):
                    pr.emit("tensor", lambda e, kc=kc, n0=n0, ib=ib: e.matmul(
                        pd_[:, :], wd[:, kc * P:(kc + 1) * P],
                        u_bf[:, kc * T + n0: kc * T + n0 + 512],
                        start=(kc == 0), stop=(kc == IBL - 1)),
                        sem="pe" if kc == IBL - 1 else None)
                pr.wait("vector", "pe")
                pr.wait("vector", "ybf", pr.counts.get("ybf", 0))
                nsl = slice(nn * 512, (nn + 1) * 512)
                pr.emit("vector", lambda e, nsl=nsl, ib=ib: e.tensor_scalar(
                    bq[:, nsl], pb_[:, :], w["bb_c"][:, ib:ib + 1], None,
                    op0=ALU.add))
                pr.emit("vector", lambda e, nsl=nsl, ib=ib: e.tensor_scalar(
                    cq[:, nsl], pc_[:, :], w["cb_c"][:, ib:ib + 1], None,
                    op0=ALU.add))
                pr.emit("vector", lambda e, nsl=nsl, ib=ib: e.tensor_scalar(
                    dq[:, nsl], pd_[:, :], w["db_c"][:, ib:ib + 1], None,
                    op0=ALU.add), sem="bcdev")
                pr.wait("tensor", "bcdev")
            pr.wait("vector", "bcdev")
            pr.emit("vector", lambda e, ib=ib: e.tensor_tensor_scan(
                hsq[:, :], a_rep[:, :], bq[:, :], carry[:, ib:ib + 1],
                op0=ALU.mult, op1=ALU.add))
            vsync()
            pr.emit("vector", lambda e, ib=ib: e.tensor_copy(
                carry[:, ib:ib + 1], hsq[:, 1023:1024]), sem="scn")
            pr.wait("pool", "scn")
            pr.emit("pool", lambda e: e.tensor_mul(cq[:, :], cq[:, :],
                                                   hsq[:, :]))
            pr.tag("pool", "ych")
            pr.wait("pool", "ych")
            pr.emit("pool", lambda e: e.tensor_add(cq[:, :], cq[:, :],
                                                   dq[:, :]))
            pr.tag("pool", "ych")
            pr.wait("pool", "ych")
            pr.emit("pool", lambda e: e.tensor_copy(y_bf[:, :], cq[:, :]),
                    sem="ybf")
            pr.wait("tensor", "ybf")
            for nn in range(2):
                for mo in range(DB):
                    s = mo % 2
                    po = po_[s]
                    pr.wait("tensor", "oev", max(0, pr.counts.get("oev", 0) - 1))
                    pr.emit("tensor", lambda e, mo=mo, nn=nn, s=s: e.matmul(
                        po_[s][:, :], wo[:, mo * P:(mo + 1) * P],
                        y_bf[:, nn * 512:(nn + 1) * 512],
                        start=True, stop=True), sem="pe")
                    pr.wait("scalar", "pe")
                    pr.emit("scalar", lambda e, s=s: e.activation(
                        t1b[:, :], po_[s][:, :], AF.Identity), sem="act")
                    pr.wait("vector", "act")
                    osl = slice(mo * T + th * 1024 + nn * 512,
                                mo * T + th * 1024 + nn * 512 + 512)
                    pr.emit("vector", lambda e, osl=osl: e.tensor_add(
                        oacc[:, osl], oacc[:, osl], t1b[:, :]), sem="oev")
                    pr.wait("scalar", "oev")
        pe_hist2.append(pr.counts.get("pe", 0))

    pr.wait("vector", "oev")
    for b in range(DB):
        pr.emit("vector", lambda e, b=b: e.scalar_tensor_tensor(
            xsb[:, b * T:(b + 1) * T], oacc[:, b * T:(b + 1) * T],
            w["outb_c"][:, b:b + 1], xsb[:, b * T:(b + 1) * T],
            op0=ALU.add, op1=ALU.add), sem="dve")


def build_l0_ffn_program():
    nc = bass.Bass("TRN2", target_bir_lowering=False)
    x_in = nc.dram_tensor("x_in", [D, T], F32, kind="ExternalInput").ap()
    x_out = nc.dram_tensor("x_out", [D, T], F32, kind="ExternalOutput").ap()
    d_WinT = nc.dram_tensor("WinT", [D, 64], F32, kind="ExternalInput").ap()
    d_bin = nc.dram_tensor("bin_", [64, 1], F32, kind="ExternalInput").ap()
    d_W1 = nc.dram_tensor("W1", [64, 16], F32, kind="ExternalInput").ap()
    d_b1 = nc.dram_tensor("b1", [64, 16], F32, kind="ExternalInput").ap()
    d_W2 = nc.dram_tensor("W2", [64, 16], F32, kind="ExternalInput").ap()
    d_b2 = nc.dram_tensor("b2", [64, 1], F32, kind="ExternalInput").ap()
    d_WoutT = nc.dram_tensor("WoutT", [64, D], F32, kind="ExternalInput").ap()
    d_bout = nc.dram_tensor("bout_c", [P, DB], F32, kind="ExternalInput").ap()
    d_ones = nc.dram_tensor("ones1", [1, P], F32, kind="ExternalInput").ap()
    d_onesc = nc.dram_tensor("ones_c", [P, 1], F32, kind="ExternalInput").ap()

    es = ExitStack()
    with es:
        pr = Prog(nc, es)
        xsb = pr.sb("xsb", [P, DB * T])
        xb = pr.sb("xb", [P, DB * T], BF16)
        sq = pr.sb("sq", [P, 2 * 512], BF16)
        rowsB = pr.sb("rowsB", [1, 4 * T])
        rep = pr.sb("rep", [P, 2 * T])
        s64 = pr.sb("s64", [P, T])
        g64 = pr.sb("g64", [P, T])
        relu = pr.sb("relu", [P, T])
        gb = pr.sb("gb", [64, T], BF16)
        tmp = pr.sb("tmp", [P, 512])
        ones_b = pr.sb("ones_b", [1, P], BF16)
        ones_f = pr.sb("ones_f", [1, P])
        ones_cf = pr.sb("ones_cf", [P, 1])
        ones_cb = pr.sb("ones_cb", [P, 1], BF16)
        WinT_f = pr.sb("WinT_f", [P, DB * 64])
        WinT = pr.sb("WinT_s", [P, DB * 64], BF16)
        WoutT_f = pr.sb("WoutT_f", [64, D])
        WoutT = pr.sb("WoutT_s", [64, D], BF16)
        W1c = pr.sb("W1c", [P, 8])
        b1c = pr.sb("b1c", [P, 8])
        W2c = pr.sb("W2c", [P, 8])
        binc = pr.sb("binc", [64, 1])
        b2c = pr.sb("b2c", [64, 1])
        bout_c = pr.sb("bout_c_s", [P, DB])
        bout_half = pr.sb("bout_half", [P, DB])
        p0 = pr.ps("p0")
        p1 = pr.ps("p1")

        def ld(dst, src):
            pr.emit("sync", lambda e, dst=dst, src=src: e.dma_start(
                out=dst, in_=src), sem=pr.ldsem, amt=16)

        for b in range(DB):
            ld(xsb[:, b * T:(b + 1) * T], x_in[b * P:(b + 1) * P, :])
        for b in range(DB):
            ld(WinT_f[:, b * 64:(b + 1) * 64], d_WinT[b * P:(b + 1) * P, :])
        ld(WoutT_f[:, :], d_WoutT[:, :])
        ld(W1c[0:64, :], d_W1[:, 0:8])
        ld(W1c[64:128, :], d_W1[:, 8:16])
        ld(b1c[0:64, :], d_b1[:, 0:8])
        ld(b1c[64:128, :], d_b1[:, 8:16])
        ld(W2c[0:64, :], d_W2[:, 0:8])
        ld(W2c[64:128, :], d_W2[:, 8:16])
        ld(binc[:, :], d_bin[:, :])
        ld(b2c[:, :], d_b2[:, :])
        ld(bout_c[:, :], d_bout[:, :])
        ld(ones_f[:, :], d_ones[:, :])
        ld(ones_cf[:, :], d_onesc[:, :])

        pr.wait("vector", pr.ldsem)
        pr.emit("vector", lambda e: e.tensor_copy(WinT[:, :], WinT_f[:, :]))
        pr.emit("vector", lambda e: e.tensor_copy(
            WoutT[0:64, :], WoutT_f[0:64, :]))
        pr.emit("vector", lambda e: e.tensor_copy(ones_b[:, :], ones_f[:, :]))
        pr.emit("vector", lambda e: e.tensor_copy(ones_cb[:, :], ones_cf[:, :]))
        pr.emit("vector", lambda e: e.tensor_scalar(
            bout_half[:, :], bout_c[:, :], 0.5, None, op0=ALU.mult), sem="dve")
        pr.wait("scalar", pr.ldsem)
        pr.wait("tensor", pr.ldsem)
        pr.wait("tensor", "dve")
        pr.wait("scalar", "dve")

        sc = dict(xb=xb, sq=sq, rowsB=rowsB, rep=rep, p0=p0, p1=p1,
                  ones_b=ones_b, ones_f=ones_f, ones_cb=ones_cb,
                  ones_cf=ones_cf, s64=s64, g64=g64,
                  relu=relu, gb=gb, tmp=tmp)
        w = dict(WinT=WinT, W1c=W1c, b1c=b1c, W2c=W2c, binc=binc, b2c=b2c,
                 WoutT=WoutT, bout_half=bout_half)
        emit_ln(pr, xsb, xb, None, sc)
        emit_l0_ffn(pr, xb, xsb, w, sc)

        pr.wait("sync", "dve")
        for b in range(DB):
            pr.emit("sync", lambda e, b=b: e.dma_start(
                out=x_out[b * P:(b + 1) * P, :],
                in_=xsb[:, b * T:(b + 1) * T]), sem="st", amt=16)
        pr.wait("sync", "st")
        pr.finalize()
    return nc


def prep_l0_ffn_inputs(inputs):
    c = common_consts()
    return {
        "WinT": np.ascontiguousarray(inputs["L0_ffn_Win"].astype(np.float32)),
        "bin_": np.ascontiguousarray(
            inputs["L0_ffn_bin_"].astype(np.float32).reshape(64, 1)),
        "W1": np.ascontiguousarray(inputs["L0_ffn_W1"].astype(np.float32)),
        "b1": np.ascontiguousarray(inputs["L0_ffn_b1"].astype(np.float32)),
        "W2": np.ascontiguousarray(inputs["L0_ffn_W2"].astype(np.float32)),
        "b2": np.ascontiguousarray(
            inputs["L0_ffn_b2"].astype(np.float32).reshape(64, 1)),
        "WoutT": np.ascontiguousarray(
            inputs["L0_ffn_Wout"].astype(np.float32)),
        "bout_c": _cols(inputs["L0_ffn_bout"]),
        "ones1": c["ones1"],
        "ones_c": c["ones_c"],
    }


def build_moe_program():
    nc = bass.Bass("TRN2", target_bir_lowering=False)
    x_in = nc.dram_tensor("x_in", [D, T], F32, kind="ExternalInput").ap()
    x_out = nc.dram_tensor("x_out", [D, T], F32, kind="ExternalOutput").ap()
    d_rwT = nc.dram_tensor("rwT", [D, NEXP], F32, kind="ExternalInput").ap()
    d_w1 = nc.dram_tensor("w1", [NEXP, D, INNER], F32,
                          kind="ExternalInput").ap()
    d_w2 = nc.dram_tensor("w2", [NEXP, INNER, D], F32,
                          kind="ExternalInput").ap()
    d_b1c = nc.dram_tensor("b1c", [NEXP, P, 16], F32,
                           kind="ExternalInput").ap()
    d_b2c = nc.dram_tensor("b2c", [NEXP, P, 8], F32,
                           kind="ExternalInput").ap()
    d_ones = nc.dram_tensor("ones1", [1, P], F32, kind="ExternalInput").ap()
    d_onesc = nc.dram_tensor("ones_c", [P, 1], F32, kind="ExternalInput").ap()
    d_ident = nc.dram_tensor("ident", [P, P], F32, kind="ExternalInput").ap()
    d_oh = nc.dram_tensor("onehot14", [NEXP, NEXP * P], F32,
                          kind="ExternalInput").ap()

    es = ExitStack()
    with es:
        pr = Prog(nc, es)
        xsb = pr.sb("xsb", [P, DB * T])
        xb = pr.sb("xb", [P, DB * T], BF16)
        ohb = pr.sb("ohb", [NEXP, NEXP * P], BF16)
        ones_b = pr.sb("ones_b", [1, P], BF16)
        ones_f = pr.sb("ones_f", [1, P])
        ones_cf = pr.sb("ones_cf", [P, 1])
        ones_cb = pr.sb("ones_cb", [P, 1], BF16)
        identf = pr.sb("identf", [P, P])
        msel_b = pr.sb("msel_b", [NEXP, T], BF16)
        ones14 = pr.sb("ones14", [P, NEXP])
        p0 = pr.ps("p0")
        p1 = pr.ps("p1")
        p2 = pr.ps("p2")
        p3 = pr.ps("p3")
        ph0 = pr.ps("ph0")
        ph1 = pr.ps("ph1")
        py0 = pr.ps("py0")
        py1 = pr.ps("py1")

        es2 = es.enter_context(ExitStack())
        sq = pr.sb("sq", [P, 2 * 512], BF16, es=es2)
        rowsB = pr.sb("rowsB", [1, 4 * T], es=es2)
        rep = pr.sb("rep", [P, 2 * T], es=es2)
        rwT = pr.sb("rwT_s", [P, DB * NEXP], es=es2)
        colsum_n = pr.sb("colsum_n", [NEXP, 1], es=es2)
        lgs = pr.sb("lgs", [NEXP, T], es=es2)
        lgT = pr.sb("lgT", [P, 16 * NEXP], es=es2)
        red = pr.sb("red", [P, 8], es=es2)
        mt = pr.sb("mt", [P, NEXP], es=es2)
        cums = pr.sb("cums", [P, NEXP], es=es2)
        msel = pr.sb("msel", [NEXP, T], es=es2)

        def ld(dst, src):
            pr.emit("sync", lambda e, dst=dst, src=src: e.dma_start(
                out=dst, in_=src), sem=pr.ldsem, amt=16)

        for b in range(DB):
            ld(xsb[:, b * T:(b + 1) * T], x_in[b * P:(b + 1) * P, :])
        for b in range(DB):
            ld(rwT[:, b * NEXP:(b + 1) * NEXP], d_rwT[b * P:(b + 1) * P, :])
        ld(ones_f[:, :], d_ones[:, :])
        ld(ones_cf[:, :], d_onesc[:, :])
        ld(identf[:, :], d_ident[:, :])
        ld(rep[0:NEXP, 0:NEXP * P], d_oh[:, :])

        pr.wait("vector", pr.ldsem)
        pr.emit("vector", lambda e: e.tensor_copy(ones_b[:, :], ones_f[:, :]))
        pr.emit("vector", lambda e: e.tensor_copy(ones_cb[:, :], ones_cf[:, :]))
        pr.emit("vector", lambda e: e.tensor_copy(
            ohb[:, :], rep[0:NEXP, 0:NEXP * P]))
        pr.emit("vector", lambda e: e.memset(ones14[:, :], 1.0), sem="dve")
        pr.wait("scalar", pr.ldsem)
        pr.wait("tensor", pr.ldsem)
        pr.wait("tensor", "dve")
        pr.wait("scalar", "dve")

        sc = dict(xb=xb, sq=sq, rowsB=rowsB, rep=rep, p0=p0, p1=p1,
                  ones_b=ones_b, ones_f=ones_f, ones_cb=ones_cb,
                  ones_cf=ones_cf)
        emit_ln(pr, xsb, xb, None, sc)
        r = dict(rwT=rwT, colsum_n=colsum_n, lgs=lgs, lgT=lgT, red=red, mt=mt,
                 cums=cums, ones14=ones14, msel=msel, msel_b=msel_b,
                 identf=identf, ones_f=ones_f, p2=p2, p3=p3)
        emit_router(pr, xsb, sc, r)
        pr.barrier("router_done")
        es2.close()

        w1s = pr.sb("w1s", [P, 8 * T], BF16)
        w2s = pr.sb("w2s", [P, 16 * 1024], BF16)
        w1f = pr.sb("w1f", [P, 2 * 1024])
        b1c = pr.sb("b1c_s", [P, NEXP * 16])
        b2c = pr.sb("b2c_s", [P, NEXP * 8])
        mselr = pr.sb("mselr", [P, 2 * T], BF16)
        hid = pr.sb("hid", [P, 16 * 512], BF16)
        t1 = pr.sb("t1", [P, 4 * 512], BF16)
        mm = dict(w1s=w1s, w2s=w2s, w1f=w1f, b1c=b1c, b2c=b2c,
                  mselr=mselr, hid=hid, t1=t1, msel_b=msel_b, ohb=ohb,
                  ph0=ph0, ph1=ph1, py0=py0, py1=py1, p2=p2,
                  p0=p0, p1=p1, p2b=p2, p3b=p3)
        dram = dict(d_w1=d_w1, d_w2=d_w2, d_b1c=d_b1c, d_b2c=d_b2c)
        emit_moe_experts(pr, xb, xsb, mm, dram)

        pr.wait("sync", "dve")
        pr.wait("sync", "yacc")
        for b in range(DB):
            pr.emit("sync", lambda e, b=b: e.dma_start(
                out=x_out[b * P:(b + 1) * P, :],
                in_=xsb[:, b * T:(b + 1) * T]), sem="st", amt=16)
        pr.wait("sync", "st")
        pr.finalize()
    return nc



def build_l0_attn_program():
    nc = bass.Bass("TRN2", target_bir_lowering=False)
    x_in = nc.dram_tensor("x_in", [D, T], F32, kind="ExternalInput").ap()
    x_out = nc.dram_tensor("x_out", [D, T], F32, kind="ExternalOutput").ap()
    d_WxT = nc.dram_tensor("WxT", [D, 96], F32, kind="ExternalInput").ap()
    d_WdtT = nc.dram_tensor("WdtT", [64, D], F32, kind="ExternalInput").ap()
    d_bdt = nc.dram_tensor("bdt_c", [P, DB], F32, kind="ExternalInput").ap()
    d_Alog = nc.dram_tensor("Alog_c", [P, DB * 16], F32, kind="ExternalInput").ap()
    d_Dc = nc.dram_tensor("D_c", [P, DB], F32, kind="ExternalInput").ap()
    d_ones = nc.dram_tensor("ones1", [1, P], F32, kind="ExternalInput").ap()
    d_onesc = nc.dram_tensor("ones_c", [P, 1], F32, kind="ExternalInput").ap()

    es = ExitStack()
    with es:
        pr = Prog(nc, es)
        xsb = pr.sb("xsb", [P, DB * T])
        xn = pr.sb("xn", [P, DB * T])
        ones_b = pr.sb("ones_b", [1, P], BF16)
        ones_f = pr.sb("ones_f", [1, P])
        ones_cf = pr.sb("ones_cf", [P, 1])
        ones_cb = pr.sb("ones_cb", [P, 1], BF16)
        WxdtT = pr.sb("WxdtT", [P, DB * 64], BF16)
        WxBT = pr.sb("WxBT", [P, DB * 16], BF16)
        WxCT = pr.sb("WxCT", [P, DB * 16], BF16)
        WdtT = pr.sb("WdtT_s", [64, D], BF16)
        bdt_c = pr.sb("bdt_c_s", [P, DB])
        A_cols = pr.sb("A_cols", [P, DB * 16])
        D_c = pr.sb("D_c_s", [P, DB])
        p0 = pr.ps("p0")
        p1 = pr.ps("p1")
        p2 = pr.ps("p2")
        p3 = pr.ps("p3")
        pB = pr.ps("pB")
        pC = pr.ps("pC")

        es2 = es.enter_context(ExitStack())
        sq = pr.sb("sq", [P, 2 * 512], F32, es=es2)
        rowsB = pr.sb("rowsB", [1, 4 * T], es=es2)
        rep = pr.sb("rep", [P, 2 * T], es=es2)
        wstage = pr.sb("wstage", [P, DB * 96], F32, es=es2)
        wstage2 = pr.sb("wstage2", [64, D], F32, es=es2)

        def ld(dst, src):
            pr.emit("sync", lambda e, dst=dst, src=src: e.dma_start(
                out=dst, in_=src), sem=pr.ldsem, amt=16)

        for b in range(DB):
            ld(xsb[:, b * T:(b + 1) * T], x_in[b * P:(b + 1) * P, :])
        for b in range(DB):
            ld(wstage[:, b * 96:(b + 1) * 96], d_WxT[b * P:(b + 1) * P, :])
        ld(wstage2[:, :], d_WdtT[:, :])
        ld(bdt_c[:, :], d_bdt[:, :])
        ld(A_cols[:, :], d_Alog[:, :])
        ld(D_c[:, :], d_Dc[:, :])
        ld(ones_f[:, :], d_ones[:, :])
        ld(ones_cf[:, :], d_onesc[:, :])

        pr.wait("vector", pr.ldsem)
        pr.emit("vector", lambda e: e.tensor_copy(ones_b[:, :], ones_f[:, :]))
        pr.emit("vector", lambda e: e.tensor_copy(ones_cb[:, :], ones_cf[:, :]))
        for b in range(DB):
            pr.emit("vector", lambda e, b=b: e.tensor_copy(
                WxdtT[:, b * 64:(b + 1) * 64],
                wstage[:, b * 96: b * 96 + 64]))
            pr.emit("vector", lambda e, b=b: e.tensor_copy(
                WxBT[:, b * 16:(b + 1) * 16],
                wstage[:, b * 96 + 64: b * 96 + 80]))
            pr.emit("vector", lambda e, b=b: e.tensor_copy(
                WxCT[:, b * 16:(b + 1) * 16],
                wstage[:, b * 96 + 80: b * 96 + 96]))
        pr.emit("vector", lambda e: e.tensor_copy(WdtT[0:64, :], wstage2[0:64, :]))
        pr.tag("vector", "dve")
        pr.wait("scalar", pr.ldsem)
        pr.emit("scalar", lambda e: e.activation(A_cols[:, :], A_cols[:, :],
                                                 AF.Exp), sem="act")
        pr.wait("vector", "act")
        pr.emit("vector", lambda e: e.tensor_scalar(
            A_cols[:, :], A_cols[:, :], -1.0, None, op0=ALU.mult), sem="dve")
        pr.wait("tensor", pr.ldsem)
        pr.wait("tensor", "dve")
        pr.wait("scalar", "dve")

        sc = dict(xb=None, sq=sq, rowsB=rowsB, rep=rep, p0=p0, p1=p1,
                  ones_b=ones_b, ones_f=ones_f, ones_cb=ones_cb,
                  ones_cf=ones_cf)
        emit_ln(pr, xsb, None, xn, sc)
        pr.barrier("ln_done")
        es2.close()

        xnb_c = pr.sb("xnb_c", [P, DB * 512], BF16)
        dtF = pr.sb("dtF", [P, DB * 512])
        dtlo = pr.sb("dtlo", [64, 512], BF16)
        Bsb = pr.sb("Bsb", [16, 512])
        Csb = pr.sb("Csb", [16, 512])
        h_state = pr.sb("h_state", [P, P])
        sp1 = pr.sb("sp1", [P, 512])
        sp2 = pr.sb("sp2", [P, 512])
        a_t = pr.sb("a_t", [P, 512])
        u2_t = pr.sb("u2_t", [P, 512])
        hs_t = pr.sb("hs_t", [P, 512])
        brow = pr.sb("brow", [1, 4 * 512])
        w = dict(WxdtT=WxdtT, WxBT=WxBT, WxCT=WxCT, WdtT=WdtT, bdt_c=bdt_c,
                 A_cols=A_cols, D_c=D_c)
        sc2 = dict(xnb_c=xnb_c, dtF=dtF, dtlo=dtlo, Bsb=Bsb, Csb=Csb,
                   h_state=h_state, sp1=sp1, sp2=sp2, a_t=a_t, u2_t=u2_t,
                   hs_t=hs_t, brow=brow, ones_f=ones_f,
                   p0=p0, p1=p1, p2=p2, p3=p3, pB=pB, pC=pC)
        emit_l0_attn(pr, xsb, xn, w, sc2)

        pr.wait("sync", "dve")
        for b in range(DB):
            pr.emit("sync", lambda e, b=b: e.dma_start(
                out=x_out[b * P:(b + 1) * P, :],
                in_=xsb[:, b * T:(b + 1) * T]), sem="st", amt=16)
        pr.wait("sync", "st")
        pr.finalize()
    return nc


def prep_l0_attn_inputs(inputs):
    c = common_consts()
    Alog = inputs["L0_attn_A_log"].astype(np.float32)
    Alog_c = np.ascontiguousarray(
        Alog.reshape(DB, P, 16).transpose(1, 0, 2).reshape(P, DB * 16))
    return {
        "WxT": np.ascontiguousarray(
            inputs["L0_attn_W_xproj"].astype(np.float32).T),
        "WdtT": np.ascontiguousarray(
            inputs["L0_attn_W_dt"].astype(np.float32).T),
        "bdt_c": _cols(inputs["L0_attn_b_dt"]),
        "Alog_c": Alog_c,
        "D_c": _cols(inputs["L0_attn_D_param"]),
        "ones1": c["ones1"],
        "ones_c": c["ones_c"],
    }



def emit_ln_zres(pr, xsb, znb, gamma_col, sc):
    rep, p0, p1 = sc["rep"], sc["p0"], sc["p1"]
    rowsB = sc["rowsB"]
    sq = sc["sq"]
    tmpb = sc["tmpb"]
    m_r = rowsB[0:1, 0:T]
    v_r = rowsB[0:1, T:2 * T]
    rstd_r = rowsB[0:1, 2 * T:3 * T]
    aux_r = rowsB[0:1, 3 * T:4 * T]

    pr.wait("tensor", "dve")
    pr.wait("scalar", "dve")
    slot_free_at = [pr.counts.get("pe", 0)] * 2
    for n in range(NT):
        for b in range(DB):
            pr.emit("tensor", lambda e, b=b, n=n: e.matmul(
                p0[0:1, :], sc["ones_cf"][:, 0:1],
                xsb[:, b * T + n * 512: b * T + (n + 1) * 512],
                start=(b == 0), stop=(b == DB - 1)),
                sem="pe" if b == DB - 1 else None)
        for b in range(DB):
            s = b % 2
            pr.wait("scalar", "pe", slot_free_at[s])
            pr.emit("scalar", lambda e, b=b, n=n, s=s: e.activation(
                sq[:, s * 512:(s + 1) * 512],
                xsb[:, b * T + n * 512: b * T + (n + 1) * 512],
                AF.Square), sem="act")
            pr.wait("tensor", "act")
            slot_free_at[s] = pr.emit("tensor", lambda e, s=s, b=b: e.matmul(
                p1[0:1, :], sc["ones_cf"][:, 0:1], sq[:, s * 512:(s + 1) * 512],
                start=(b == 0), stop=(b == DB - 1)), sem="pe")
        pr.wait("vector", "pe")
        pr.emit("vector", lambda e, n=n: e.tensor_copy(
            m_r[:, n * 512:(n + 1) * 512], p0[0:1, :]))
        pr.emit("vector", lambda e, n=n: e.tensor_copy(
            v_r[:, n * 512:(n + 1) * 512], p1[0:1, :]), sem="dve")
        pr.wait("tensor", "dve")

    def vs():
        pr.tag("vector", "dve")
        pr.wait("vector", "dve")

    pr.emit("vector", lambda e: e.tensor_scalar(
        m_r, m_r, 1.0 / D, None, op0=ALU.mult))
    vs()
    pr.emit("vector", lambda e: e.tensor_mul(aux_r, m_r, m_r))
    vs()
    pr.emit("vector", lambda e: e.tensor_scalar(
        v_r, v_r, 1.0 / D, EPS_LN, op0=ALU.mult, op1=ALU.add))
    vs()
    pr.emit("vector", lambda e: e.tensor_sub(v_r, v_r, aux_r))
    vs()
    pr.emit("vector", lambda e: e.tensor_scalar(
        aux_r, v_r, -EPS_LN, None, op0=ALU.add))
    vs()
    pr.emit("vector", lambda e: e.reciprocal(rstd_r, v_r))
    vs()
    pr.emit("vector", lambda e: e.tensor_mul(aux_r, aux_r, rstd_r))
    vs()
    pr.emit("vector", lambda e: e.tensor_scalar(
        aux_r, aux_r, EPS_RMS, None, op0=ALU.add), sem="dve")
    pr.wait("scalar", "dve")
    pr.emit("scalar", lambda e: e.activation(rstd_r, v_r, AF.Ln))
    pr.tag("scalar", "act")
    pr.wait("scalar", "act")
    pr.emit("scalar", lambda e: e.activation(
        rstd_r, rstd_r, AF.Exp, scale=-0.5))
    pr.tag("scalar", "act")
    pr.wait("scalar", "act")
    pr.emit("scalar", lambda e: e.activation(aux_r, aux_r, AF.Ln))
    pr.tag("scalar", "act")
    pr.wait("scalar", "act")
    pr.emit("scalar", lambda e: e.activation(
        aux_r, aux_r, AF.Exp, scale=-0.5), sem="act")

    pr.wait("tensor", "act")
    for n in range(NT):
        pr.emit("tensor", lambda e, n=n: e.matmul(
            p0[:, :], sc["ones_f"][0:1, :], m_r[:, n * 512:(n + 1) * 512],
            start=True, stop=True))
        pr.emit("tensor", lambda e, n=n: e.matmul(
            p1[:, :], sc["ones_f"][0:1, :], rstd_r[:, n * 512:(n + 1) * 512],
            start=True, stop=True), sem="pe")
        pr.wait("vector", "pe")
        pr.emit("vector", lambda e, n=n: e.tensor_copy(
            rep[:, n * 512:(n + 1) * 512], p0[:, :]))
        pr.emit("vector", lambda e, n=n: e.tensor_copy(
            rep[:, T + n * 512:T + (n + 1) * 512], p1[:, :]), sem="dve")
        pr.wait("tensor", "dve")
        pr.emit("tensor", lambda e, n=n: e.matmul(
            p0[:, :], sc["ones_f"][0:1, :], aux_r[:, n * 512:(n + 1) * 512],
            start=True, stop=True), sem="pe")
        pr.wait("vector", "pe")
        pr.emit("vector", lambda e, n=n: e.tensor_copy(
            rep[:, 2 * T + n * 512:2 * T + (n + 1) * 512], p0[:, :]),
            sem="dve")
        pr.wait("tensor", "dve")

    m_rep = rep[:, 0:T]
    rstd_rep = rep[:, T:2 * T]
    r2_rep = rep[:, 2 * T:3 * T]
    for b in range(DB):
        xb_sl = xsb[:, b * T:(b + 1) * T]
        zn_sl = znb[:, b * T:(b + 1) * T]
        pr.emit("vector", lambda e, xb_sl=xb_sl: e.tensor_sub(
            tmpb[:, :], xb_sl, m_rep))
        vs()
        pr.emit("vector", lambda e: e.tensor_mul(
            tmpb[:, :], tmpb[:, :], rstd_rep))
        vs()
        pr.emit("vector", lambda e, xb_sl=xb_sl: e.tensor_add(
            xb_sl, xb_sl, tmpb[:, :]))
        pr.emit("vector", lambda e, zn_sl=zn_sl: e.tensor_mul(
            zn_sl, tmpb[:, :], r2_rep))
        vs()
        pr.emit("vector", lambda e, zn_sl=zn_sl, b=b: e.tensor_scalar(
            zn_sl, zn_sl, sc["normw_c"][:, b:b + 1], None, op0=ALU.mult))
    pr.tag("vector", "dve")


def build_l1_attn_program():
    nc = bass.Bass("TRN2", target_bir_lowering=False)
    x_in = nc.dram_tensor("x_in", [D, T], F32, kind="ExternalInput").ap()
    x_out = nc.dram_tensor("x_out", [D, T], F32, kind="ExternalOutput").ap()
    d_inwT = nc.dram_tensor("inwT", [D, INNER], F32, kind="ExternalInput").ap()
    d_gatewT = nc.dram_tensor("gatewT", [D, INNER], F32, kind="ExternalInput").ap()
    d_bwT = nc.dram_tensor("bwT", [INNER, INNER], F32, kind="ExternalInput").ap()
    d_cwT = nc.dram_tensor("cwT", [INNER, INNER], F32, kind="ExternalInput").ap()
    d_dwT = nc.dram_tensor("dwT", [INNER, INNER], F32, kind="ExternalInput").ap()
    d_outwT = nc.dram_tensor("outwT", [INNER, D], F32, kind="ExternalInput").ap()
    d_inb = nc.dram_tensor("inb_c", [P, IBL], F32, kind="ExternalInput").ap()
    d_gateb = nc.dram_tensor("gateb_c", [P, IBL], F32, kind="ExternalInput").ap()
    d_bb = nc.dram_tensor("bb_c", [P, IBL], F32, kind="ExternalInput").ap()
    d_cb = nc.dram_tensor("cb_c", [P, IBL], F32, kind="ExternalInput").ap()
    d_db = nc.dram_tensor("db_c", [P, IBL], F32, kind="ExternalInput").ap()
    d_outb = nc.dram_tensor("outb_c", [P, DB], F32, kind="ExternalInput").ap()
    d_alog = nc.dram_tensor("alog_c", [P, IBL], F32, kind="ExternalInput").ap()
    d_normw = nc.dram_tensor("normw_c", [P, DB], F32, kind="ExternalInput").ap()
    d_ones = nc.dram_tensor("ones1", [1, P], F32, kind="ExternalInput").ap()
    d_onesc = nc.dram_tensor("ones_c", [P, 1], F32, kind="ExternalInput").ap()

    es = ExitStack()
    with es:
        pr = Prog(nc, es)
        xsb = pr.sb("xsb", [P, DB * T])
        znb = pr.sb("znb", [P, DB * T], BF16)
        ones_f = pr.sb("ones_f", [1, P])
        ones_cf = pr.sb("ones_cf", [P, 1])
        inb_c = pr.sb("inb_c_s", [P, IBL])
        gateb_c = pr.sb("gateb_c_s", [P, IBL])
        bb_c = pr.sb("bb_c_s", [P, IBL])
        cb_c = pr.sb("cb_c_s", [P, IBL])
        db_c = pr.sb("db_c_s", [P, IBL])
        outb_c = pr.sb("outb_c_s", [P, DB])
        a_dec = pr.sb("a_dec", [P, IBL])
        normw_c = pr.sb("normw_c_s", [P, DB])
        asp = pr.sb("asp", [P, 2 * IBL])
        p0 = pr.ps("p0")
        p1 = pr.ps("p1")
        p2 = pr.ps("p2")
        p3 = pr.ps("p3")
        p4 = pr.ps("p4")

        def ld(dst, srcd):
            pr.emit("sync", lambda e, dst=dst, srcd=srcd: e.dma_start(
                out=dst, in_=srcd), sem=pr.ldsem, amt=16)

        for b in range(DB):
            ld(xsb[:, b * T:(b + 1) * T], x_in[b * P:(b + 1) * P, :])
        ld(inb_c[:, :], d_inb[:, :])
        ld(gateb_c[:, :], d_gateb[:, :])
        ld(bb_c[:, :], d_bb[:, :])
        ld(cb_c[:, :], d_cb[:, :])
        ld(db_c[:, :], d_db[:, :])
        ld(outb_c[:, :], d_outb[:, :])
        ld(a_dec[:, :], d_alog[:, :])
        ld(normw_c[:, :], d_normw[:, :])
        ld(ones_f[:, :], d_ones[:, :])
        ld(ones_cf[:, :], d_onesc[:, :])

        def ss():
            pr.tag("scalar", "act")
            pr.wait("scalar", "act")

        pr.wait("scalar", pr.ldsem)
        pr.emit("scalar", lambda e: e.activation(asp[:, 0:IBL], a_dec[:, :],
                                                 AF.Relu))
        ss()
        pr.emit("scalar", lambda e: e.activation(asp[:, IBL:2 * IBL],
                                                 a_dec[:, :], AF.Abs))
        ss()
        pr.emit("scalar", lambda e: e.activation(
            asp[:, IBL:2 * IBL], asp[:, IBL:2 * IBL], AF.Exp, scale=-1.0))
        ss()
        pr.emit("scalar", lambda e: e.activation(
            asp[:, IBL:2 * IBL], asp[:, IBL:2 * IBL], AF.Ln, bias=1.0),
            sem="act")
        pr.wait("vector", "act")
        pr.emit("vector", lambda e: e.tensor_add(
            asp[:, 0:IBL], asp[:, 0:IBL], asp[:, IBL:2 * IBL]), sem="dve")
        pr.wait("scalar", "dve")
        pr.emit("scalar", lambda e: e.activation(
            a_dec[:, :], asp[:, 0:IBL], AF.Exp, scale=-1.0), sem="act")
        pr.wait("vector", pr.ldsem)
        pr.wait("tensor", pr.ldsem)
        pr.wait("vector", "act")
        pr.wait("tensor", "act")

        es2 = es.enter_context(ExitStack())
        sq = pr.sb("sq", [P, 2 * 512], F32, es=es2)
        rowsB = pr.sb("rowsB", [1, 4 * T], es=es2)
        rep = pr.sb("rep", [P, 3 * T], es=es2)
        tmpb = pr.sb("tmpb", [P, T], es=es2)
        sc = dict(sq=sq, rowsB=rowsB, rep=rep, tmpb=tmpb, ones_cf=ones_cf,
                  ones_f=ones_f, normw_c=normw_c, p0=p0, p1=p1)
        emit_ln_zres(pr, xsb, znb, normw_c, sc)
        pr.barrier("zres_done")
        es2.close()

        u_bf = pr.sb("u_bf", [P, IBL * T], BF16)
        es3 = es.enter_context(ExitStack())
        wstg1 = pr.sb("wstg1", [P, 2 * 4 * P], F32, es=es3)
        wcol_in = pr.sb("wcol_in", [P, 2 * DB * P], BF16, es=es3)
        wcol_g = pr.sb("wcol_g", [P, 2 * DB * P], BF16, es=es3)
        t2 = pr.sb("t2", [P, 512], F32, es=es3)
        t3 = pr.sb("t3", [P, 512], F32, es=es3)
        w1 = dict(d_inwT=d_inwT, d_gatewT=d_gatewT, inb_c=inb_c,
                  gateb_c=gateb_c)
        sc1 = dict(wstg=wstg1, wcol_in=wcol_in, wcol_g=wcol_g, t2=t2, t3=t3,
                   pu=p0, pg=p1)
        emit_l1_attn_e1(pr, znb, u_bf, w1, sc1)
        pr.barrier("e1_done")
        es3.close()

        wstg = pr.sb("wstg", [P, 2 * 4 * P], F32)
        wb = pr.sb("wb", [P, IBL * P], BF16)
        wc = pr.sb("wc", [P, IBL * P], BF16)
        wd = pr.sb("wd", [P, IBL * P], BF16)
        wo = pr.sb("wo", [P, 1024], BF16)
        bq = pr.sb("bq", [P, 1024])
        cq = pr.sb("cq", [P, 1024])
        dq = pr.sb("dq", [P, 1024])
        hsq = pr.sb("hsq", [P, 1024])
        a_rep = pr.sb("a_rep", [P, 1024])
        onesf_t = pr.sb("onesf_t", [P, 1024])
        carry = pr.sb("carry", [P, IBL])
        y_bf = pr.sb("y_bf", [P, 1024], BF16)
        t1b = pr.sb("t1b", [P, 512], BF16)
        pr.emit("vector", lambda e: e.memset(onesf_t[:, :], 1.0), sem="dve")
        w2 = dict(d_bwT=d_bwT, d_cwT=d_cwT, d_dwT=d_dwT, d_outwT=d_outwT,
                  bb_c=bb_c, cb_c=cb_c, db_c=db_c, outb_c=outb_c,
                  a_dec=a_dec)
        sc2 = dict(wstg=wstg, wb=wb, wc=wc, wd=wd, wo=wo, bq=bq, cq=cq,
                   dq=dq, hsq=hsq, a_rep=a_rep, onesf_t=onesf_t, carry=carry,
                   y_bf=y_bf, oacc=znb, t1b=t1b,
                   pb_=p0, pc_=p1, pd_=p2, po_=[p3, p4])
        emit_l1_attn_e2(pr, u_bf, xsb, w2, sc2)

        pr.wait("sync", "dve")
        for b in range(DB):
            pr.emit("sync", lambda e, b=b: e.dma_start(
                out=x_out[b * P:(b + 1) * P, :],
                in_=xsb[:, b * T:(b + 1) * T]), sem="st", amt=16)
        pr.wait("sync", "st")
        pr.finalize()
    return nc


def prep_l1_attn_inputs(inputs):
    c = common_consts()
    f = np.float32
    return {
        "inwT": np.ascontiguousarray(inputs["L1_attn_in_w"].astype(f).T),
        "gatewT": np.ascontiguousarray(inputs["L1_attn_gate_w"].astype(f).T),
        "bwT": np.ascontiguousarray(inputs["L1_attn_b_w"].astype(f).T),
        "cwT": np.ascontiguousarray(inputs["L1_attn_c_w"].astype(f).T),
        "dwT": np.ascontiguousarray(inputs["L1_attn_d_w"].astype(f).T),
        "outwT": np.ascontiguousarray(inputs["L1_attn_out_w"].astype(f).T),
        "inb_c": _cols(inputs["L1_attn_in_b"]),
        "gateb_c": _cols(inputs["L1_attn_gate_b"]),
        "bb_c": _cols(inputs["L1_attn_b_b"]),
        "cb_c": _cols(inputs["L1_attn_c_b"]),
        "db_c": _cols(inputs["L1_attn_d_b"]),
        "outb_c": _cols(inputs["L1_attn_out_b"]),
        "alog_c": _cols(inputs["L1_attn_a_log"]),
        "normw_c": _cols(inputs["L1_attn_norm_w"]),
        "ones1": c["ones1"],
        "ones_c": c["ones_c"],
    }


def prep_moe_inputs(inputs):
    c = common_consts()
    b1 = inputs["L1_ffn_b1"].astype(np.float32)
    b2 = inputs["L1_ffn_b2"].astype(np.float32)
    return {
        "rwT": np.ascontiguousarray(
            inputs["L1_ffn_router_w"].astype(np.float32)),
        "w1": np.ascontiguousarray(inputs["L1_ffn_w1"].astype(np.float32)),
        "w2": np.ascontiguousarray(inputs["L1_ffn_w2"].astype(np.float32)),
        "b1c": np.ascontiguousarray(
            np.stack([_cols(b1[e]) for e in range(NEXP)])),
        "b2c": np.ascontiguousarray(
            np.stack([_cols(b2[e]) for e in range(NEXP)])),
        "ones1": c["ones1"],
        "ones_c": c["ones_c"],
        "ident": c["ident"],
        "onehot14": c["onehot14"],
    }




def build_fused_program():
    nc = bass.Bass("TRN2", target_bir_lowering=False)
    f = F32
    x_in = nc.dram_tensor("x_in", [D, T], f, kind="ExternalInput").ap()
    x_out = nc.dram_tensor("x_out", [D, T], f, kind="ExternalOutput").ap()
    dd = {}
    def din(name, shape):
        dd[name] = nc.dram_tensor(name, list(shape), f, kind="ExternalInput").ap()
    din("WinT", [D, 64]); din("bin_", [64, 1]); din("W1", [64, 16])
    din("b1", [64, 16]); din("W2", [64, 16]); din("b2", [64, 1])
    din("WoutT", [64, D]); din("bout_c", [P, DB])
    din("WxT", [D, 96]); din("WdtT", [64, D]); din("bdt_c", [P, DB])
    din("Alog_c", [P, DB * 16]); din("D_c", [P, DB])
    din("rwT", [D, NEXP]); din("w1", [NEXP, D, INNER]); din("w2", [NEXP, INNER, D])
    din("b1c", [NEXP, P, 16]); din("b2c", [NEXP, P, 8]); din("onehot14", [NEXP, NEXP * P])
    din("inwT", [D, INNER]); din("gatewT", [D, INNER]); din("bwT", [INNER, INNER])
    din("cwT", [INNER, INNER]); din("dwT", [INNER, INNER]); din("outwT", [INNER, D])
    din("inb_c", [P, IBL]); din("gateb_c", [P, IBL]); din("bb_c", [P, IBL])
    din("cb_c", [P, IBL]); din("db_c", [P, IBL]); din("outb_c", [P, DB])
    din("alog_c", [P, IBL]); din("normw_c", [P, DB])
    din("ones1", [1, P]); din("ones_c", [P, 1]); din("ident", [P, P])
    d_w1b = nc.dram_tensor("w1bcache", [NEXP, D, INNER], BF16,
                           kind="Internal").ap()
    d_w2b = nc.dram_tensor("w2bcache", [NEXP, INNER, D], BF16,
                           kind="Internal").ap()

    es = ExitStack()
    with es:
        pr = Prog(nc, es)
        xsb = pr.sb("xsb", [P, DB * T])
        ones_b = pr.sb("ones_b", [1, P], BF16)
        ones_f = pr.sb("ones_f", [1, P])
        ones_cf = pr.sb("ones_cf", [P, 1])
        ones_cb = pr.sb("ones_cb", [P, 1], BF16)
        identf = pr.sb("identf", [P, P])
        junk = pr.sb("junk", [P, 8])
        psum = [pr.ps(f"pp{i}") for i in range(8)]

        def ld(dst, srcd, es_=None):
            pr.emit("sync", lambda e, dst=dst, srcd=srcd: e.dma_start(
                out=dst, in_=srcd), sem=pr.ldsem, amt=16)

        for b in range(DB):
            ld(xsb[:, b * T:(b + 1) * T], x_in[b * P:(b + 1) * P, :])
        ld(ones_f[:, :], dd["ones1"][:, :])
        ld(ones_cf[:, :], dd["ones_c"][:, :])
        ld(identf[:, :], dd["ident"][:, :])
        pr.wait("vector", pr.ldsem)
        pr.emit("vector", lambda e: e.memset(junk[:, :], 0.0))
        pr.emit("vector", lambda e: e.tensor_copy(ones_b[:, :], ones_f[:, :]))
        pr.emit("vector", lambda e: e.tensor_copy(ones_cb[:, :], ones_cf[:, :]),
                sem="dve")
        pr.wait("scalar", pr.ldsem)
        pr.wait("tensor", pr.ldsem)
        pr.wait("tensor", "dve")
        pr.wait("scalar", "dve")

        def endphase(tag):
            pr.wait("vector", pr.ldsem)
            pr.emit("vector", lambda e: e.tensor_copy(junk[:, 0:4],
                                                      junk[:, 4:8]))
            pr.tag("vector", "dve")
            pr.barrier(tag)

        def phase_l0_ffn(tag):
            s2 = es.enter_context(ExitStack())
            xb = pr.sb(f"xb_{tag}", [P, DB * T], BF16, es=s2)
            sq = pr.sb(f"sq_{tag}", [P, 2 * 512], BF16, es=s2)
            rowsB = pr.sb(f"rowsB_{tag}", [1, 4 * T], es=s2)
            rep = pr.sb(f"rep_{tag}", [P, 2 * T], es=s2)
            s64 = pr.sb(f"s64_{tag}", [P, T], es=s2)
            g64 = pr.sb(f"g64_{tag}", [P, T], es=s2)
            relu = pr.sb(f"relu_{tag}", [P, T], es=s2)
            gb = pr.sb(f"gb_{tag}", [64, T], BF16, es=s2)
            tmp = pr.sb(f"tmp_{tag}", [P, 512], es=s2)
            WinT_f = pr.sb(f"WinT_f_{tag}", [P, DB * 64], es=s2)
            WinT = pr.sb(f"WinT_s_{tag}", [P, DB * 64], BF16, es=s2)
            WoutT_f = pr.sb(f"WoutT_f_{tag}", [64, D], es=s2)
            WoutT = pr.sb(f"WoutT_s_{tag}", [64, D], BF16, es=s2)
            W1c = pr.sb(f"W1c_{tag}", [P, 8], es=s2)
            b1c = pr.sb(f"b1c_{tag}", [P, 8], es=s2)
            W2c = pr.sb(f"W2c_{tag}", [P, 8], es=s2)
            binc = pr.sb(f"binc_{tag}", [64, 1], es=s2)
            b2c = pr.sb(f"b2c_{tag}", [64, 1], es=s2)
            bout_c = pr.sb(f"bout_c_{tag}", [P, DB], es=s2)
            bout_half = pr.sb(f"bout_half_{tag}", [P, DB], es=s2)
            for b in range(DB):
                ld(xb[:, b * 64:(b + 1) * 64] if False else
                   WinT_f[:, b * 64:(b + 1) * 64],
                   dd["WinT"][b * P:(b + 1) * P, :])
            ld(WoutT_f[:, :], dd["WoutT"][:, :])
            ld(W1c[0:64, :], dd["W1"][:, 0:8]); ld(W1c[64:128, :], dd["W1"][:, 8:16])
            ld(b1c[0:64, :], dd["b1"][:, 0:8]); ld(b1c[64:128, :], dd["b1"][:, 8:16])
            ld(W2c[0:64, :], dd["W2"][:, 0:8]); ld(W2c[64:128, :], dd["W2"][:, 8:16])
            ld(binc[:, :], dd["bin_"][:, :]); ld(b2c[:, :], dd["b2"][:, :])
            ld(bout_c[:, :], dd["bout_c"][:, :])
            pr.wait("vector", pr.ldsem)
            pr.emit("vector", lambda e: e.tensor_copy(WinT[:, :], WinT_f[:, :]))
            pr.emit("vector", lambda e: e.tensor_copy(WoutT[0:64, :],
                                                      WoutT_f[0:64, :]))
            pr.emit("vector", lambda e: e.tensor_scalar(
                bout_half[:, :], bout_c[:, :], 0.5, None, op0=ALU.mult),
                sem="dve")
            pr.wait("tensor", "dve")
            pr.wait("scalar", "dve")
            sc = dict(xb=xb, sq=sq, rowsB=rowsB, rep=rep, p0=psum[0],
                      p1=psum[1], ones_b=ones_b, ones_f=ones_f,
                      ones_cb=ones_cb, ones_cf=ones_cf, s64=s64, g64=g64,
                      relu=relu, gb=gb, tmp=tmp)
            w = dict(WinT=WinT, W1c=W1c, b1c=b1c, W2c=W2c, binc=binc,
                     b2c=b2c, WoutT=WoutT, bout_half=bout_half)
            emit_ln(pr, xsb, xb, None, sc)
            emit_l0_ffn(pr, xb, xsb, w, sc)
            endphase(tag)
            s2.close()

        def phase_l0_attn():
            s2 = es.enter_context(ExitStack())
            xn = pr.sb("xn_B", [P, DB * T], es=s2)
            WxdtT = pr.sb("WxdtT_B", [P, DB * 64], BF16, es=s2)
            WxBT = pr.sb("WxBT_B", [P, DB * 16], BF16, es=s2)
            WxCT = pr.sb("WxCT_B", [P, DB * 16], BF16, es=s2)
            WdtT = pr.sb("WdtT_B", [64, D], BF16, es=s2)
            bdt_c = pr.sb("bdt_c_B", [P, DB], es=s2)
            A_cols = pr.sb("A_cols_B", [P, DB * 16], es=s2)
            D_c = pr.sb("D_c_B", [P, DB], es=s2)
            s3 = es.enter_context(ExitStack())
            sq = pr.sb("sq_B", [P, 2 * 512], F32, es=s3)
            rowsB = pr.sb("rowsB_B", [1, 4 * T], es=s3)
            rep = pr.sb("rep_B", [P, 2 * T], es=s3)
            wstage = pr.sb("wstage_B", [P, DB * 96], F32, es=s3)
            wstage2 = pr.sb("wstage2_B", [64, D], F32, es=s3)
            for b in range(DB):
                ld(wstage[:, b * 96:(b + 1) * 96], dd["WxT"][b * P:(b + 1) * P, :])
            ld(wstage2[:, :], dd["WdtT"][:, :])
            ld(bdt_c[:, :], dd["bdt_c"][:, :])
            ld(A_cols[:, :], dd["Alog_c"][:, :])
            ld(D_c[:, :], dd["D_c"][:, :])
            pr.wait("vector", pr.ldsem)
            for b in range(DB):
                pr.emit("vector", lambda e, b=b: e.tensor_copy(
                    WxdtT[:, b * 64:(b + 1) * 64], wstage[:, b * 96: b * 96 + 64]))
                pr.emit("vector", lambda e, b=b: e.tensor_copy(
                    WxBT[:, b * 16:(b + 1) * 16], wstage[:, b * 96 + 64: b * 96 + 80]))
                pr.emit("vector", lambda e, b=b: e.tensor_copy(
                    WxCT[:, b * 16:(b + 1) * 16], wstage[:, b * 96 + 80: b * 96 + 96]))
            pr.emit("vector", lambda e: e.tensor_copy(WdtT[0:64, :], wstage2[0:64, :]))
            pr.tag("vector", "dve")
            pr.wait("scalar", pr.ldsem)
            pr.wait("scalar", "dve")
            pr.emit("scalar", lambda e: e.activation(A_cols[:, :], A_cols[:, :],
                                                     AF.Exp), sem="act")
            pr.wait("vector", "act")
            pr.emit("vector", lambda e: e.tensor_scalar(
                A_cols[:, :], A_cols[:, :], -1.0, None, op0=ALU.mult), sem="dve")
            pr.wait("tensor", "dve")
            pr.wait("scalar", "dve")
            sc = dict(xb=None, sq=sq, rowsB=rowsB, rep=rep, p0=psum[0],
                      p1=psum[1], ones_b=ones_b, ones_f=ones_f,
                      ones_cb=ones_cb, ones_cf=ones_cf)
            emit_ln(pr, xsb, None, xn, sc)
            pr.barrier("B_ln_done")
            s3.close()
            xnb_c = pr.sb("xnb_c_B", [P, DB * 512], BF16, es=s2)
            dtF = pr.sb("dtF_B", [P, DB * 512], es=s2)
            dtlo = pr.sb("dtlo_B", [64, 512], BF16, es=s2)
            Bsb = pr.sb("Bsb_B", [16, 512], es=s2)
            Csb = pr.sb("Csb_B", [16, 512], es=s2)
            h_state = pr.sb("h_state_B", [P, P], es=s2)
            sp1 = pr.sb("sp1_B", [P, 512], es=s2)
            sp2 = pr.sb("sp2_B", [P, 512], es=s2)
            a_t = pr.sb("a_t_B", [P, 512], es=s2)
            u2_t = pr.sb("u2_t_B", [P, 512], es=s2)
            hs_t = pr.sb("hs_t_B", [P, 512], es=s2)
            brow = pr.sb("brow_B", [1, 4 * 512], es=s2)
            w = dict(WxdtT=WxdtT, WxBT=WxBT, WxCT=WxCT, WdtT=WdtT,
                     bdt_c=bdt_c, A_cols=A_cols, D_c=D_c)
            sc2 = dict(xnb_c=xnb_c, dtF=dtF, dtlo=dtlo, Bsb=Bsb, Csb=Csb,
                       h_state=h_state, sp1=sp1, sp2=sp2, a_t=a_t, u2_t=u2_t,
                       hs_t=hs_t, brow=brow, ones_f=ones_f,
                       p0=psum[0], p1=psum[1], p2=psum[2], p3=psum[3],
                       pB=psum[4], pC=psum[5])
            emit_l0_attn(pr, xsb, xn, w, sc2)
            endphase("B_done")
            s2.close()

        def phase_moe(tag):
            s2 = es.enter_context(ExitStack())
            xb = pr.sb(f"xb_{tag}", [P, DB * T], BF16, es=s2)
            ohb = pr.sb(f"ohb_{tag}", [NEXP, NEXP * P], BF16, es=s2)
            msel_b = pr.sb(f"msel_b_{tag}", [NEXP, T], BF16, es=s2)
            ones14 = pr.sb(f"ones14_{tag}", [P, NEXP], es=s2)
            s3 = es.enter_context(ExitStack())
            sq = pr.sb(f"sq_{tag}", [P, 2 * 512], BF16, es=s3)
            rowsB = pr.sb(f"rowsB_{tag}", [1, 4 * T], es=s3)
            rep = pr.sb(f"rep_{tag}", [P, 2 * T], es=s3)
            rwT = pr.sb(f"rwT_{tag}", [P, DB * NEXP], es=s3)
            colsum_n = pr.sb(f"colsum_{tag}", [NEXP, 1], es=s3)
            lgs = pr.sb(f"lgs_{tag}", [NEXP, T], es=s3)
            lgT = pr.sb(f"lgT_{tag}", [P, 16 * NEXP], es=s3)
            red = pr.sb(f"red_{tag}", [P, 8], es=s3)
            mt = pr.sb(f"mt_{tag}", [P, NEXP], es=s3)
            cums = pr.sb(f"cums_{tag}", [P, NEXP], es=s3)
            msel = pr.sb(f"msel_{tag}", [NEXP, T], es=s3)
            for b in range(DB):
                ld(rwT[:, b * NEXP:(b + 1) * NEXP], dd["rwT"][b * P:(b + 1) * P, :])
            ld(rep[0:NEXP, 0:NEXP * P], dd["onehot14"][:, :])
            pr.wait("vector", pr.ldsem)
            pr.emit("vector", lambda e: e.tensor_copy(
                ohb[:, :], rep[0:NEXP, 0:NEXP * P]))
            pr.emit("vector", lambda e: e.memset(ones14[:, :], 1.0), sem="dve")
            pr.wait("tensor", "dve")
            pr.wait("scalar", "dve")
            sc = dict(xb=xb, sq=sq, rowsB=rowsB, rep=rep, p0=psum[0],
                      p1=psum[1], ones_b=ones_b, ones_f=ones_f,
                      ones_cb=ones_cb, ones_cf=ones_cf)
            emit_ln(pr, xsb, xb, None, sc)
            r = dict(rwT=rwT, colsum_n=colsum_n, lgs=lgs, lgT=lgT, red=red,
                     mt=mt, cums=cums, ones14=ones14, msel=msel,
                     msel_b=msel_b, identf=identf, ones_f=ones_f,
                     p2=psum[2], p3=psum[3])
            emit_router(pr, xsb, sc, r)
            pr.barrier(f"router_done_{tag}")
            s3.close()
            w1s = pr.sb(f"w1s_{tag}", [P, 8 * T], BF16, es=s2)
            w2s = pr.sb(f"w2s_{tag}", [P, 16 * 1024], BF16, es=s2)
            w1f = pr.sb(f"w1f_{tag}", [P, 2 * 1024], es=s2)
            b1c = pr.sb(f"b1cs_{tag}", [P, NEXP * 16], es=s2)
            b2c = pr.sb(f"b2cs_{tag}", [P, NEXP * 8], es=s2)
            mselr = pr.sb(f"mselr_{tag}", [P, 2 * T], BF16, es=s2)
            hid = pr.sb(f"hid_{tag}", [P, 16 * 512], BF16, es=s2)
            t1 = pr.sb(f"t1_{tag}", [P, 4 * 512], BF16, es=s2)
            mm = dict(w1s=w1s, w2s=w2s, w1f=w1f, b1c=b1c, b2c=b2c,
                      mselr=mselr, hid=hid, t1=t1, msel_b=msel_b, ohb=ohb,
                      ph0=psum[4], ph1=psum[5], py0=psum[6], py1=psum[7],
                      p2=psum[2], p0=psum[0], p1=psum[1], p2b=psum[2],
                      p3b=psum[3])
            dram = dict(d_w1=dd["w1"], d_w2=dd["w2"], d_b1c=dd["b1c"],
                        d_b2c=dd["b2c"], d_w1b=d_w1b, d_w2b=d_w2b)
            emit_moe_experts(pr, xb, xsb, mm, dram,
                             cache="out" if tag == "Dx" else "in")
            pr.wait("vector", "yacc")
            endphase(tag)
            s2.close()

        def phase_l1_attn():
            s2 = es.enter_context(ExitStack())
            znb = pr.sb("znb_E", [P, DB * T], BF16, es=s2)
            inb_c = pr.sb("inb_c_E", [P, IBL], es=s2)
            gateb_c = pr.sb("gateb_c_E", [P, IBL], es=s2)
            bb_c = pr.sb("bb_c_E", [P, IBL], es=s2)
            cb_c = pr.sb("cb_c_E", [P, IBL], es=s2)
            db_c = pr.sb("db_c_E", [P, IBL], es=s2)
            outb_c = pr.sb("outb_c_E", [P, DB], es=s2)
            a_dec = pr.sb("a_dec_E", [P, IBL], es=s2)
            normw_c = pr.sb("normw_c_E", [P, DB], es=s2)
            asp = pr.sb("asp_E", [P, 2 * IBL], es=s2)
            ld(inb_c[:, :], dd["inb_c"][:, :])
            ld(gateb_c[:, :], dd["gateb_c"][:, :])
            ld(bb_c[:, :], dd["bb_c"][:, :])
            ld(cb_c[:, :], dd["cb_c"][:, :])
            ld(db_c[:, :], dd["db_c"][:, :])
            ld(outb_c[:, :], dd["outb_c"][:, :])
            ld(a_dec[:, :], dd["alog_c"][:, :])
            ld(normw_c[:, :], dd["normw_c"][:, :])

            def ss():
                pr.tag("scalar", "act")
                pr.wait("scalar", "act")

            pr.wait("scalar", pr.ldsem)
            pr.emit("scalar", lambda e: e.activation(asp[:, 0:IBL], a_dec[:, :],
                                                     AF.Relu))
            ss()
            pr.emit("scalar", lambda e: e.activation(asp[:, IBL:2 * IBL],
                                                     a_dec[:, :], AF.Abs))
            ss()
            pr.emit("scalar", lambda e: e.activation(
                asp[:, IBL:2 * IBL], asp[:, IBL:2 * IBL], AF.Exp, scale=-1.0))
            ss()
            pr.emit("scalar", lambda e: e.activation(
                asp[:, IBL:2 * IBL], asp[:, IBL:2 * IBL], AF.Ln, bias=1.0),
                sem="act")
            pr.wait("vector", "act")
            pr.emit("vector", lambda e: e.tensor_add(
                asp[:, 0:IBL], asp[:, 0:IBL], asp[:, IBL:2 * IBL]), sem="dve")
            pr.wait("scalar", "dve")
            pr.emit("scalar", lambda e: e.activation(
                a_dec[:, :], asp[:, 0:IBL], AF.Exp, scale=-1.0), sem="act")
            pr.wait("vector", pr.ldsem)
            pr.wait("tensor", pr.ldsem)
            pr.wait("vector", "act")
            pr.wait("tensor", "act")

            s3 = es.enter_context(ExitStack())
            sq = pr.sb("sq_E", [P, 2 * 512], F32, es=s3)
            rowsB = pr.sb("rowsB_E", [1, 4 * T], es=s3)
            rep = pr.sb("rep_E", [P, 3 * T], es=s3)
            tmpb = pr.sb("tmpb_E", [P, T], es=s3)
            sc = dict(sq=sq, rowsB=rowsB, rep=rep, tmpb=tmpb, ones_cf=ones_cf,
                      ones_f=ones_f, normw_c=normw_c, p0=psum[0], p1=psum[1])
            emit_ln_zres(pr, xsb, znb, normw_c, sc)
            pr.barrier("E_zres_done")
            s3.close()
            u_bf = pr.sb("u_bf_E", [P, IBL * T], BF16, es=s2)
            s4 = es.enter_context(ExitStack())
            wstg1 = pr.sb("wstg1_E", [P, 2 * 4 * P], F32, es=s4)
            wcol_in = pr.sb("wcol_in_E", [P, 2 * DB * P], BF16, es=s4)
            wcol_g = pr.sb("wcol_g_E", [P, 2 * DB * P], BF16, es=s4)
            t2 = pr.sb("t2_E", [P, 512], F32, es=s4)
            t3 = pr.sb("t3_E", [P, 512], F32, es=s4)
            w1 = dict(d_inwT=dd["inwT"], d_gatewT=dd["gatewT"], inb_c=inb_c,
                      gateb_c=gateb_c)
            sc1 = dict(wstg=wstg1, wcol_in=wcol_in, wcol_g=wcol_g, t2=t2,
                       t3=t3, pu=psum[0], pg=psum[1])
            emit_l1_attn_e1(pr, znb, u_bf, w1, sc1)
            pr.barrier("E_e1_done")
            s4.close()
            wstg = pr.sb("wstg_E", [P, 2 * 4 * P], es=s2)
            wb = pr.sb("wb_E", [P, IBL * P], BF16, es=s2)
            wc = pr.sb("wc_E", [P, IBL * P], BF16, es=s2)
            wd = pr.sb("wd_E", [P, IBL * P], BF16, es=s2)
            wo = pr.sb("wo_E", [P, 1024], BF16, es=s2)
            bq = pr.sb("bq_E", [P, 1024], es=s2)
            cq = pr.sb("cq_E", [P, 1024], es=s2)
            dq = pr.sb("dq_E", [P, 1024], es=s2)
            hsq = pr.sb("hsq_E", [P, 1024], es=s2)
            a_rep = pr.sb("a_rep_E", [P, 1024], es=s2)
            onesf_t = pr.sb("onesf_t_E", [P, 1024], es=s2)
            carry = pr.sb("carry_E", [P, IBL], es=s2)
            y_bf = pr.sb("y_bf_E", [P, 1024], BF16, es=s2)
            t1b = pr.sb("t1b_E", [P, 512], BF16, es=s2)
            pr.emit("vector", lambda e: e.memset(onesf_t[:, :], 1.0), sem="dve")
            w2 = dict(d_bwT=dd["bwT"], d_cwT=dd["cwT"], d_dwT=dd["dwT"],
                      d_outwT=dd["outwT"], bb_c=bb_c, cb_c=cb_c, db_c=db_c,
                      outb_c=outb_c, a_dec=a_dec)
            sc2 = dict(wstg=wstg, wb=wb, wc=wc, wd=wd, wo=wo, bq=bq, cq=cq,
                       dq=dq, hsq=hsq, a_rep=a_rep, onesf_t=onesf_t,
                       carry=carry, y_bf=y_bf, oacc=znb, t1b=t1b,
                       pb_=psum[0], pc_=psum[1], pd_=psum[2],
                       po_=[psum[3], psum[4]])
            emit_l1_attn_e2(pr, u_bf, xsb, w2, sc2)
            endphase("E_done")
            s2.close()

        pr.scope = "phA"; pr.ldsem = "ld_A"; phase_l0_ffn("A")
        pr.scope = "phB"; pr.ldsem = "ld_B"; phase_l0_attn()
        pr.scope = "phC"; pr.ldsem = "ld_C"; phase_l0_ffn("C")
        pr.scope = "phD"; pr.ldsem = "ld_D"; phase_moe("Dx")
        pr.scope = "phE"; pr.ldsem = "ld_E"; phase_l1_attn()
        pr.scope = "phF"; pr.ldsem = "ld_F"; phase_moe("Fx")
        pr.scope = None

        pr.wait("sync", "dve")
        for b in range(DB):
            pr.emit("sync", lambda e, b=b: e.dma_start(
                out=x_out[b * P:(b + 1) * P, :],
                in_=xsb[:, b * T:(b + 1) * T]), sem="st", amt=16)
        pr.wait("sync", "st")
        pr.finalize()
    return nc


def prep_fused_inputs(inputs):
    m = {}
    m.update(prep_l0_ffn_inputs(inputs))
    m.update(prep_l0_attn_inputs(inputs))
    m.update(prep_moe_inputs(inputs))
    m.update(prep_l1_attn_inputs(inputs))
    return m


def kernel(**inputs):
    x = inputs["L0_attn_x"]
    subs = globals().get("_SUBLAYERS", "FUSED")
    xt = [np.ascontiguousarray(x[c].T.astype(np.float32))
          for c in range(N_CORES)]
    if subs == "FUSED":
        nc = build_fused_program()
        base = prep_fused_inputs(inputs)
        in_maps = []
        for c in range(N_CORES):
            m = dict(base)
            m["x_in"] = xt[c]
            in_maps.append(m)
        res = run_bass_kernel_spmd(nc, in_maps, core_ids=list(range(N_CORES)))
        out = np.stack([res.results[c]["x_out"].T for c in range(N_CORES)])
        return out.astype(np.float32)
    for s in subs:
        if s in "AC":
            nc = build_l0_ffn_program()
            base = prep_l0_ffn_inputs(inputs)
        elif s == "B":
            nc = build_l0_attn_program()
            base = prep_l0_attn_inputs(inputs)
        elif s == "E":
            nc = build_l1_attn_program()
            base = prep_l1_attn_inputs(inputs)
        elif s in "DF":
            nc = build_moe_program()
            base = prep_moe_inputs(inputs)
        else:
            raise ValueError(s)
        in_maps = []
        for c in range(N_CORES):
            m = dict(base)
            m["x_in"] = xt[c]
            in_maps.append(m)
        res = run_bass_kernel_spmd(nc, in_maps, core_ids=list(range(N_CORES)))
        xt = [np.ascontiguousarray(res.results[c]["x_out"])
              for c in range(N_CORES)]
    out = np.stack([xt[c].T for c in range(N_CORES)])
    return out.astype(np.float32)


# version 18
# speedup vs baseline: 1.0351x; 1.0020x over previous
import numpy as np
from contextlib import ExitStack

import concourse.bass as bass
import concourse.mybir as mybir
from concourse.bass_utils import run_bass_kernel_spmd

F32 = mybir.dt.float32
BF16 = mybir.dt.bfloat16
AF = mybir.ActivationFunctionType
ALU = mybir.AluOpType
AX = mybir.AxisListType

P = 128
T = 2048
D = 1024
DB = D // P
NT = T // 512
N_CORES = 8
EPS_LN = 1e-5
EPS_RMS = 1e-6
NEXP = 14
INNER = 2048
IBL = INNER // P
DSTATE = 16
DTRANK = 64

ENGINES = ("tensor", "scalar", "vector", "pool", "sync")


class Prog:

    def __init__(self, nc, es):
        self.nc = nc
        self.es = es
        self.ops = {e: [] for e in ENGINES}
        self._sems = {}
        self.counts = {}
        self.scope = None
        self.ldsem = "ld"

    def sem(self, name):
        if name not in self._sems:
            self._sems[name] = self.es.enter_context(self.nc.semaphore(name))
            self.counts[name] = 0
        return self._sems[name]

    def emit(self, eng, f, sem=None, amt=1):
        if sem is not None:
            h = self.sem(sem)
            self.counts[sem] += amt

            def g(e, f=f, h=h, amt=amt):
                ins = f(e)
                ins.then_inc(h, amt)
                return ins
            self.ops[eng].append((self.scope, g))
            return self.counts[sem]
        self.ops[eng].append((self.scope, f))
        return None

    def tag(self, eng, sem, amt=1):
        h = self.sem(sem)
        self.counts[sem] += amt
        assert self.ops[eng], f"tag on empty stream {eng}"
        lbl, f = self.ops[eng].pop()

        def g(e, f=f, h=h, amt=amt):
            ins = f(e)
            assert ins is not None, "tagged op returned no instruction"
            ins.then_inc(h, amt)
        self.ops[eng].append((lbl, g))
        return self.counts[sem]

    def wait(self, eng, sem, val=None):
        h = self.sem(sem)
        v = self.counts[sem] if val is None else val

        def g(e, h=h, v=v):
            e.wait_ge(h, v)
        self.ops[eng].append((self.scope, g))

    def barrier(self, tag, engines=ENGINES):
        name = f"bar_{tag}"
        for eng in engines:
            self.emit(eng, lambda e: e.nop(), sem=name)
        tgt = self.counts[name]
        for eng in engines:
            self.wait(eng, name, tgt)

    def sb(self, name, shape, dtype=F32, es=None):
        return (es or self.es).enter_context(
            self.nc.sbuf_tensor(name, shape, dtype))

    def ps(self, name, shape=(P, 512)):
        return self.es.enter_context(
            self.nc.psum_tensor(name, list(shape), F32))

    def _replay(self, eng_name, e):
        nc = self.nc
        cur = None
        for lbl, f in self.ops[eng_name]:
            if lbl != (cur[0] if cur else None):
                if cur is not None:
                    nc.leave_named_scope(cur[0], cur[1], False)
                    cur = None
                if lbl is not None:
                    sid, _ = nc.enter_named_scope(lbl, False)
                    cur = (lbl, sid)
            f(e)
        if cur is not None:
            nc.leave_named_scope(cur[0], cur[1], False)

    def finalize(self):
        with self.nc.Block() as block:
            @block.tensor
            def _(e):
                self._replay("tensor", e)

            @block.scalar
            def _(e):
                self._replay("scalar", e)

            @block.vector
            def _(e):
                self._replay("vector", e)

            @block.gpsimd
            def _(e):
                self._replay("pool", e)

            @block.sync
            def _(e):
                self._replay("sync", e)



def _cols(v):
    return np.ascontiguousarray(v.reshape(-1, P).T.astype(np.float32))


def common_consts():
    c = {}
    c["ones1"] = np.ones((1, P), dtype=np.float32)
    c["ones_c"] = np.ones((P, 1), dtype=np.float32)
    c["ident"] = np.eye(P, dtype=np.float32)
    oh = np.zeros((NEXP, NEXP * P), dtype=np.float32)
    for e in range(NEXP):
        oh[e, e * P:(e + 1) * P] = 1.0
    c["onehot14"] = oh
    return c



def emit_ln(pr, xsb, out_bf, out_f32, sc, rms=False, gamma_col=None,
            eps=EPS_LN):
    rep, p0, p1 = sc["rep"], sc["p0"], sc["p1"]
    rowsB = sc["rowsB"]
    xb, sq = sc["xb"], sc["sq"]
    m_r = rowsB[0:1, 0:T]
    v_r = rowsB[0:1, T:2 * T]
    rstd_r = rowsB[0:1, 2 * T:3 * T]
    mrstd_r = rowsB[0:1, 3 * T:4 * T]
    msq_r = mrstd_r

    if xb is not None:
        for b in range(DB):
            pr.emit("vector", lambda e, b=b: e.tensor_copy(
                xb[:, b * T:(b + 1) * T], xsb[:, b * T:(b + 1) * T]))
        pr.tag("vector", "dve")
        pr.wait("scalar", "dve")
        pr.wait("tensor", "dve")
    stat_src = xb if xb is not None else xsb
    stat_ones = sc["ones_cb"] if xb is not None else sc["ones_cf"]

    slot_free_at = [pr.counts.get("pe", 0)] * 2
    for n in range(NT):
        for b in range(DB):
            pr.emit("tensor", lambda e, b=b, n=n: e.matmul(
                p0[0:1, :], stat_ones[:, 0:1],
                stat_src[:, b * T + n * 512: b * T + (n + 1) * 512],
                start=(b == 0), stop=(b == DB - 1)),
                sem="pe" if b == DB - 1 else None)
        for b in range(DB):
            s = b % 2
            pr.wait("scalar", "pe", slot_free_at[s])
            pr.emit("scalar", lambda e, b=b, n=n, s=s: e.activation(
                sq[:, s * 512:(s + 1) * 512],
                stat_src[:, b * T + n * 512: b * T + (n + 1) * 512],
                AF.Square), sem="act")
            pr.wait("tensor", "act")
            slot_free_at[s] = pr.emit("tensor", lambda e, s=s, b=b: e.matmul(
                p1[0:1, :], stat_ones[:, 0:1], sq[:, s * 512:(s + 1) * 512],
                start=(b == 0), stop=(b == DB - 1)), sem="pe")
        pr.wait("vector", "pe")
        pr.emit("vector", lambda e, n=n: e.tensor_copy(
            m_r[:, n * 512:(n + 1) * 512], p0[0:1, :]), sem="dve")
        pr.emit("vector", lambda e, n=n: e.tensor_copy(
            v_r[:, n * 512:(n + 1) * 512], p1[0:1, :]), sem="dve")
        pr.wait("tensor", "dve")

    def _vs():
        pr.tag("vector", "dve")
        pr.wait("vector", "dve")

    pr.wait("vector", "dve")
    if not rms:
        pr.emit("vector", lambda e: e.tensor_scalar(
            m_r, m_r, 1.0 / D, None, op0=ALU.mult))
        _vs()
        pr.emit("vector", lambda e: e.tensor_mul(msq_r, m_r, m_r))
        _vs()
        pr.emit("vector", lambda e: e.tensor_scalar(
            v_r, v_r, 1.0 / D, eps, op0=ALU.mult, op1=ALU.add))
        _vs()
        pr.emit("vector", lambda e: e.tensor_sub(
            v_r, v_r, msq_r), sem="dve")
    else:
        pr.emit("vector", lambda e: e.tensor_scalar(
            v_r, v_r, 1.0 / D, eps, op0=ALU.mult, op1=ALU.add), sem="dve")
    pr.wait("scalar", "dve")
    pr.emit("scalar", lambda e: e.activation(rstd_r, v_r, AF.Ln))
    pr.tag("scalar", "act")
    pr.wait("scalar", "act")
    pr.emit("scalar", lambda e: e.activation(
        rstd_r, rstd_r, AF.Exp, scale=-0.5), sem="act")
    if not rms:
        pr.wait("vector", "act")
        pr.emit("vector", lambda e: e.tensor_mul(
            mrstd_r, m_r, rstd_r), sem="dve")

    pr.wait("tensor", "act")
    if not rms:
        pr.wait("tensor", "dve")
    for n in range(NT):
        pr.emit("tensor", lambda e, n=n: e.matmul(
            p0[:, :], sc["ones_f"][0:1, :], rstd_r[:, n * 512:(n + 1) * 512],
            start=True, stop=True), sem="pe" if rms else None)
        if not rms:
            pr.emit("tensor", lambda e, n=n: e.matmul(
                p1[:, :], sc["ones_f"][0:1, :],
                mrstd_r[:, n * 512:(n + 1) * 512],
                start=True, stop=True), sem="pe")
        pr.wait("vector", "pe")
        pr.emit("vector", lambda e, n=n: e.tensor_copy(
            rep[:, n * 512:(n + 1) * 512], p0[:, :]), sem="dve")
        if not rms:
            pr.emit("vector", lambda e, n=n: e.tensor_copy(
                rep[:, T + n * 512:T + (n + 1) * 512], p1[:, :]), sem="dve")
        pr.wait("tensor", "dve")

    pr.wait("vector", "dve")
    rstd_rep = rep[:, 0:T]
    mrstd_rep = rep[:, T:2 * T]
    for b in range(DB):
        src = xsb[:, b * T:(b + 1) * T]
        if out_f32 is not None:
            dstf = out_f32[:, b * T:(b + 1) * T]
            if not rms:
                pr.emit("vector", lambda e, src=src, dstf=dstf: e.tensor_sub(
                    dstf, src, mrstd_rep))
                _vs()
                pr.emit("vector", lambda e, dstf=dstf: e.tensor_mul(
                    dstf, dstf, rstd_rep))
            else:
                pr.emit("vector", lambda e, src=src, dstf=dstf: e.tensor_mul(
                    dstf, src, rstd_rep))
            if out_bf is not None:
                dstb = out_bf[:, b * T:(b + 1) * T]
                if gamma_col is not None:
                    pr.emit("vector", lambda e, dstf=dstf, dstb=dstb, b=b:
                            e.tensor_scalar(dstb, dstf, gamma_col[:, b:b + 1],
                                            None, op0=ALU.mult))
                else:
                    pr.emit("vector", lambda e, dstf=dstf, dstb=dstb:
                            e.tensor_copy(dstb, dstf))
        else:
            dstb = out_bf[:, b * T:(b + 1) * T]
            if not rms:
                pr.emit("vector", lambda e, src=src, dstb=dstb: e.tensor_sub(
                    dstb, src, mrstd_rep))
                _vs()
                pr.emit("vector", lambda e, dstb=dstb: e.tensor_mul(
                    dstb, dstb, rstd_rep))
            else:
                pr.emit("vector", lambda e, src=src, dstb=dstb: e.tensor_mul(
                    dstb, src, rstd_rep))
            if gamma_col is not None:
                _vs()
                pr.emit("vector", lambda e, dstb=dstb, b=b: e.tensor_scalar(
                    dstb, dstb, gamma_col[:, b:b + 1], None, op0=ALU.mult))
    pr.tag("vector", "dve")



def emit_l0_ffn(pr, xnb, xsb, w, sc):
    s64, g64, relu, gb = sc["s64"], sc["g64"], sc["relu"], sc["gb"]
    tmp = sc["tmp"]
    p0, p1 = sc["p0"], sc["p1"]

    pr.wait("tensor", "dve")
    for n in range(NT):
        for b in range(DB):
            pr.emit("tensor", lambda e, b=b, n=n: e.matmul(
                p0[0:64, :], w["WinT"][:, b * 64:(b + 1) * 64],
                xnb[:, b * T + n * 512: b * T + (n + 1) * 512],
                start=(b == 0), stop=(b == DB - 1)),
                sem="pe" if b == DB - 1 else None)
        pr.wait("scalar", "pe")
        pr.emit("scalar", lambda e, n=n: e.activation(
            s64[0:64, n * 512:(n + 1) * 512], p0[0:64, :], AF.Identity,
            bias=w["binc"][0:64, 0:1]), sem="act")
        pr.wait("tensor", "act")

    dup1 = pr.ldsem + "_dup1"
    pr.wait("sync", "act")
    pr.emit("sync", lambda e: e.dma_start(out=s64[64:128, :], in_=s64[0:64, :]),
            sem=dup1, amt=16)
    pr.wait("scalar", dup1)
    pr.wait("vector", dup1)

    for hp in range(8):
        pr.emit("scalar", lambda e, hp=hp: e.activation(
            relu[:, :], s64[:, :], AF.Relu,
            scale=w["W1c"][:, hp:hp + 1], bias=w["b1c"][:, hp:hp + 1]),
            sem="act")
        pr.wait("vector", "act")
        if hp == 0:
            pr.emit("vector", lambda e: e.tensor_scalar(
                g64[:, :], relu[:, :], w["W2c"][:, 0:1], None, op0=ALU.mult),
                sem="dve")
        else:
            pr.emit("vector", lambda e, hp=hp: e.scalar_tensor_tensor(
                g64[:, :], relu[:, :], w["W2c"][:, hp:hp + 1], g64[:, :],
                op0=ALU.mult, op1=ALU.add), sem="dve")
        pr.wait("vector", "dve")
        pr.wait("scalar", "dve")
    dup2 = pr.ldsem + "_dup2"
    pr.wait("sync", "dve")
    pr.emit("sync", lambda e: e.dma_start(out=relu[0:64, :], in_=g64[64:128, :]),
            sem=dup2, amt=16)
    pr.wait("vector", dup2)
    pr.emit("vector", lambda e: e.tensor_add(
        g64[0:64, :], g64[0:64, :], relu[0:64, :]))
    pr.tag("vector", "dve")
    pr.wait("vector", "dve")
    pr.emit("vector", lambda e: e.tensor_scalar(
        gb[0:64, :], g64[0:64, :], w["b2c"][0:64, 0:1], None, op0=ALU.add),
        sem="dve")

    pr.wait("tensor", "dve")
    for b in range(DB):
        for n in range(NT):
            pr.emit("tensor", lambda e, b=b, n=n: e.matmul(
                p1[:, :], w["WoutT"][0:64, b * P:(b + 1) * P],
                gb[0:64, n * 512:(n + 1) * 512], start=True, stop=True),
                sem="pe")
            pr.wait("scalar", "pe")
            pr.emit("scalar", lambda e, b=b: e.activation(
                tmp[:, 0:512], p1[:, :], AF.Identity, scale=0.5,
                bias=w["bout_half"][:, b:b + 1]), sem="act")
            pr.wait("vector", "act")
            dst = xsb[:, b * T + n * 512: b * T + (n + 1) * 512]
            pr.emit("vector", lambda e, dst=dst: e.tensor_add(
                dst, dst, tmp[:, 0:512]), sem="dve")
            pr.wait("tensor", "dve")
            pr.wait("scalar", "dve")



def emit_router(pr, xsb, sc, r):
    rowsB = sc["rowsB"]
    m_r = rowsB[0:1, 0:T]
    rstd_r = rowsB[0:1, 2 * T:3 * T]
    p2, p3 = r["p2"], r["p3"]
    lgs, lgT = r["lgs"], r["lgT"]
    red, mt, cums = r["red"], r["mt"], r["cums"]

    pr.wait("tensor", "dve")
    for b in range(DB):
        pr.emit("tensor", lambda e, b=b: e.matmul(
            p2[0:NEXP, 0:1], r["rwT"][:, b * NEXP:(b + 1) * NEXP],
            sc["ones_cf"][:, 0:1], start=(b == 0), stop=(b == DB - 1)),
            sem="pe" if b == DB - 1 else None)
    pr.wait("vector", "pe")
    pr.emit("vector", lambda e: e.tensor_scalar(
        r["colsum_n"][:, 0:1], p2[0:NEXP, 0:1], -1.0, None, op0=ALU.mult),
        sem="dve")
    pr.wait("tensor", "dve")

    for n in range(NT):
        sl = slice(n * 512, (n + 1) * 512)
        for b in range(DB):
            pr.emit("tensor", lambda e, b=b, n=n: e.matmul(
                p2[0:NEXP, :], r["rwT"][:, b * NEXP:(b + 1) * NEXP],
                xsb[:, b * T + n * 512: b * T + (n + 1) * 512],
                start=(b == 0), stop=(b == DB - 1)))
        pr.emit("tensor", lambda e, n=n: e.matmul(
            p3[0:NEXP, :], r["ones_f"][0:1, 0:NEXP],
            m_r[:, n * 512:(n + 1) * 512], start=True, stop=True), sem="pe")
        pr.wait("vector", "pe")
        pr.emit("vector", lambda e, sl=sl: e.tensor_copy(
            lgs[0:NEXP, sl], p2[0:NEXP, :]))
        pr.tag("vector", "dve")
        pr.wait("vector", "dve")
        pr.emit("vector", lambda e, sl=sl: e.scalar_tensor_tensor(
            lgs[0:NEXP, sl], p3[0:NEXP, :], r["colsum_n"][0:NEXP, 0:1],
            lgs[0:NEXP, sl], op0=ALU.mult, op1=ALU.add), sem="dve")
        pr.wait("tensor", "dve")
        pr.emit("tensor", lambda e, n=n: e.matmul(
            p3[0:NEXP, :], r["ones_f"][0:1, 0:NEXP],
            rstd_r[:, n * 512:(n + 1) * 512], start=True, stop=True), sem="pe")
        pr.wait("vector", "pe")
        pr.wait("vector", "dve")
        pr.emit("vector", lambda e, sl=sl: e.tensor_tensor(
            lgs[0:NEXP, sl], lgs[0:NEXP, sl], p3[0:NEXP, :], op=ALU.mult),
            sem="dve")
        pr.wait("tensor", "dve")

    for i in range(16):
        pr.emit("tensor", lambda e, i=i: e.transpose(
            p2[:, 0:NEXP], lgs[0:NEXP, i * P:(i + 1) * P],
            r["identf"][0:NEXP, 0:NEXP]), sem="pe")
        pr.wait("vector", "pe")
        pr.emit("vector", lambda e, i=i: e.tensor_copy(
            lgT[:, i * NEXP:(i + 1) * NEXP], p2[:, 0:NEXP]), sem="dve")
        pr.wait("tensor", "dve")

    def vsync():
        pr.tag("vector", "dve")
        pr.wait("vector", "dve")

    for i in range(16):
        tile = lgT[:, i * NEXP:(i + 1) * NEXP]
        pr.emit("vector", lambda e, tile=tile: e.reduce_max(
            red[:, 0:1], tile, axis=AX.X))
        vsync()
        pr.emit("vector", lambda e, tile=tile: e.tensor_scalar(
            mt[:, :], tile, red[:, 0:1], None, op0=ALU.is_ge))
        vsync()
        pr.emit("vector", lambda e: e.tensor_tensor_scan(
            cums[:, :], r["ones14"][:, :], mt[:, :], 0.0,
            op0=ALU.mult, op1=ALU.add))
        vsync()
        pr.emit("vector", lambda e: e.tensor_scalar(
            cums[:, :], cums[:, :], 1.0, None, op0=ALU.is_le))
        vsync()
        pr.emit("vector", lambda e: e.tensor_tensor(
            mt[:, :], mt[:, :], cums[:, :], op=ALU.mult))
        vsync()
        pr.emit("vector", lambda e, tile=tile: e.tensor_scalar(
            cums[:, :], tile, red[:, 0:1], None, op0=ALU.subtract),
            sem="dve")
        pr.wait("scalar", "dve")
        pr.emit("scalar", lambda e: e.activation(
            cums[:, :], cums[:, :], AF.Exp), sem="act")
        pr.wait("vector", "act")
        pr.emit("vector", lambda e: e.reduce_sum(
            red[:, 1:2], cums[:, :], axis=AX.X))
        vsync()
        pr.emit("vector", lambda e: e.reciprocal(red[:, 2:3], red[:, 1:2]))
        vsync()
        pr.emit("vector", lambda e: e.tensor_scalar(
            red[:, 2:3], red[:, 2:3], 0.5, None, op0=ALU.mult))
        vsync()
        pr.emit("vector", lambda e: e.tensor_scalar(
            mt[:, :], mt[:, :], red[:, 2:3], None, op0=ALU.mult), sem="dve")
        pr.wait("tensor", "dve")
        pr.emit("tensor", lambda e: e.transpose(
            p2[0:NEXP, 0:P], mt[:, :], r["identf"][:, :]), sem="pe")
        pr.wait("vector", "pe")
        pr.emit("vector", lambda e, i=i: e.tensor_copy(
            r["msel"][0:NEXP, i * P:(i + 1) * P], p2[0:NEXP, 0:P]), sem="dve")
        pr.wait("tensor", "dve")
    pr.emit("vector", lambda e: e.tensor_copy(
        r["msel_b"][0:NEXP, :], r["msel"][0:NEXP, :]), sem="dve")


def emit_moe_experts(pr, xnb, xsb, m, dram, cache=None):
    w1s, w2s = m["w1s"], m["w2s"]
    w1f = m["w1f"]
    hid, t1 = m["hid"], m["t1"]
    ph = [m["ph0"], m["ph1"], m["p0"], m["p1"]]
    py = [m["py0"], m["py1"], m["p2b"]]

    last_w1_mm = 0
    last_w2_mm = 0
    cast_done = [0, 0]
    msel_slot_free = [0, 0]

    if cache == "in":
        pr.wait("sync", "cwb")
    pr.emit("sync", lambda e: e.dma_start(
        out=m["b1c"][:, :].rearrange("p (x k) -> p x k", k=16),
        in_=dram["d_b1c"].rearrange("x p k -> p x k")), sem="wld", amt=16)
    pr.emit("sync", lambda e: e.dma_start(
        out=m["b2c"][:, :].rearrange("p (x k) -> p x k", k=8),
        in_=dram["d_b2c"].rearrange("x p k -> p x k")), sem="wld", amt=16)
    pr.wait("scalar", "wld")
    pr.wait("vector", "wld")
    pr.wait("tensor", "wld")

    for ex in range(NEXP):
        chunks = ([("w1", kc, h) for kc in range(8) for h in range(2)] +
                  [("w2", kc, 0) for kc in range(16)])
        if cache == "in":
            pr.wait("sync", "pe", last_w1_mm)
            for kc in range(8):
                for h in range(2):
                    pr.emit("sync", lambda e, ex=ex, kc=kc, h=h: e.dma_start(
                        out=w1s[:, kc * T + h * 1024: kc * T + (h + 1) * 1024],
                        in_=dram["d_w1b"][ex, kc * P:(kc + 1) * P,
                                          h * 1024:(h + 1) * 1024]),
                        sem="fwld", amt=16)
            pr.wait("sync", "pe", last_w2_mm)
            for kc in range(16):
                pr.emit("sync", lambda e, ex=ex, kc=kc: e.dma_start(
                    out=w2s[:, kc * 1024:(kc + 1) * 1024],
                    in_=dram["d_w2b"][ex, kc * P:(kc + 1) * P, :]),
                    sem="fwld", amt=16)
            chunks = []
        for i, (which, kc, h) in enumerate(chunks):
            s = i % 2
            wsem = f"wld{s}"
            csem = f"wcast{s}"
            pr.wait("sync", csem, cast_done[s])
            if which == "w1":
                pr.emit("sync", lambda e, ex=ex, kc=kc, h=h, s=s: e.dma_start(
                    out=w1f[:, s * 1024:(s + 1) * 1024],
                    in_=dram["d_w1"][ex, kc * P:(kc + 1) * P,
                                     h * 1024:(h + 1) * 1024]),
                    sem=wsem, amt=16)
                dst = w1s[:, kc * T + h * 1024: kc * T + (h + 1) * 1024]
                gate = last_w1_mm
            else:
                pr.emit("sync", lambda e, ex=ex, kc=kc, s=s: e.dma_start(
                    out=w1f[:, s * 1024:(s + 1) * 1024],
                    in_=dram["d_w2"][ex, kc * P:(kc + 1) * P, :]),
                    sem=wsem, amt=16)
                dst = w2s[:, kc * 1024:(kc + 1) * 1024]
                gate = last_w2_mm
            ceng = "scalar" if s == 0 else "vector"
            pr.wait(ceng, wsem)
            pr.wait(ceng, "pe", gate)
            if ceng == "scalar":
                cast_done[s] = pr.emit(
                    ceng, lambda e, dst=dst, s=s: e.activation(
                        dst, w1f[:, s * 1024:(s + 1) * 1024], AF.Copy),
                    sem=csem)
            else:
                cast_done[s] = pr.emit(
                    ceng, lambda e, dst=dst, s=s: e.tensor_copy(
                        dst, w1f[:, s * 1024:(s + 1) * 1024]), sem=csem)
            if cache == "out":
                if which == "w1":
                    cdst = dram["d_w1b"][ex, kc * P:(kc + 1) * P,
                                         h * 1024:(h + 1) * 1024]
                else:
                    cdst = dram["d_w2b"][ex, kc * P:(kc + 1) * P, :]
                pr.wait("pool", csem)
                pr.emit("pool", lambda e, cdst=cdst, dst=dst: e.dma_start(
                    out=cdst, in_=dst), sem="cwb", amt=16)
        b1col = m["b1c"][:, ex * 16:(ex + 1) * 16]
        b2col = m["b2c"][:, ex * 8:(ex + 1) * 8]
        if cache == "in":
            pr.wait("tensor", "fwld")
        else:
            pr.wait("tensor", "wcast0")
            pr.wait("tensor", "wcast1")
        mslot = ex % 2
        mv = m["mselr"][:, mslot * T:(mslot + 1) * T]
        pr.wait("tensor", "yacc", msel_slot_free[mslot])
        for n4 in range(NT):
            pr.emit("tensor", lambda e, ex=ex, n4=n4: e.matmul(
                m["p3b"][:, :], m["ohb"][0:NEXP, ex * P:(ex + 1) * P],
                m["msel_b"][0:NEXP, n4 * 512:(n4 + 1) * 512],
                start=True, stop=True), sem="pe")
            pr.wait("scalar", "pe")
            pr.emit("scalar", lambda e, n4=n4, mv=mv: e.activation(
                mv[:, n4 * 512:(n4 + 1) * 512], m["p3b"][:, :],
                AF.Identity), sem="mselev")
            pr.wait("tensor", "mselev")

        w1_mm_mark = 0
        for tq in range(NT):
            for mm in range(16):
                s = mm % 4
                pr.wait("tensor", "hevac",
                        max(0, pr.counts.get("hevac", 0) - 3))
                for kc in range(8):
                    pr.emit("tensor",
                            lambda e, mm=mm, kc=kc, tq=tq, s=s: e.matmul(
                                ph[s][:, :],
                                w1s[:, kc * T + mm * P: kc * T + (mm + 1) * P],
                                xnb[:, kc * T + tq * 512:
                                    kc * T + (tq + 1) * 512],
                                start=(kc == 0), stop=(kc == 7)),
                            sem="pe" if kc == 7 else None)
                pr.wait("scalar", "pe")
                pr.emit("scalar", lambda e, mm=mm, s=s, b1col=b1col:
                        e.activation(
                    hid[:, mm * 512:(mm + 1) * 512], ph[s][:, :], AF.Gelu,
                    bias=b1col[:, mm:mm + 1]), sem="hevac")
            if tq == NT - 1:
                w1_mm_mark = pr.counts.get("pe", 0)
            pr.wait("tensor", "hevac")
            for mo in range(8):
                s = mo % 3
                pr.wait("tensor", "moev", max(0, pr.counts.get("moev", 0) - 2))
                for kc in range(16):
                    pr.emit("tensor", lambda e, mo=mo, kc=kc, s=s: e.matmul(
                        py[s][:, :],
                        w2s[:, kc * 1024 + mo * P: kc * 1024 + (mo + 1) * P],
                        hid[:, kc * 512:(kc + 1) * 512],
                        start=(kc == 0), stop=(kc == 15)),
                        sem="pe" if kc == 15 else None)
                pr.wait("scalar", "pe")
                pr.wait("scalar", "yacc",
                        max(0, pr.counts.get("yacc", 0) - 2))
                pr.emit("scalar", lambda e, mo=mo, s=s, b2col=b2col:
                        e.activation(
                    t1[:, s * 512:(s + 1) * 512], py[s][:, :], AF.Identity,
                    bias=b2col[:, mo:mo + 1]), sem="moev")
                pr.wait("vector", "moev")
                dst = xsb[:, mo * T + tq * 512: mo * T + (tq + 1) * 512]
                pr.emit("vector", lambda e, tq=tq, s=s, mv=mv: e.tensor_mul(
                    t1[:, s * 512:(s + 1) * 512], t1[:, s * 512:(s + 1) * 512],
                    mv[:, tq * 512:(tq + 1) * 512]))
                pr.tag("vector", "t1ch")
                pr.wait("vector", "t1ch")
                pr.emit("vector", lambda e, dst=dst, s=s: e.tensor_add(
                    dst, dst, t1[:, s * 512:(s + 1) * 512]), sem="yacc")
        last_w1_mm = w1_mm_mark
        last_w2_mm = pr.counts.get("pe", 0)
        msel_slot_free[ex % 2] = pr.counts.get("yacc", 0)



def emit_l0_attn(pr, xsb, xn, w, sc, chunks=NT, chunk_cb=None):
    p0, p1, p2, p3 = sc["p0"], sc["p1"], sc["p2"], sc["p3"]
    pB, pC = sc["pB"], sc["pC"]
    xnb_c, dtF = sc["xnb_c"], sc["dtF"]
    dtlo, Bsb, Csb = sc["dtlo"], sc["Bsb"], sc["Csb"]
    h_state = sc["h_state"]
    sp1, sp2 = sc["sp1"], sc["sp2"]
    a_t, u2_t, hs_t = sc["a_t"], sc["u2_t"], sc["hs_t"]

    def vsync():
        pr.tag("vector", "dve")
        pr.wait("vector", "dve")

    def ssync():
        pr.tag("scalar", "act")
        pr.wait("scalar", "act")

    pr.emit("vector", lambda e: e.memset(h_state[:, :], 0.0), sem="dve")
    sc["last_scan_dve"] = [0]
    sc["pladd_hist"] = [[0, 0]]
    sc["tile_idx"] = [0]
    yt_t = sc["yt_t"]

    import os as _os0
    _skip_all = int(_os0.environ.get("L0A_SKIP_ALL_FROM", "99"))
    for c in range(chunks):
        t0 = c * 512
        if int(_os0.environ.get("L0A_BARRIER", "1")):
            pr.barrier(f"l0a_chunk{c}_{pr.counts.get('dve', 0)}")
        if c >= _skip_all:
            continue
        pr.wait("vector", "dve")
        for b in range(DB):
            pr.emit("vector", lambda e, b=b, t0=t0: e.tensor_copy(
                xnb_c[:, b * 512:(b + 1) * 512],
                xn[:, b * T + t0: b * T + t0 + 512]))
        pr.tag("vector", "dve")

        pr.wait("tensor", "dve")
        for b in range(DB):
            pr.emit("tensor", lambda e, b=b: e.matmul(
                p0[0:64, :], w["WxdtT"][:, b * 64:(b + 1) * 64],
                xnb_c[:, b * 512:(b + 1) * 512],
                start=(b == 0), stop=(b == DB - 1)))
        for b in range(DB):
            pr.emit("tensor", lambda e, b=b: e.matmul(
                p1[0:16, :], w["WxBT"][:, b * 16:(b + 1) * 16],
                xnb_c[:, b * 512:(b + 1) * 512],
                start=(b == 0), stop=(b == DB - 1)))
        for b in range(DB):
            pr.emit("tensor", lambda e, b=b: e.matmul(
                p2[0:16, :], w["WxCT"][:, b * 16:(b + 1) * 16],
                xnb_c[:, b * 512:(b + 1) * 512],
                start=(b == 0), stop=(b == DB - 1)), sem="pe" if b == DB - 1 else None)
        pr.wait("scalar", "pe")
        pr.emit("scalar", lambda e: e.activation(dtlo[0:64, :], p0[0:64, :],
                                                 AF.Identity), sem="act")
        pr.wait("vector", "pe")
        pr.emit("vector", lambda e: e.tensor_copy(Bsb[0:16, :], p1[0:16, :]))
        pr.emit("vector", lambda e: e.tensor_copy(Csb[0:16, :], p2[0:16, :]),
                sem="dve")
        pr.wait("tensor", "dve")
        pr.wait("tensor", "act")

        for mc in range(DB):
            pr.emit("tensor", lambda e, mc=mc: e.matmul(
                p3[:, :], w["WdtT"][0:64, mc * P:(mc + 1) * P], dtlo[0:64, :],
                start=True, stop=True), sem="pe")
            pr.wait("scalar", "pe")
            dst = dtF[:, mc * 512:(mc + 1) * 512]
            pr.emit("scalar", lambda e, mc=mc: e.activation(
                sp1[:, :], p3[:, :], AF.Identity,
                bias=w["bdt_c"][:, mc:mc + 1]))
            ssync()
            pr.emit("scalar", lambda e, dst=dst: e.activation(
                dst, sp1[:, :], AF.Relu))
            ssync()
            pr.emit("scalar", lambda e: e.activation(sp2[:, :], sp1[:, :],
                                                     AF.Abs))
            ssync()
            pr.emit("scalar", lambda e: e.activation(sp2[:, :], sp2[:, :],
                                                     AF.Exp, scale=-1.0))
            ssync()
            pr.emit("scalar", lambda e: e.activation(sp2[:, :], sp2[:, :],
                                                     AF.Ln, bias=1.0),
                    sem="act")
            pr.wait("vector", "act")
            pr.emit("vector", lambda e, dst=dst: e.tensor_add(
                dst, dst, sp2[:, :]), sem="dve")
            pr.wait("tensor", "dve")
            pr.wait("scalar", "dve")

        import os as _os
        _skip_d = int(_os.environ.get("L0A_SKIP_DSKIP_FROM", "99"))
        pr.wait("vector", "dve")
        for b in range(DB if c < _skip_d else 0):
            xsl = xsb[:, b * T + t0: b * T + t0 + 512]
            nsl = xn[:, b * T + t0: b * T + t0 + 512]
            pr.emit("vector", lambda e, xsl=xsl, nsl=nsl, b=b:
                    e.scalar_tensor_tensor(xsl, nsl, w["D_c"][:, b:b + 1],
                                           xsl, op0=ALU.mult, op1=ALU.add))
        if c < _skip_d:
            vsync()
        for b in range(DB if c < _skip_d else 0):
            nsl = xn[:, b * T + t0: b * T + t0 + 512]
            pr.emit("vector", lambda e, nsl=nsl, b=b: e.tensor_mul(
                nsl, dtF[:, b * 512:(b + 1) * 512], nsl))
        if c < _skip_d:
            vsync()

        brow = sc["brow"]
        pr.wait("sync", "dve")
        import os
        _skip_n = int(os.environ.get("L0A_SKIP_NLOOP_FROM", "99"))
        _nbar = int(os.environ.get("L0A_NBARRIER", "0"))
        for n in range(DSTATE if c < _skip_n else 0):
            if _nbar:
                pr.barrier(f"l0a_nb_{c}_{n}_{pr.counts.get('dve', 0)}")
            s = n % 2
            bsem = f"browld{s}"
            pr.wait("sync", "brow_free", pr.counts.get("brow_free", 0) - 1
                    if pr.counts.get("brow_free", 0) >= 2 else 0)
            pr.emit("sync", lambda e, n=n, s=s: e.dma_start(
                out=brow[0:1, s * 1024: s * 1024 + 512], in_=Bsb[n:n + 1, :]),
                sem=bsem, amt=16)
            pr.emit("sync", lambda e, n=n, s=s: e.dma_start(
                out=brow[0:1, s * 1024 + 512: s * 1024 + 1024],
                in_=Csb[n:n + 1, :]), sem=bsem, amt=16)
            pr.wait("tensor", "dve")
            pr.wait("tensor", "bc_free", pr.counts.get("bc_free", 0))
            pr.wait("tensor", bsem)
            pr.emit("tensor", lambda e, s=s: e.matmul(
                pB[:, :], sc["ones_f"][0:1, :],
                brow[0:1, s * 1024: s * 1024 + 512],
                start=True, stop=True))
            pr.emit("tensor", lambda e, s=s: e.matmul(
                pC[:, :], sc["ones_f"][0:1, :],
                brow[0:1, s * 1024 + 512: s * 1024 + 1024],
                start=True, stop=True), sem="brow_free")
            pr.wait("vector", "brow_free")
            for b in range(DB):
                col = b * 16 + n
                usl = xn[:, b * T + t0: b * T + t0 + 512]
                xsl = xsb[:, b * T + t0: b * T + t0 + 512]
                kidx = sc["tile_idx"][0]
                yslot = kidx % 2
                ysl = slice(yslot * 512, (yslot + 1) * 512)
                pr.wait("scalar", "dve", sc["last_scan_dve"][0])
                pr.emit("scalar", lambda e, b=b, col=col: e.activation(
                    a_t[:, :], dtF[:, b * 512:(b + 1) * 512], AF.Exp,
                    scale=w["A_cols"][:, col:col + 1]), sem="act")
                pr.emit("vector", lambda e, usl=usl: e.tensor_tensor(
                    u2_t[:, :], usl, pB[:, :], op=ALU.mult))
                vsync()
                pr.wait("vector", "act")
                pr.emit("vector", lambda e, col=col: e.tensor_tensor_scan(
                    hs_t[:, :], a_t[:, :], u2_t[:, :],
                    h_state[:, col:col + 1], op0=ALU.mult, op1=ALU.add))
                vsync()
                sc["last_scan_dve"][0] = pr.counts["dve"]
                pr.emit("vector", lambda e, col=col: e.tensor_copy(
                    h_state[:, col:col + 1], hs_t[:, 511:512]))
                pr.wait("vector", "pladd", sc["pladd_hist"][0][yslot])
                pr.emit("vector", lambda e, ysl=ysl: e.tensor_tensor(
                    yt_t[:, ysl], hs_t[:, :], pC[:, :], op=ALU.mult),
                    sem="bc_free" if b == DB - 1 else "dve")
                if b == DB - 1:
                    pr.wait("vector", "bc_free")
                else:
                    pr.wait("vector", "dve")
                pr.wait("pool", "bc_free" if b == DB - 1 else "dve",
                        pr.counts["bc_free" if b == DB - 1 else "dve"])
                sc["pladd_hist"][0][yslot] = pr.emit(
                    "pool", lambda e, xsl=xsl, ysl=ysl: e.tensor_add(
                        xsl, xsl, yt_t[:, ysl]), sem="pladd")
                sc["tile_idx"][0] += 1
                td = sc.get("tile_dump")
                if td is not None and td[:3] == (c, n, b):
                    bufs = td[3]
                    pr.emit("vector", lambda e: e.tensor_copy(
                        bufs["hs"][:, :], hs_t[:, :]))
                    pr.emit("vector", lambda e: e.tensor_copy(
                        bufs["pb"][:, :], pB[:, :]))
                    pr.emit("vector", lambda e: e.tensor_copy(
                        bufs["pc"][:, :], pC[:, :]), sem="dve")
        if chunk_cb is not None:
            chunk_cb(c)



def emit_l1_attn_e1(pr, znb, u_bf, w, sc):
    wstg = sc["wstg"]
    t2 = sc["t2"]
    pu, pg = sc["pu"], sc["pg"]
    wcin, wcg = sc["wcol_in"], sc["wcol_g"]
    pe_hist = []

    cast_done1 = [0, 0]
    for m in range(IBL):
        s = m % 2
        sl = slice(s * 8 * P, (s + 1) * 8 * P)
        slot_gate = pe_hist[m - 2] if m >= 2 else 0
        jobs = [(w["d_inwT"], wcin, 0), (w["d_gatewT"], wcg, 0),
                (w["d_inwT"], wcin, 1), (w["d_gatewT"], wcg, 1)]
        for j, (dsrc, dmat, half) in enumerate(jobs):
            q = j % 2
            wsem = f"wlde1_{q}"
            csem = f"wcste1_{q}"
            pr.wait("sync", csem, cast_done1[q])
            pr.emit("sync", lambda e, m=m, half=half, q=q, dsrc=dsrc:
                    e.dma_start(
                        out=wstg[:, q * 4 * P:(q + 1) * 4 * P],
                        in_=dsrc[half * 512:(half + 1) * 512,
                                 m * P:(m + 1) * P].rearrange(
                            "(c p) k -> p c k", p=P)), sem=wsem, amt=16)
            pr.wait("pool", wsem)
            pr.wait("pool", "pe", slot_gate)
            cast_done1[q] = pr.emit(
                "pool", lambda e, half=half, sl=sl, q=q, dmat=dmat:
                e.tensor_copy(
                    dmat[:, sl][:, half * 4 * P:(half + 1) * 4 * P],
                    wstg[:, q * 4 * P:(q + 1) * 4 * P]), sem=csem)
        pr.wait("tensor", "wcste1_0")
        pr.wait("tensor", "wcste1_1")
        for tq in range(NT):
            for kc in range(DB):
                pr.emit("tensor", lambda e, kc=kc, tq=tq, sl=sl: e.matmul(
                    pu[:, :], wcin[:, sl][:, kc * P:(kc + 1) * P],
                    znb[:, kc * T + tq * 512: kc * T + (tq + 1) * 512],
                    start=(kc == 0), stop=(kc == DB - 1)))
            for kc in range(DB):
                pr.emit("tensor", lambda e, kc=kc, tq=tq, sl=sl: e.matmul(
                    pg[:, :], wcg[:, sl][:, kc * P:(kc + 1) * P],
                    znb[:, kc * T + tq * 512: kc * T + (tq + 1) * 512],
                    start=(kc == 0), stop=(kc == DB - 1)),
                    sem="pe" if kc == DB - 1 else None)
            pr.wait("scalar", "pe")
            pr.emit("scalar", lambda e, m=m: e.activation(
                t2[:, :], pg[:, :], AF.Sigmoid,
                bias=w["gateb_c"][:, m:m + 1]), sem="act")
            pr.wait("vector", "act")
            dst = u_bf[:, m * T + tq * 512: m * T + (tq + 1) * 512]
            pr.emit("vector", lambda e, m=m: e.tensor_scalar(
                t2[:, :] if False else sc["t3"][:, :], pu[:, :],
                w["inb_c"][:, m:m + 1], None, op0=ALU.add))
            pr.tag("vector", "dve")
            pr.wait("vector", "dve")
            pr.emit("vector", lambda e, dst=dst: e.tensor_mul(
                dst, sc["t3"][:, :], t2[:, :]), sem="dve")
            pr.wait("tensor", "dve")
            pr.wait("scalar", "dve")
        pe_hist.append(pr.counts.get("pe", 0))


def emit_l1_attn_e2(pr, u_bf, xsb, w, sc):
    wstg = sc["wstg"]
    wb, wc, wd = sc["wb"], sc["wc"], sc["wd"]
    wo = sc["wo"]
    bq, cq, dq = sc["bq"], sc["cq"], sc["dq"]
    hsq = sc["hsq"]
    a_rep = sc["a_rep"]
    onesf_t = sc["onesf_t"]
    carry = sc["carry"]
    y_bf = sc["y_bf"]
    oacc = sc["oacc"]
    t1b = sc["t1b"]
    pb_, pc_, pd_, po_ = sc["pb_"], sc["pc_"], sc["pd_"], sc["po_"]

    def vsync():
        pr.tag("vector", "dve")
        pr.wait("vector", "dve")

    pr.emit("vector", lambda e: e.memset(oacc[:, :], 0.0))
    pr.emit("vector", lambda e: e.memset(carry[:, :], 0.0), sem="dve")
    pe_hist2 = []
    cast_done2 = [0, 0]

    for ib in range(IBL):
        slot_gate = pe_hist2[ib - 1] if ib >= 1 else 0
        jobs2 = ([(dn, dst, q, False) for dn, dst in
                  (("d_bwT", wb), ("d_cwT", wc), ("d_dwT", wd))
                  for q in range(4)] +
                 [("d_outwT", wo, q, True) for q in range(2)])
        for j, (dname, dst, q, is_out) in enumerate(jobs2):
            sl2 = j % 2
            wsem = f"wlde2_{sl2}"
            csem = f"wcste2_{sl2}"
            pr.wait("sync", csem, cast_done2[sl2])
            if is_out:
                pr.emit("sync", lambda e, ib=ib, q=q, sl2=sl2: e.dma_start(
                    out=wstg[:, sl2 * 4 * P:(sl2 + 1) * 4 * P],
                    in_=w["d_outwT"][ib * P:(ib + 1) * P,
                                     q * 512:(q + 1) * 512].rearrange(
                        "p (c k) -> p c k", k=P)), sem=wsem, amt=16)
                tgt = wo[:, q * 512:(q + 1) * 512]
            else:
                pr.emit("sync", lambda e, ib=ib, q=q, sl2=sl2, dname=dname:
                        e.dma_start(
                            out=wstg[:, sl2 * 4 * P:(sl2 + 1) * 4 * P],
                            in_=w[dname][q * 512:(q + 1) * 512,
                                         ib * P:(ib + 1) * P].rearrange(
                                "(c p) k -> p c k", p=P)), sem=wsem, amt=16)
                tgt = dst[:, q * 4 * P:(q + 1) * 4 * P]
            pr.wait("pool", wsem)
            pr.wait("pool", "pe", slot_gate)
            cast_done2[sl2] = pr.emit(
                "pool", lambda e, tgt=tgt, sl2=sl2: e.tensor_copy(
                    tgt, wstg[:, sl2 * 4 * P:(sl2 + 1) * 4 * P]), sem=csem)
        pr.wait("tensor", "wcste2_0")
        pr.wait("tensor", "wcste2_1")
        pr.wait("vector", "dve")
        pr.emit("vector", lambda e, ib=ib: e.tensor_scalar(
            a_rep[:, :], onesf_t[:, :], w["a_dec"][:, ib:ib + 1], None,
            op0=ALU.mult), sem="dve")

        for th in range(2):
            tsl = slice(th * 1024, (th + 1) * 1024)
            for nn in range(2):
                n0 = th * 1024 + nn * 512
                for kc in range(IBL):
                    pr.emit("tensor", lambda e, kc=kc, n0=n0, ib=ib: e.matmul(
                        pb_[:, :], wb[:, kc * P:(kc + 1) * P],
                        u_bf[:, kc * T + n0: kc * T + n0 + 512],
                        start=(kc == 0), stop=(kc == IBL - 1)))
                for kc in range(IBL):
                    pr.emit("tensor", lambda e, kc=kc, n0=n0, ib=ib: e.matmul(
                        pc_[:, :], wc[:, kc * P:(kc + 1) * P],
                        u_bf[:, kc * T + n0: kc * T + n0 + 512],
                        start=(kc == 0), stop=(kc == IBL - 1)))
                for kc in range(IBL):
                    pr.emit("tensor", lambda e, kc=kc, n0=n0, ib=ib: e.matmul(
                        pd_[:, :], wd[:, kc * P:(kc + 1) * P],
                        u_bf[:, kc * T + n0: kc * T + n0 + 512],
                        start=(kc == 0), stop=(kc == IBL - 1)),
                        sem="pe" if kc == IBL - 1 else None)
                pr.wait("scalar", "pe")
                pr.wait("scalar", "ybf", pr.counts.get("ybf", 0))
                nsl = slice(nn * 512, (nn + 1) * 512)
                pr.emit("scalar", lambda e, nsl=nsl, ib=ib: e.activation(
                    bq[:, nsl], pb_[:, :], AF.Identity,
                    bias=w["bb_c"][:, ib:ib + 1]))
                pr.emit("scalar", lambda e, nsl=nsl, ib=ib: e.activation(
                    cq[:, nsl], pc_[:, :], AF.Identity,
                    bias=w["cb_c"][:, ib:ib + 1]))
                pr.emit("scalar", lambda e, nsl=nsl, ib=ib: e.activation(
                    dq[:, nsl], pd_[:, :], AF.Identity,
                    bias=w["db_c"][:, ib:ib + 1]), sem="bcdev")
                pr.wait("tensor", "bcdev")
            pr.wait("vector", "bcdev")
            pr.emit("vector", lambda e, ib=ib: e.tensor_tensor_scan(
                hsq[:, :], a_rep[:, :], bq[:, :], carry[:, ib:ib + 1],
                op0=ALU.mult, op1=ALU.add))
            vsync()
            pr.emit("vector", lambda e, ib=ib: e.tensor_copy(
                carry[:, ib:ib + 1], hsq[:, 1023:1024]), sem="scn")
            pr.wait("pool", "scn")
            pr.emit("pool", lambda e: e.tensor_mul(cq[:, :], cq[:, :],
                                                   hsq[:, :]))
            pr.tag("pool", "ych")
            pr.wait("pool", "ych")
            pr.emit("pool", lambda e: e.tensor_add(cq[:, :], cq[:, :],
                                                   dq[:, :]))
            pr.tag("pool", "ych")
            pr.wait("pool", "ych")
            pr.emit("pool", lambda e: e.tensor_copy(y_bf[:, :], cq[:, :]),
                    sem="ybf")
            pr.wait("tensor", "ybf")
            for nn in range(2):
                for mo in range(DB):
                    s = mo % 2
                    po = po_[s]
                    pr.wait("tensor", "oev", max(0, pr.counts.get("oev", 0) - 1))
                    pr.emit("tensor", lambda e, mo=mo, nn=nn, s=s: e.matmul(
                        po_[s][:, :], wo[:, mo * P:(mo + 1) * P],
                        y_bf[:, nn * 512:(nn + 1) * 512],
                        start=True, stop=True), sem="pe")
                    pr.wait("scalar", "pe")
                    pr.emit("scalar", lambda e, s=s: e.activation(
                        t1b[:, :], po_[s][:, :], AF.Identity), sem="act")
                    pr.wait("vector", "act")
                    osl = slice(mo * T + th * 1024 + nn * 512,
                                mo * T + th * 1024 + nn * 512 + 512)
                    pr.emit("vector", lambda e, osl=osl: e.tensor_add(
                        oacc[:, osl], oacc[:, osl], t1b[:, :]), sem="oev")
                    pr.wait("scalar", "oev")
        pe_hist2.append(pr.counts.get("pe", 0))

    pr.wait("vector", "oev")
    for b in range(DB):
        pr.emit("vector", lambda e, b=b: e.scalar_tensor_tensor(
            xsb[:, b * T:(b + 1) * T], oacc[:, b * T:(b + 1) * T],
            w["outb_c"][:, b:b + 1], xsb[:, b * T:(b + 1) * T],
            op0=ALU.add, op1=ALU.add), sem="dve")


def build_l0_ffn_program():
    nc = bass.Bass("TRN2", target_bir_lowering=False)
    x_in = nc.dram_tensor("x_in", [D, T], F32, kind="ExternalInput").ap()
    x_out = nc.dram_tensor("x_out", [D, T], F32, kind="ExternalOutput").ap()
    d_WinT = nc.dram_tensor("WinT", [D, 64], F32, kind="ExternalInput").ap()
    d_bin = nc.dram_tensor("bin_", [64, 1], F32, kind="ExternalInput").ap()
    d_W1 = nc.dram_tensor("W1", [64, 16], F32, kind="ExternalInput").ap()
    d_b1 = nc.dram_tensor("b1", [64, 16], F32, kind="ExternalInput").ap()
    d_W2 = nc.dram_tensor("W2", [64, 16], F32, kind="ExternalInput").ap()
    d_b2 = nc.dram_tensor("b2", [64, 1], F32, kind="ExternalInput").ap()
    d_WoutT = nc.dram_tensor("WoutT", [64, D], F32, kind="ExternalInput").ap()
    d_bout = nc.dram_tensor("bout_c", [P, DB], F32, kind="ExternalInput").ap()
    d_ones = nc.dram_tensor("ones1", [1, P], F32, kind="ExternalInput").ap()
    d_onesc = nc.dram_tensor("ones_c", [P, 1], F32, kind="ExternalInput").ap()

    es = ExitStack()
    with es:
        pr = Prog(nc, es)
        xsb = pr.sb("xsb", [P, DB * T])
        xb = pr.sb("xb", [P, DB * T], BF16)
        sq = pr.sb("sq", [P, 2 * 512], BF16)
        rowsB = pr.sb("rowsB", [1, 4 * T])
        rep = pr.sb("rep", [P, 2 * T])
        s64 = pr.sb("s64", [P, T])
        g64 = pr.sb("g64", [P, T])
        relu = pr.sb("relu", [P, T])
        gb = pr.sb("gb", [64, T], BF16)
        tmp = pr.sb("tmp", [P, 512])
        ones_b = pr.sb("ones_b", [1, P], BF16)
        ones_f = pr.sb("ones_f", [1, P])
        ones_cf = pr.sb("ones_cf", [P, 1])
        ones_cb = pr.sb("ones_cb", [P, 1], BF16)
        WinT_f = pr.sb("WinT_f", [P, DB * 64])
        WinT = pr.sb("WinT_s", [P, DB * 64], BF16)
        WoutT_f = pr.sb("WoutT_f", [64, D])
        WoutT = pr.sb("WoutT_s", [64, D], BF16)
        W1c = pr.sb("W1c", [P, 8])
        b1c = pr.sb("b1c", [P, 8])
        W2c = pr.sb("W2c", [P, 8])
        binc = pr.sb("binc", [64, 1])
        b2c = pr.sb("b2c", [64, 1])
        bout_c = pr.sb("bout_c_s", [P, DB])
        bout_half = pr.sb("bout_half", [P, DB])
        p0 = pr.ps("p0")
        p1 = pr.ps("p1")

        def ld(dst, src):
            pr.emit("sync", lambda e, dst=dst, src=src: e.dma_start(
                out=dst, in_=src), sem=pr.ldsem, amt=16)

        for b in range(DB):
            ld(xsb[:, b * T:(b + 1) * T], x_in[b * P:(b + 1) * P, :])
        for b in range(DB):
            ld(WinT_f[:, b * 64:(b + 1) * 64], d_WinT[b * P:(b + 1) * P, :])
        ld(WoutT_f[:, :], d_WoutT[:, :])
        ld(W1c[0:64, :], d_W1[:, 0:8])
        ld(W1c[64:128, :], d_W1[:, 8:16])
        ld(b1c[0:64, :], d_b1[:, 0:8])
        ld(b1c[64:128, :], d_b1[:, 8:16])
        ld(W2c[0:64, :], d_W2[:, 0:8])
        ld(W2c[64:128, :], d_W2[:, 8:16])
        ld(binc[:, :], d_bin[:, :])
        ld(b2c[:, :], d_b2[:, :])
        ld(bout_c[:, :], d_bout[:, :])
        ld(ones_f[:, :], d_ones[:, :])
        ld(ones_cf[:, :], d_onesc[:, :])

        pr.wait("vector", pr.ldsem)
        pr.emit("vector", lambda e: e.tensor_copy(WinT[:, :], WinT_f[:, :]))
        pr.emit("vector", lambda e: e.tensor_copy(
            WoutT[0:64, :], WoutT_f[0:64, :]))
        pr.emit("vector", lambda e: e.tensor_copy(ones_b[:, :], ones_f[:, :]))
        pr.emit("vector", lambda e: e.tensor_copy(ones_cb[:, :], ones_cf[:, :]))
        pr.emit("vector", lambda e: e.tensor_scalar(
            bout_half[:, :], bout_c[:, :], 0.5, None, op0=ALU.mult), sem="dve")
        pr.wait("scalar", pr.ldsem)
        pr.wait("tensor", pr.ldsem)
        pr.wait("tensor", "dve")
        pr.wait("scalar", "dve")

        sc = dict(xb=xb, sq=sq, rowsB=rowsB, rep=rep, p0=p0, p1=p1,
                  ones_b=ones_b, ones_f=ones_f, ones_cb=ones_cb,
                  ones_cf=ones_cf, s64=s64, g64=g64,
                  relu=relu, gb=gb, tmp=tmp)
        w = dict(WinT=WinT, W1c=W1c, b1c=b1c, W2c=W2c, binc=binc, b2c=b2c,
                 WoutT=WoutT, bout_half=bout_half)
        emit_ln(pr, xsb, xb, None, sc)
        emit_l0_ffn(pr, xb, xsb, w, sc)

        pr.wait("sync", "dve")
        for b in range(DB):
            pr.emit("sync", lambda e, b=b: e.dma_start(
                out=x_out[b * P:(b + 1) * P, :],
                in_=xsb[:, b * T:(b + 1) * T]), sem="st", amt=16)
        pr.wait("sync", "st")
        pr.finalize()
    return nc


def prep_l0_ffn_inputs(inputs):
    c = common_consts()
    return {
        "WinT": np.ascontiguousarray(inputs["L0_ffn_Win"].astype(np.float32)),
        "bin_": np.ascontiguousarray(
            inputs["L0_ffn_bin_"].astype(np.float32).reshape(64, 1)),
        "W1": np.ascontiguousarray(inputs["L0_ffn_W1"].astype(np.float32)),
        "b1": np.ascontiguousarray(inputs["L0_ffn_b1"].astype(np.float32)),
        "W2": np.ascontiguousarray(inputs["L0_ffn_W2"].astype(np.float32)),
        "b2": np.ascontiguousarray(
            inputs["L0_ffn_b2"].astype(np.float32).reshape(64, 1)),
        "WoutT": np.ascontiguousarray(
            inputs["L0_ffn_Wout"].astype(np.float32)),
        "bout_c": _cols(inputs["L0_ffn_bout"]),
        "ones1": c["ones1"],
        "ones_c": c["ones_c"],
    }


def build_moe_program():
    nc = bass.Bass("TRN2", target_bir_lowering=False)
    x_in = nc.dram_tensor("x_in", [D, T], F32, kind="ExternalInput").ap()
    x_out = nc.dram_tensor("x_out", [D, T], F32, kind="ExternalOutput").ap()
    d_rwT = nc.dram_tensor("rwT", [D, NEXP], F32, kind="ExternalInput").ap()
    d_w1 = nc.dram_tensor("w1", [NEXP, D, INNER], F32,
                          kind="ExternalInput").ap()
    d_w2 = nc.dram_tensor("w2", [NEXP, INNER, D], F32,
                          kind="ExternalInput").ap()
    d_b1c = nc.dram_tensor("b1c", [NEXP, P, 16], F32,
                           kind="ExternalInput").ap()
    d_b2c = nc.dram_tensor("b2c", [NEXP, P, 8], F32,
                           kind="ExternalInput").ap()
    d_ones = nc.dram_tensor("ones1", [1, P], F32, kind="ExternalInput").ap()
    d_onesc = nc.dram_tensor("ones_c", [P, 1], F32, kind="ExternalInput").ap()
    d_ident = nc.dram_tensor("ident", [P, P], F32, kind="ExternalInput").ap()
    d_oh = nc.dram_tensor("onehot14", [NEXP, NEXP * P], F32,
                          kind="ExternalInput").ap()

    es = ExitStack()
    with es:
        pr = Prog(nc, es)
        xsb = pr.sb("xsb", [P, DB * T])
        xb = pr.sb("xb", [P, DB * T], BF16)
        ohb = pr.sb("ohb", [NEXP, NEXP * P], BF16)
        ones_b = pr.sb("ones_b", [1, P], BF16)
        ones_f = pr.sb("ones_f", [1, P])
        ones_cf = pr.sb("ones_cf", [P, 1])
        ones_cb = pr.sb("ones_cb", [P, 1], BF16)
        identf = pr.sb("identf", [P, P])
        msel_b = pr.sb("msel_b", [NEXP, T], BF16)
        ones14 = pr.sb("ones14", [P, NEXP])
        p0 = pr.ps("p0")
        p1 = pr.ps("p1")
        p2 = pr.ps("p2")
        p3 = pr.ps("p3")
        ph0 = pr.ps("ph0")
        ph1 = pr.ps("ph1")
        py0 = pr.ps("py0")
        py1 = pr.ps("py1")

        es2 = es.enter_context(ExitStack())
        sq = pr.sb("sq", [P, 2 * 512], BF16, es=es2)
        rowsB = pr.sb("rowsB", [1, 4 * T], es=es2)
        rep = pr.sb("rep", [P, 2 * T], es=es2)
        rwT = pr.sb("rwT_s", [P, DB * NEXP], es=es2)
        colsum_n = pr.sb("colsum_n", [NEXP, 1], es=es2)
        lgs = pr.sb("lgs", [NEXP, T], es=es2)
        lgT = pr.sb("lgT", [P, 16 * NEXP], es=es2)
        red = pr.sb("red", [P, 8], es=es2)
        mt = pr.sb("mt", [P, NEXP], es=es2)
        cums = pr.sb("cums", [P, NEXP], es=es2)
        msel = pr.sb("msel", [NEXP, T], es=es2)

        def ld(dst, src):
            pr.emit("sync", lambda e, dst=dst, src=src: e.dma_start(
                out=dst, in_=src), sem=pr.ldsem, amt=16)

        for b in range(DB):
            ld(xsb[:, b * T:(b + 1) * T], x_in[b * P:(b + 1) * P, :])
        for b in range(DB):
            ld(rwT[:, b * NEXP:(b + 1) * NEXP], d_rwT[b * P:(b + 1) * P, :])
        ld(ones_f[:, :], d_ones[:, :])
        ld(ones_cf[:, :], d_onesc[:, :])
        ld(identf[:, :], d_ident[:, :])
        ld(rep[0:NEXP, 0:NEXP * P], d_oh[:, :])

        pr.wait("vector", pr.ldsem)
        pr.emit("vector", lambda e: e.tensor_copy(ones_b[:, :], ones_f[:, :]))
        pr.emit("vector", lambda e: e.tensor_copy(ones_cb[:, :], ones_cf[:, :]))
        pr.emit("vector", lambda e: e.tensor_copy(
            ohb[:, :], rep[0:NEXP, 0:NEXP * P]))
        pr.emit("vector", lambda e: e.memset(ones14[:, :], 1.0), sem="dve")
        pr.wait("scalar", pr.ldsem)
        pr.wait("tensor", pr.ldsem)
        pr.wait("tensor", "dve")
        pr.wait("scalar", "dve")

        sc = dict(xb=xb, sq=sq, rowsB=rowsB, rep=rep, p0=p0, p1=p1,
                  ones_b=ones_b, ones_f=ones_f, ones_cb=ones_cb,
                  ones_cf=ones_cf)
        emit_ln(pr, xsb, xb, None, sc)
        r = dict(rwT=rwT, colsum_n=colsum_n, lgs=lgs, lgT=lgT, red=red, mt=mt,
                 cums=cums, ones14=ones14, msel=msel, msel_b=msel_b,
                 identf=identf, ones_f=ones_f, p2=p2, p3=p3)
        emit_router(pr, xsb, sc, r)
        pr.barrier("router_done")
        es2.close()

        w1s = pr.sb("w1s", [P, 8 * T], BF16)
        w2s = pr.sb("w2s", [P, 16 * 1024], BF16)
        w1f = pr.sb("w1f", [P, 2 * 1024])
        b1c = pr.sb("b1c_s", [P, NEXP * 16])
        b2c = pr.sb("b2c_s", [P, NEXP * 8])
        mselr = pr.sb("mselr", [P, 2 * T], BF16)
        hid = pr.sb("hid", [P, 16 * 512], BF16)
        t1 = pr.sb("t1", [P, 4 * 512], BF16)
        mm = dict(w1s=w1s, w2s=w2s, w1f=w1f, b1c=b1c, b2c=b2c,
                  mselr=mselr, hid=hid, t1=t1, msel_b=msel_b, ohb=ohb,
                  ph0=ph0, ph1=ph1, py0=py0, py1=py1, p2=p2,
                  p0=p0, p1=p1, p2b=p2, p3b=p3)
        dram = dict(d_w1=d_w1, d_w2=d_w2, d_b1c=d_b1c, d_b2c=d_b2c)
        emit_moe_experts(pr, xb, xsb, mm, dram)

        pr.wait("sync", "dve")
        pr.wait("sync", "yacc")
        for b in range(DB):
            pr.emit("sync", lambda e, b=b: e.dma_start(
                out=x_out[b * P:(b + 1) * P, :],
                in_=xsb[:, b * T:(b + 1) * T]), sem="st", amt=16)
        pr.wait("sync", "st")
        pr.finalize()
    return nc



def build_l0_attn_program():
    nc = bass.Bass("TRN2", target_bir_lowering=False)
    x_in = nc.dram_tensor("x_in", [D, T], F32, kind="ExternalInput").ap()
    x_out = nc.dram_tensor("x_out", [D, T], F32, kind="ExternalOutput").ap()
    d_WxT = nc.dram_tensor("WxT", [D, 96], F32, kind="ExternalInput").ap()
    d_WdtT = nc.dram_tensor("WdtT", [64, D], F32, kind="ExternalInput").ap()
    d_bdt = nc.dram_tensor("bdt_c", [P, DB], F32, kind="ExternalInput").ap()
    d_Alog = nc.dram_tensor("Alog_c", [P, DB * 16], F32, kind="ExternalInput").ap()
    d_Dc = nc.dram_tensor("D_c", [P, DB], F32, kind="ExternalInput").ap()
    d_ones = nc.dram_tensor("ones1", [1, P], F32, kind="ExternalInput").ap()
    d_onesc = nc.dram_tensor("ones_c", [P, 1], F32, kind="ExternalInput").ap()

    es = ExitStack()
    with es:
        pr = Prog(nc, es)
        xsb = pr.sb("xsb", [P, DB * T])
        xn = pr.sb("xn", [P, DB * T])
        ones_b = pr.sb("ones_b", [1, P], BF16)
        ones_f = pr.sb("ones_f", [1, P])
        ones_cf = pr.sb("ones_cf", [P, 1])
        ones_cb = pr.sb("ones_cb", [P, 1], BF16)
        WxdtT = pr.sb("WxdtT", [P, DB * 64], BF16)
        WxBT = pr.sb("WxBT", [P, DB * 16], BF16)
        WxCT = pr.sb("WxCT", [P, DB * 16], BF16)
        WdtT = pr.sb("WdtT_s", [64, D], BF16)
        bdt_c = pr.sb("bdt_c_s", [P, DB])
        A_cols = pr.sb("A_cols", [P, DB * 16])
        D_c = pr.sb("D_c_s", [P, DB])
        p0 = pr.ps("p0")
        p1 = pr.ps("p1")
        p2 = pr.ps("p2")
        p3 = pr.ps("p3")
        pB = pr.ps("pB")
        pC = pr.ps("pC")

        es2 = es.enter_context(ExitStack())
        sq = pr.sb("sq", [P, 2 * 512], F32, es=es2)
        rowsB = pr.sb("rowsB", [1, 4 * T], es=es2)
        rep = pr.sb("rep", [P, 2 * T], es=es2)
        wstage = pr.sb("wstage", [P, DB * 96], F32, es=es2)
        wstage2 = pr.sb("wstage2", [64, D], F32, es=es2)

        def ld(dst, src):
            pr.emit("sync", lambda e, dst=dst, src=src: e.dma_start(
                out=dst, in_=src), sem=pr.ldsem, amt=16)

        for b in range(DB):
            ld(xsb[:, b * T:(b + 1) * T], x_in[b * P:(b + 1) * P, :])
        for b in range(DB):
            ld(wstage[:, b * 96:(b + 1) * 96], d_WxT[b * P:(b + 1) * P, :])
        ld(wstage2[:, :], d_WdtT[:, :])
        ld(bdt_c[:, :], d_bdt[:, :])
        ld(A_cols[:, :], d_Alog[:, :])
        ld(D_c[:, :], d_Dc[:, :])
        ld(ones_f[:, :], d_ones[:, :])
        ld(ones_cf[:, :], d_onesc[:, :])

        pr.wait("vector", pr.ldsem)
        pr.emit("vector", lambda e: e.tensor_copy(ones_b[:, :], ones_f[:, :]))
        pr.emit("vector", lambda e: e.tensor_copy(ones_cb[:, :], ones_cf[:, :]))
        for b in range(DB):
            pr.emit("vector", lambda e, b=b: e.tensor_copy(
                WxdtT[:, b * 64:(b + 1) * 64],
                wstage[:, b * 96: b * 96 + 64]))
            pr.emit("vector", lambda e, b=b: e.tensor_copy(
                WxBT[:, b * 16:(b + 1) * 16],
                wstage[:, b * 96 + 64: b * 96 + 80]))
            pr.emit("vector", lambda e, b=b: e.tensor_copy(
                WxCT[:, b * 16:(b + 1) * 16],
                wstage[:, b * 96 + 80: b * 96 + 96]))
        pr.emit("vector", lambda e: e.tensor_copy(WdtT[0:64, :], wstage2[0:64, :]))
        pr.tag("vector", "dve")
        pr.wait("scalar", pr.ldsem)
        pr.emit("scalar", lambda e: e.activation(A_cols[:, :], A_cols[:, :],
                                                 AF.Exp), sem="act")
        pr.wait("vector", "act")
        pr.emit("vector", lambda e: e.tensor_scalar(
            A_cols[:, :], A_cols[:, :], -1.0, None, op0=ALU.mult), sem="dve")
        pr.wait("tensor", pr.ldsem)
        pr.wait("tensor", "dve")
        pr.wait("scalar", "dve")

        sc = dict(xb=None, sq=sq, rowsB=rowsB, rep=rep, p0=p0, p1=p1,
                  ones_b=ones_b, ones_f=ones_f, ones_cb=ones_cb,
                  ones_cf=ones_cf)
        emit_ln(pr, xsb, None, xn, sc)
        pr.barrier("ln_done")
        es2.close()

        xnb_c = pr.sb("xnb_c", [P, DB * 512], BF16)
        dtF = pr.sb("dtF", [P, DB * 512])
        dtlo = pr.sb("dtlo", [64, 512], BF16)
        Bsb = pr.sb("Bsb", [16, 512])
        Csb = pr.sb("Csb", [16, 512])
        h_state = pr.sb("h_state", [P, P])
        sp1 = pr.sb("sp1", [P, 512])
        sp2 = pr.sb("sp2", [P, 512])
        a_t = pr.sb("a_t", [P, 512])
        u2_t = pr.sb("u2_t", [P, 512])
        hs_t = pr.sb("hs_t", [P, 512])
        yt_t = pr.sb("yt_t", [P, 2 * 512])
        brow = pr.sb("brow", [1, 4 * 512])
        w = dict(WxdtT=WxdtT, WxBT=WxBT, WxCT=WxCT, WdtT=WdtT, bdt_c=bdt_c,
                 A_cols=A_cols, D_c=D_c)
        sc2 = dict(xnb_c=xnb_c, dtF=dtF, dtlo=dtlo, Bsb=Bsb, Csb=Csb,
                   h_state=h_state, sp1=sp1, sp2=sp2, a_t=a_t, u2_t=u2_t,
                   hs_t=hs_t, yt_t=yt_t, brow=brow, ones_f=ones_f,
                   p0=p0, p1=p1, p2=p2, p3=p3, pB=pB, pC=pC)
        emit_l0_attn(pr, xsb, xn, w, sc2)

        pr.wait("sync", "dve")
        for b in range(DB):
            pr.emit("sync", lambda e, b=b: e.dma_start(
                out=x_out[b * P:(b + 1) * P, :],
                in_=xsb[:, b * T:(b + 1) * T]), sem="st", amt=16)
        pr.wait("sync", "st")
        pr.finalize()
    return nc


def prep_l0_attn_inputs(inputs):
    c = common_consts()
    Alog = inputs["L0_attn_A_log"].astype(np.float32)
    Alog_c = np.ascontiguousarray(
        Alog.reshape(DB, P, 16).transpose(1, 0, 2).reshape(P, DB * 16))
    return {
        "WxT": np.ascontiguousarray(
            inputs["L0_attn_W_xproj"].astype(np.float32).T),
        "WdtT": np.ascontiguousarray(
            inputs["L0_attn_W_dt"].astype(np.float32).T),
        "bdt_c": _cols(inputs["L0_attn_b_dt"]),
        "Alog_c": Alog_c,
        "D_c": _cols(inputs["L0_attn_D_param"]),
        "ones1": c["ones1"],
        "ones_c": c["ones_c"],
    }



def emit_ln_zres(pr, xsb, znb, gamma_col, sc):
    rep, p0, p1 = sc["rep"], sc["p0"], sc["p1"]
    rowsB = sc["rowsB"]
    sq = sc["sq"]
    tmpb = sc["tmpb"]
    m_r = rowsB[0:1, 0:T]
    v_r = rowsB[0:1, T:2 * T]
    rstd_r = rowsB[0:1, 2 * T:3 * T]
    aux_r = rowsB[0:1, 3 * T:4 * T]

    pr.wait("tensor", "dve")
    pr.wait("scalar", "dve")
    slot_free_at = [pr.counts.get("pe", 0)] * 2
    for n in range(NT):
        for b in range(DB):
            pr.emit("tensor", lambda e, b=b, n=n: e.matmul(
                p0[0:1, :], sc["ones_cf"][:, 0:1],
                xsb[:, b * T + n * 512: b * T + (n + 1) * 512],
                start=(b == 0), stop=(b == DB - 1)),
                sem="pe" if b == DB - 1 else None)
        for b in range(DB):
            s = b % 2
            pr.wait("scalar", "pe", slot_free_at[s])
            pr.emit("scalar", lambda e, b=b, n=n, s=s: e.activation(
                sq[:, s * 512:(s + 1) * 512],
                xsb[:, b * T + n * 512: b * T + (n + 1) * 512],
                AF.Square), sem="act")
            pr.wait("tensor", "act")
            slot_free_at[s] = pr.emit("tensor", lambda e, s=s, b=b: e.matmul(
                p1[0:1, :], sc["ones_cf"][:, 0:1], sq[:, s * 512:(s + 1) * 512],
                start=(b == 0), stop=(b == DB - 1)), sem="pe")
        pr.wait("vector", "pe")
        pr.emit("vector", lambda e, n=n: e.tensor_copy(
            m_r[:, n * 512:(n + 1) * 512], p0[0:1, :]))
        pr.emit("vector", lambda e, n=n: e.tensor_copy(
            v_r[:, n * 512:(n + 1) * 512], p1[0:1, :]), sem="dve")
        pr.wait("tensor", "dve")

    def vs():
        pr.tag("vector", "dve")
        pr.wait("vector", "dve")

    pr.emit("vector", lambda e: e.tensor_scalar(
        m_r, m_r, 1.0 / D, None, op0=ALU.mult))
    vs()
    pr.emit("vector", lambda e: e.tensor_mul(aux_r, m_r, m_r))
    vs()
    pr.emit("vector", lambda e: e.tensor_scalar(
        v_r, v_r, 1.0 / D, EPS_LN, op0=ALU.mult, op1=ALU.add))
    vs()
    pr.emit("vector", lambda e: e.tensor_sub(v_r, v_r, aux_r))
    vs()
    pr.emit("vector", lambda e: e.tensor_scalar(
        aux_r, v_r, -EPS_LN, None, op0=ALU.add))
    vs()
    pr.emit("vector", lambda e: e.reciprocal(rstd_r, v_r))
    vs()
    pr.emit("vector", lambda e: e.tensor_mul(aux_r, aux_r, rstd_r))
    vs()
    pr.emit("vector", lambda e: e.tensor_scalar(
        aux_r, aux_r, EPS_RMS, None, op0=ALU.add), sem="dve")
    pr.wait("scalar", "dve")
    pr.emit("scalar", lambda e: e.activation(rstd_r, v_r, AF.Ln))
    pr.tag("scalar", "act")
    pr.wait("scalar", "act")
    pr.emit("scalar", lambda e: e.activation(
        rstd_r, rstd_r, AF.Exp, scale=-0.5))
    pr.tag("scalar", "act")
    pr.wait("scalar", "act")
    pr.emit("scalar", lambda e: e.activation(aux_r, aux_r, AF.Ln))
    pr.tag("scalar", "act")
    pr.wait("scalar", "act")
    pr.emit("scalar", lambda e: e.activation(
        aux_r, aux_r, AF.Exp, scale=-0.5), sem="act")

    pr.wait("tensor", "act")
    for n in range(NT):
        pr.emit("tensor", lambda e, n=n: e.matmul(
            p0[:, :], sc["ones_f"][0:1, :], m_r[:, n * 512:(n + 1) * 512],
            start=True, stop=True))
        pr.emit("tensor", lambda e, n=n: e.matmul(
            p1[:, :], sc["ones_f"][0:1, :], rstd_r[:, n * 512:(n + 1) * 512],
            start=True, stop=True), sem="pe")
        pr.wait("vector", "pe")
        pr.emit("vector", lambda e, n=n: e.tensor_copy(
            rep[:, n * 512:(n + 1) * 512], p0[:, :]))
        pr.emit("vector", lambda e, n=n: e.tensor_copy(
            rep[:, T + n * 512:T + (n + 1) * 512], p1[:, :]), sem="dve")
        pr.wait("tensor", "dve")
        pr.emit("tensor", lambda e, n=n: e.matmul(
            p0[:, :], sc["ones_f"][0:1, :], aux_r[:, n * 512:(n + 1) * 512],
            start=True, stop=True), sem="pe")
        pr.wait("vector", "pe")
        pr.emit("vector", lambda e, n=n: e.tensor_copy(
            rep[:, 2 * T + n * 512:2 * T + (n + 1) * 512], p0[:, :]),
            sem="dve")
        pr.wait("tensor", "dve")

    m_rep = rep[:, 0:T]
    rstd_rep = rep[:, T:2 * T]
    r2_rep = rep[:, 2 * T:3 * T]
    for b in range(DB):
        xb_sl = xsb[:, b * T:(b + 1) * T]
        zn_sl = znb[:, b * T:(b + 1) * T]
        pr.emit("vector", lambda e, xb_sl=xb_sl: e.tensor_sub(
            tmpb[:, :], xb_sl, m_rep))
        vs()
        pr.emit("vector", lambda e: e.tensor_mul(
            tmpb[:, :], tmpb[:, :], rstd_rep))
        vs()
        pr.emit("vector", lambda e, xb_sl=xb_sl: e.tensor_add(
            xb_sl, xb_sl, tmpb[:, :]))
        pr.emit("vector", lambda e, zn_sl=zn_sl: e.tensor_mul(
            zn_sl, tmpb[:, :], r2_rep))
        vs()
        pr.emit("vector", lambda e, zn_sl=zn_sl, b=b: e.tensor_scalar(
            zn_sl, zn_sl, sc["normw_c"][:, b:b + 1], None, op0=ALU.mult))
    pr.tag("vector", "dve")


def build_l1_attn_program():
    nc = bass.Bass("TRN2", target_bir_lowering=False)
    x_in = nc.dram_tensor("x_in", [D, T], F32, kind="ExternalInput").ap()
    x_out = nc.dram_tensor("x_out", [D, T], F32, kind="ExternalOutput").ap()
    d_inwT = nc.dram_tensor("inwT", [D, INNER], F32, kind="ExternalInput").ap()
    d_gatewT = nc.dram_tensor("gatewT", [D, INNER], F32, kind="ExternalInput").ap()
    d_bwT = nc.dram_tensor("bwT", [INNER, INNER], F32, kind="ExternalInput").ap()
    d_cwT = nc.dram_tensor("cwT", [INNER, INNER], F32, kind="ExternalInput").ap()
    d_dwT = nc.dram_tensor("dwT", [INNER, INNER], F32, kind="ExternalInput").ap()
    d_outwT = nc.dram_tensor("outwT", [INNER, D], F32, kind="ExternalInput").ap()
    d_inb = nc.dram_tensor("inb_c", [P, IBL], F32, kind="ExternalInput").ap()
    d_gateb = nc.dram_tensor("gateb_c", [P, IBL], F32, kind="ExternalInput").ap()
    d_bb = nc.dram_tensor("bb_c", [P, IBL], F32, kind="ExternalInput").ap()
    d_cb = nc.dram_tensor("cb_c", [P, IBL], F32, kind="ExternalInput").ap()
    d_db = nc.dram_tensor("db_c", [P, IBL], F32, kind="ExternalInput").ap()
    d_outb = nc.dram_tensor("outb_c", [P, DB], F32, kind="ExternalInput").ap()
    d_alog = nc.dram_tensor("alog_c", [P, IBL], F32, kind="ExternalInput").ap()
    d_normw = nc.dram_tensor("normw_c", [P, DB], F32, kind="ExternalInput").ap()
    d_ones = nc.dram_tensor("ones1", [1, P], F32, kind="ExternalInput").ap()
    d_onesc = nc.dram_tensor("ones_c", [P, 1], F32, kind="ExternalInput").ap()

    es = ExitStack()
    with es:
        pr = Prog(nc, es)
        xsb = pr.sb("xsb", [P, DB * T])
        znb = pr.sb("znb", [P, DB * T], BF16)
        ones_f = pr.sb("ones_f", [1, P])
        ones_cf = pr.sb("ones_cf", [P, 1])
        inb_c = pr.sb("inb_c_s", [P, IBL])
        gateb_c = pr.sb("gateb_c_s", [P, IBL])
        bb_c = pr.sb("bb_c_s", [P, IBL])
        cb_c = pr.sb("cb_c_s", [P, IBL])
        db_c = pr.sb("db_c_s", [P, IBL])
        outb_c = pr.sb("outb_c_s", [P, DB])
        a_dec = pr.sb("a_dec", [P, IBL])
        normw_c = pr.sb("normw_c_s", [P, DB])
        asp = pr.sb("asp", [P, 2 * IBL])
        p0 = pr.ps("p0")
        p1 = pr.ps("p1")
        p2 = pr.ps("p2")
        p3 = pr.ps("p3")
        p4 = pr.ps("p4")

        def ld(dst, srcd):
            pr.emit("sync", lambda e, dst=dst, srcd=srcd: e.dma_start(
                out=dst, in_=srcd), sem=pr.ldsem, amt=16)

        for b in range(DB):
            ld(xsb[:, b * T:(b + 1) * T], x_in[b * P:(b + 1) * P, :])
        ld(inb_c[:, :], d_inb[:, :])
        ld(gateb_c[:, :], d_gateb[:, :])
        ld(bb_c[:, :], d_bb[:, :])
        ld(cb_c[:, :], d_cb[:, :])
        ld(db_c[:, :], d_db[:, :])
        ld(outb_c[:, :], d_outb[:, :])
        ld(a_dec[:, :], d_alog[:, :])
        ld(normw_c[:, :], d_normw[:, :])
        ld(ones_f[:, :], d_ones[:, :])
        ld(ones_cf[:, :], d_onesc[:, :])

        def ss():
            pr.tag("scalar", "act")
            pr.wait("scalar", "act")

        pr.wait("scalar", pr.ldsem)
        pr.emit("scalar", lambda e: e.activation(asp[:, 0:IBL], a_dec[:, :],
                                                 AF.Relu))
        ss()
        pr.emit("scalar", lambda e: e.activation(asp[:, IBL:2 * IBL],
                                                 a_dec[:, :], AF.Abs))
        ss()
        pr.emit("scalar", lambda e: e.activation(
            asp[:, IBL:2 * IBL], asp[:, IBL:2 * IBL], AF.Exp, scale=-1.0))
        ss()
        pr.emit("scalar", lambda e: e.activation(
            asp[:, IBL:2 * IBL], asp[:, IBL:2 * IBL], AF.Ln, bias=1.0),
            sem="act")
        pr.wait("vector", "act")
        pr.emit("vector", lambda e: e.tensor_add(
            asp[:, 0:IBL], asp[:, 0:IBL], asp[:, IBL:2 * IBL]), sem="dve")
        pr.wait("scalar", "dve")
        pr.emit("scalar", lambda e: e.activation(
            a_dec[:, :], asp[:, 0:IBL], AF.Exp, scale=-1.0), sem="act")
        pr.wait("vector", pr.ldsem)
        pr.wait("tensor", pr.ldsem)
        pr.wait("vector", "act")
        pr.wait("tensor", "act")

        es2 = es.enter_context(ExitStack())
        sq = pr.sb("sq", [P, 2 * 512], F32, es=es2)
        rowsB = pr.sb("rowsB", [1, 4 * T], es=es2)
        rep = pr.sb("rep", [P, 3 * T], es=es2)
        tmpb = pr.sb("tmpb", [P, T], es=es2)
        sc = dict(sq=sq, rowsB=rowsB, rep=rep, tmpb=tmpb, ones_cf=ones_cf,
                  ones_f=ones_f, normw_c=normw_c, p0=p0, p1=p1)
        emit_ln_zres(pr, xsb, znb, normw_c, sc)
        pr.barrier("zres_done")
        es2.close()

        u_bf = pr.sb("u_bf", [P, IBL * T], BF16)
        es3 = es.enter_context(ExitStack())
        wstg1 = pr.sb("wstg1", [P, 2 * 4 * P], F32, es=es3)
        wcol_in = pr.sb("wcol_in", [P, 2 * DB * P], BF16, es=es3)
        wcol_g = pr.sb("wcol_g", [P, 2 * DB * P], BF16, es=es3)
        t2 = pr.sb("t2", [P, 512], F32, es=es3)
        t3 = pr.sb("t3", [P, 512], F32, es=es3)
        w1 = dict(d_inwT=d_inwT, d_gatewT=d_gatewT, inb_c=inb_c,
                  gateb_c=gateb_c)
        sc1 = dict(wstg=wstg1, wcol_in=wcol_in, wcol_g=wcol_g, t2=t2, t3=t3,
                   pu=p0, pg=p1)
        emit_l1_attn_e1(pr, znb, u_bf, w1, sc1)
        pr.barrier("e1_done")
        es3.close()

        wstg = pr.sb("wstg", [P, 2 * 4 * P], F32)
        wb = pr.sb("wb", [P, IBL * P], BF16)
        wc = pr.sb("wc", [P, IBL * P], BF16)
        wd = pr.sb("wd", [P, IBL * P], BF16)
        wo = pr.sb("wo", [P, 1024], BF16)
        bq = pr.sb("bq", [P, 1024])
        cq = pr.sb("cq", [P, 1024])
        dq = pr.sb("dq", [P, 1024])
        hsq = pr.sb("hsq", [P, 1024])
        a_rep = pr.sb("a_rep", [P, 1024])
        onesf_t = pr.sb("onesf_t", [P, 1024])
        carry = pr.sb("carry", [P, IBL])
        y_bf = pr.sb("y_bf", [P, 1024], BF16)
        t1b = pr.sb("t1b", [P, 512], BF16)
        pr.emit("vector", lambda e: e.memset(onesf_t[:, :], 1.0), sem="dve")
        w2 = dict(d_bwT=d_bwT, d_cwT=d_cwT, d_dwT=d_dwT, d_outwT=d_outwT,
                  bb_c=bb_c, cb_c=cb_c, db_c=db_c, outb_c=outb_c,
                  a_dec=a_dec)
        sc2 = dict(wstg=wstg, wb=wb, wc=wc, wd=wd, wo=wo, bq=bq, cq=cq,
                   dq=dq, hsq=hsq, a_rep=a_rep, onesf_t=onesf_t, carry=carry,
                   y_bf=y_bf, oacc=znb, t1b=t1b,
                   pb_=p0, pc_=p1, pd_=p2, po_=[p3, p4])
        emit_l1_attn_e2(pr, u_bf, xsb, w2, sc2)

        pr.wait("sync", "dve")
        for b in range(DB):
            pr.emit("sync", lambda e, b=b: e.dma_start(
                out=x_out[b * P:(b + 1) * P, :],
                in_=xsb[:, b * T:(b + 1) * T]), sem="st", amt=16)
        pr.wait("sync", "st")
        pr.finalize()
    return nc


def prep_l1_attn_inputs(inputs):
    c = common_consts()
    f = np.float32
    return {
        "inwT": np.ascontiguousarray(inputs["L1_attn_in_w"].astype(f).T),
        "gatewT": np.ascontiguousarray(inputs["L1_attn_gate_w"].astype(f).T),
        "bwT": np.ascontiguousarray(inputs["L1_attn_b_w"].astype(f).T),
        "cwT": np.ascontiguousarray(inputs["L1_attn_c_w"].astype(f).T),
        "dwT": np.ascontiguousarray(inputs["L1_attn_d_w"].astype(f).T),
        "outwT": np.ascontiguousarray(inputs["L1_attn_out_w"].astype(f).T),
        "inb_c": _cols(inputs["L1_attn_in_b"]),
        "gateb_c": _cols(inputs["L1_attn_gate_b"]),
        "bb_c": _cols(inputs["L1_attn_b_b"]),
        "cb_c": _cols(inputs["L1_attn_c_b"]),
        "db_c": _cols(inputs["L1_attn_d_b"]),
        "outb_c": _cols(inputs["L1_attn_out_b"]),
        "alog_c": _cols(inputs["L1_attn_a_log"]),
        "normw_c": _cols(inputs["L1_attn_norm_w"]),
        "ones1": c["ones1"],
        "ones_c": c["ones_c"],
    }


def prep_moe_inputs(inputs):
    c = common_consts()
    b1 = inputs["L1_ffn_b1"].astype(np.float32)
    b2 = inputs["L1_ffn_b2"].astype(np.float32)
    return {
        "rwT": np.ascontiguousarray(
            inputs["L1_ffn_router_w"].astype(np.float32)),
        "w1": np.ascontiguousarray(inputs["L1_ffn_w1"].astype(np.float32)),
        "w2": np.ascontiguousarray(inputs["L1_ffn_w2"].astype(np.float32)),
        "b1c": np.ascontiguousarray(
            np.stack([_cols(b1[e]) for e in range(NEXP)])),
        "b2c": np.ascontiguousarray(
            np.stack([_cols(b2[e]) for e in range(NEXP)])),
        "ones1": c["ones1"],
        "ones_c": c["ones_c"],
        "ident": c["ident"],
        "onehot14": c["onehot14"],
    }




def build_fused_program():
    nc = bass.Bass("TRN2", target_bir_lowering=False)
    f = F32
    x_in = nc.dram_tensor("x_in", [D, T], f, kind="ExternalInput").ap()
    x_out = nc.dram_tensor("x_out", [D, T], f, kind="ExternalOutput").ap()
    dd = {}
    def din(name, shape):
        dd[name] = nc.dram_tensor(name, list(shape), f, kind="ExternalInput").ap()
    din("WinT", [D, 64]); din("bin_", [64, 1]); din("W1", [64, 16])
    din("b1", [64, 16]); din("W2", [64, 16]); din("b2", [64, 1])
    din("WoutT", [64, D]); din("bout_c", [P, DB])
    din("WxT", [D, 96]); din("WdtT", [64, D]); din("bdt_c", [P, DB])
    din("Alog_c", [P, DB * 16]); din("D_c", [P, DB])
    din("rwT", [D, NEXP]); din("w1", [NEXP, D, INNER]); din("w2", [NEXP, INNER, D])
    din("b1c", [NEXP, P, 16]); din("b2c", [NEXP, P, 8]); din("onehot14", [NEXP, NEXP * P])
    din("inwT", [D, INNER]); din("gatewT", [D, INNER]); din("bwT", [INNER, INNER])
    din("cwT", [INNER, INNER]); din("dwT", [INNER, INNER]); din("outwT", [INNER, D])
    din("inb_c", [P, IBL]); din("gateb_c", [P, IBL]); din("bb_c", [P, IBL])
    din("cb_c", [P, IBL]); din("db_c", [P, IBL]); din("outb_c", [P, DB])
    din("alog_c", [P, IBL]); din("normw_c", [P, DB])
    din("ones1", [1, P]); din("ones_c", [P, 1]); din("ident", [P, P])
    d_w1b = nc.dram_tensor("w1bcache", [NEXP, D, INNER], BF16,
                           kind="Internal").ap()
    d_w2b = nc.dram_tensor("w2bcache", [NEXP, INNER, D], BF16,
                           kind="Internal").ap()

    es = ExitStack()
    with es:
        pr = Prog(nc, es)
        xsb = pr.sb("xsb", [P, DB * T])
        ones_b = pr.sb("ones_b", [1, P], BF16)
        ones_f = pr.sb("ones_f", [1, P])
        ones_cf = pr.sb("ones_cf", [P, 1])
        ones_cb = pr.sb("ones_cb", [P, 1], BF16)
        identf = pr.sb("identf", [P, P])
        junk = pr.sb("junk", [P, 8])
        psum = [pr.ps(f"pp{i}") for i in range(8)]

        def ld(dst, srcd, es_=None):
            pr.emit("sync", lambda e, dst=dst, srcd=srcd: e.dma_start(
                out=dst, in_=srcd), sem=pr.ldsem, amt=16)

        for b in range(DB):
            ld(xsb[:, b * T:(b + 1) * T], x_in[b * P:(b + 1) * P, :])
        ld(ones_f[:, :], dd["ones1"][:, :])
        ld(ones_cf[:, :], dd["ones_c"][:, :])
        ld(identf[:, :], dd["ident"][:, :])
        pr.wait("vector", pr.ldsem)
        pr.emit("vector", lambda e: e.memset(junk[:, :], 0.0))
        pr.emit("vector", lambda e: e.tensor_copy(ones_b[:, :], ones_f[:, :]))
        pr.emit("vector", lambda e: e.tensor_copy(ones_cb[:, :], ones_cf[:, :]),
                sem="dve")
        pr.wait("scalar", pr.ldsem)
        pr.wait("tensor", pr.ldsem)
        pr.wait("tensor", "dve")
        pr.wait("scalar", "dve")

        def endphase(tag):
            pr.wait("vector", pr.ldsem)
            pr.emit("vector", lambda e: e.tensor_copy(junk[:, 0:4],
                                                      junk[:, 4:8]))
            pr.tag("vector", "dve")
            pr.barrier(tag)

        def phase_l0_ffn(tag):
            s2 = es.enter_context(ExitStack())
            xb = pr.sb(f"xb_{tag}", [P, DB * T], BF16, es=s2)
            sq = pr.sb(f"sq_{tag}", [P, 2 * 512], BF16, es=s2)
            rowsB = pr.sb(f"rowsB_{tag}", [1, 4 * T], es=s2)
            rep = pr.sb(f"rep_{tag}", [P, 2 * T], es=s2)
            s64 = pr.sb(f"s64_{tag}", [P, T], es=s2)
            g64 = pr.sb(f"g64_{tag}", [P, T], es=s2)
            relu = pr.sb(f"relu_{tag}", [P, T], es=s2)
            gb = pr.sb(f"gb_{tag}", [64, T], BF16, es=s2)
            tmp = pr.sb(f"tmp_{tag}", [P, 512], es=s2)
            WinT_f = pr.sb(f"WinT_f_{tag}", [P, DB * 64], es=s2)
            WinT = pr.sb(f"WinT_s_{tag}", [P, DB * 64], BF16, es=s2)
            WoutT_f = pr.sb(f"WoutT_f_{tag}", [64, D], es=s2)
            WoutT = pr.sb(f"WoutT_s_{tag}", [64, D], BF16, es=s2)
            W1c = pr.sb(f"W1c_{tag}", [P, 8], es=s2)
            b1c = pr.sb(f"b1c_{tag}", [P, 8], es=s2)
            W2c = pr.sb(f"W2c_{tag}", [P, 8], es=s2)
            binc = pr.sb(f"binc_{tag}", [64, 1], es=s2)
            b2c = pr.sb(f"b2c_{tag}", [64, 1], es=s2)
            bout_c = pr.sb(f"bout_c_{tag}", [P, DB], es=s2)
            bout_half = pr.sb(f"bout_half_{tag}", [P, DB], es=s2)
            for b in range(DB):
                ld(xb[:, b * 64:(b + 1) * 64] if False else
                   WinT_f[:, b * 64:(b + 1) * 64],
                   dd["WinT"][b * P:(b + 1) * P, :])
            ld(WoutT_f[:, :], dd["WoutT"][:, :])
            ld(W1c[0:64, :], dd["W1"][:, 0:8]); ld(W1c[64:128, :], dd["W1"][:, 8:16])
            ld(b1c[0:64, :], dd["b1"][:, 0:8]); ld(b1c[64:128, :], dd["b1"][:, 8:16])
            ld(W2c[0:64, :], dd["W2"][:, 0:8]); ld(W2c[64:128, :], dd["W2"][:, 8:16])
            ld(binc[:, :], dd["bin_"][:, :]); ld(b2c[:, :], dd["b2"][:, :])
            ld(bout_c[:, :], dd["bout_c"][:, :])
            pr.wait("vector", pr.ldsem)
            pr.emit("vector", lambda e: e.tensor_copy(WinT[:, :], WinT_f[:, :]))
            pr.emit("vector", lambda e: e.tensor_copy(WoutT[0:64, :],
                                                      WoutT_f[0:64, :]))
            pr.emit("vector", lambda e: e.tensor_scalar(
                bout_half[:, :], bout_c[:, :], 0.5, None, op0=ALU.mult),
                sem="dve")
            pr.wait("tensor", "dve")
            pr.wait("scalar", "dve")
            sc = dict(xb=xb, sq=sq, rowsB=rowsB, rep=rep, p0=psum[0],
                      p1=psum[1], ones_b=ones_b, ones_f=ones_f,
                      ones_cb=ones_cb, ones_cf=ones_cf, s64=s64, g64=g64,
                      relu=relu, gb=gb, tmp=tmp)
            w = dict(WinT=WinT, W1c=W1c, b1c=b1c, W2c=W2c, binc=binc,
                     b2c=b2c, WoutT=WoutT, bout_half=bout_half)
            emit_ln(pr, xsb, xb, None, sc)
            emit_l0_ffn(pr, xb, xsb, w, sc)
            endphase(tag)
            s2.close()

        def phase_l0_attn():
            s2 = es.enter_context(ExitStack())
            xn = pr.sb("xn_B", [P, DB * T], es=s2)
            WxdtT = pr.sb("WxdtT_B", [P, DB * 64], BF16, es=s2)
            WxBT = pr.sb("WxBT_B", [P, DB * 16], BF16, es=s2)
            WxCT = pr.sb("WxCT_B", [P, DB * 16], BF16, es=s2)
            WdtT = pr.sb("WdtT_B", [64, D], BF16, es=s2)
            bdt_c = pr.sb("bdt_c_B", [P, DB], es=s2)
            A_cols = pr.sb("A_cols_B", [P, DB * 16], es=s2)
            D_c = pr.sb("D_c_B", [P, DB], es=s2)
            s3 = es.enter_context(ExitStack())
            sq = pr.sb("sq_B", [P, 2 * 512], F32, es=s3)
            rowsB = pr.sb("rowsB_B", [1, 4 * T], es=s3)
            rep = pr.sb("rep_B", [P, 2 * T], es=s3)
            wstage = pr.sb("wstage_B", [P, DB * 96], F32, es=s3)
            wstage2 = pr.sb("wstage2_B", [64, D], F32, es=s3)
            for b in range(DB):
                ld(wstage[:, b * 96:(b + 1) * 96], dd["WxT"][b * P:(b + 1) * P, :])
            ld(wstage2[:, :], dd["WdtT"][:, :])
            ld(bdt_c[:, :], dd["bdt_c"][:, :])
            ld(A_cols[:, :], dd["Alog_c"][:, :])
            ld(D_c[:, :], dd["D_c"][:, :])
            pr.wait("vector", pr.ldsem)
            for b in range(DB):
                pr.emit("vector", lambda e, b=b: e.tensor_copy(
                    WxdtT[:, b * 64:(b + 1) * 64], wstage[:, b * 96: b * 96 + 64]))
                pr.emit("vector", lambda e, b=b: e.tensor_copy(
                    WxBT[:, b * 16:(b + 1) * 16], wstage[:, b * 96 + 64: b * 96 + 80]))
                pr.emit("vector", lambda e, b=b: e.tensor_copy(
                    WxCT[:, b * 16:(b + 1) * 16], wstage[:, b * 96 + 80: b * 96 + 96]))
            pr.emit("vector", lambda e: e.tensor_copy(WdtT[0:64, :], wstage2[0:64, :]))
            pr.tag("vector", "dve")
            pr.wait("scalar", pr.ldsem)
            pr.wait("scalar", "dve")
            pr.emit("scalar", lambda e: e.activation(A_cols[:, :], A_cols[:, :],
                                                     AF.Exp), sem="act")
            pr.wait("vector", "act")
            pr.emit("vector", lambda e: e.tensor_scalar(
                A_cols[:, :], A_cols[:, :], -1.0, None, op0=ALU.mult), sem="dve")
            pr.wait("tensor", "dve")
            pr.wait("scalar", "dve")
            sc = dict(xb=None, sq=sq, rowsB=rowsB, rep=rep, p0=psum[0],
                      p1=psum[1], ones_b=ones_b, ones_f=ones_f,
                      ones_cb=ones_cb, ones_cf=ones_cf)
            emit_ln(pr, xsb, None, xn, sc)
            pr.barrier("B_ln_done")
            s3.close()
            xnb_c = pr.sb("xnb_c_B", [P, DB * 512], BF16, es=s2)
            dtF = pr.sb("dtF_B", [P, DB * 512], es=s2)
            dtlo = pr.sb("dtlo_B", [64, 512], BF16, es=s2)
            Bsb = pr.sb("Bsb_B", [16, 512], es=s2)
            Csb = pr.sb("Csb_B", [16, 512], es=s2)
            h_state = pr.sb("h_state_B", [P, P], es=s2)
            sp1 = pr.sb("sp1_B", [P, 512], es=s2)
            sp2 = pr.sb("sp2_B", [P, 512], es=s2)
            a_t = pr.sb("a_t_B", [P, 512], es=s2)
            u2_t = pr.sb("u2_t_B", [P, 512], es=s2)
            hs_t = pr.sb("hs_t_B", [P, 512], es=s2)
            yt_t = pr.sb("yt_t_B", [P, 2 * 512], es=s2)
            brow = pr.sb("brow_B", [1, 4 * 512], es=s2)
            w = dict(WxdtT=WxdtT, WxBT=WxBT, WxCT=WxCT, WdtT=WdtT,
                     bdt_c=bdt_c, A_cols=A_cols, D_c=D_c)
            sc2 = dict(xnb_c=xnb_c, dtF=dtF, dtlo=dtlo, Bsb=Bsb, Csb=Csb,
                       h_state=h_state, sp1=sp1, sp2=sp2, a_t=a_t, u2_t=u2_t,
                       hs_t=hs_t, yt_t=yt_t, brow=brow, ones_f=ones_f,
                       p0=psum[0], p1=psum[1], p2=psum[2], p3=psum[3],
                       pB=psum[4], pC=psum[5])
            emit_l0_attn(pr, xsb, xn, w, sc2)
            endphase("B_done")
            s2.close()

        def phase_moe(tag):
            s2 = es.enter_context(ExitStack())
            xb = pr.sb(f"xb_{tag}", [P, DB * T], BF16, es=s2)
            ohb = pr.sb(f"ohb_{tag}", [NEXP, NEXP * P], BF16, es=s2)
            msel_b = pr.sb(f"msel_b_{tag}", [NEXP, T], BF16, es=s2)
            ones14 = pr.sb(f"ones14_{tag}", [P, NEXP], es=s2)
            s3 = es.enter_context(ExitStack())
            sq = pr.sb(f"sq_{tag}", [P, 2 * 512], BF16, es=s3)
            rowsB = pr.sb(f"rowsB_{tag}", [1, 4 * T], es=s3)
            rep = pr.sb(f"rep_{tag}", [P, 2 * T], es=s3)
            rwT = pr.sb(f"rwT_{tag}", [P, DB * NEXP], es=s3)
            colsum_n = pr.sb(f"colsum_{tag}", [NEXP, 1], es=s3)
            lgs = pr.sb(f"lgs_{tag}", [NEXP, T], es=s3)
            lgT = pr.sb(f"lgT_{tag}", [P, 16 * NEXP], es=s3)
            red = pr.sb(f"red_{tag}", [P, 8], es=s3)
            mt = pr.sb(f"mt_{tag}", [P, NEXP], es=s3)
            cums = pr.sb(f"cums_{tag}", [P, NEXP], es=s3)
            msel = pr.sb(f"msel_{tag}", [NEXP, T], es=s3)
            for b in range(DB):
                ld(rwT[:, b * NEXP:(b + 1) * NEXP], dd["rwT"][b * P:(b + 1) * P, :])
            ld(rep[0:NEXP, 0:NEXP * P], dd["onehot14"][:, :])
            pr.wait("vector", pr.ldsem)
            pr.emit("vector", lambda e: e.tensor_copy(
                ohb[:, :], rep[0:NEXP, 0:NEXP * P]))
            pr.emit("vector", lambda e: e.memset(ones14[:, :], 1.0), sem="dve")
            pr.wait("tensor", "dve")
            pr.wait("scalar", "dve")
            sc = dict(xb=xb, sq=sq, rowsB=rowsB, rep=rep, p0=psum[0],
                      p1=psum[1], ones_b=ones_b, ones_f=ones_f,
                      ones_cb=ones_cb, ones_cf=ones_cf)
            emit_ln(pr, xsb, xb, None, sc)
            r = dict(rwT=rwT, colsum_n=colsum_n, lgs=lgs, lgT=lgT, red=red,
                     mt=mt, cums=cums, ones14=ones14, msel=msel,
                     msel_b=msel_b, identf=identf, ones_f=ones_f,
                     p2=psum[2], p3=psum[3])
            emit_router(pr, xsb, sc, r)
            pr.barrier(f"router_done_{tag}")
            s3.close()
            w1s = pr.sb(f"w1s_{tag}", [P, 8 * T], BF16, es=s2)
            w2s = pr.sb(f"w2s_{tag}", [P, 16 * 1024], BF16, es=s2)
            w1f = pr.sb(f"w1f_{tag}", [P, 2 * 1024], es=s2)
            b1c = pr.sb(f"b1cs_{tag}", [P, NEXP * 16], es=s2)
            b2c = pr.sb(f"b2cs_{tag}", [P, NEXP * 8], es=s2)
            mselr = pr.sb(f"mselr_{tag}", [P, 2 * T], BF16, es=s2)
            hid = pr.sb(f"hid_{tag}", [P, 16 * 512], BF16, es=s2)
            t1 = pr.sb(f"t1_{tag}", [P, 4 * 512], BF16, es=s2)
            mm = dict(w1s=w1s, w2s=w2s, w1f=w1f, b1c=b1c, b2c=b2c,
                      mselr=mselr, hid=hid, t1=t1, msel_b=msel_b, ohb=ohb,
                      ph0=psum[4], ph1=psum[5], py0=psum[6], py1=psum[7],
                      p2=psum[2], p0=psum[0], p1=psum[1], p2b=psum[2],
                      p3b=psum[3])
            dram = dict(d_w1=dd["w1"], d_w2=dd["w2"], d_b1c=dd["b1c"],
                        d_b2c=dd["b2c"], d_w1b=d_w1b, d_w2b=d_w2b)
            emit_moe_experts(pr, xb, xsb, mm, dram,
                             cache="out" if tag == "Dx" else "in")
            pr.wait("vector", "yacc")
            endphase(tag)
            s2.close()

        def phase_l1_attn():
            s2 = es.enter_context(ExitStack())
            znb = pr.sb("znb_E", [P, DB * T], BF16, es=s2)
            inb_c = pr.sb("inb_c_E", [P, IBL], es=s2)
            gateb_c = pr.sb("gateb_c_E", [P, IBL], es=s2)
            bb_c = pr.sb("bb_c_E", [P, IBL], es=s2)
            cb_c = pr.sb("cb_c_E", [P, IBL], es=s2)
            db_c = pr.sb("db_c_E", [P, IBL], es=s2)
            outb_c = pr.sb("outb_c_E", [P, DB], es=s2)
            a_dec = pr.sb("a_dec_E", [P, IBL], es=s2)
            normw_c = pr.sb("normw_c_E", [P, DB], es=s2)
            asp = pr.sb("asp_E", [P, 2 * IBL], es=s2)
            ld(inb_c[:, :], dd["inb_c"][:, :])
            ld(gateb_c[:, :], dd["gateb_c"][:, :])
            ld(bb_c[:, :], dd["bb_c"][:, :])
            ld(cb_c[:, :], dd["cb_c"][:, :])
            ld(db_c[:, :], dd["db_c"][:, :])
            ld(outb_c[:, :], dd["outb_c"][:, :])
            ld(a_dec[:, :], dd["alog_c"][:, :])
            ld(normw_c[:, :], dd["normw_c"][:, :])

            def ss():
                pr.tag("scalar", "act")
                pr.wait("scalar", "act")

            pr.wait("scalar", pr.ldsem)
            pr.emit("scalar", lambda e: e.activation(asp[:, 0:IBL], a_dec[:, :],
                                                     AF.Relu))
            ss()
            pr.emit("scalar", lambda e: e.activation(asp[:, IBL:2 * IBL],
                                                     a_dec[:, :], AF.Abs))
            ss()
            pr.emit("scalar", lambda e: e.activation(
                asp[:, IBL:2 * IBL], asp[:, IBL:2 * IBL], AF.Exp, scale=-1.0))
            ss()
            pr.emit("scalar", lambda e: e.activation(
                asp[:, IBL:2 * IBL], asp[:, IBL:2 * IBL], AF.Ln, bias=1.0),
                sem="act")
            pr.wait("vector", "act")
            pr.emit("vector", lambda e: e.tensor_add(
                asp[:, 0:IBL], asp[:, 0:IBL], asp[:, IBL:2 * IBL]), sem="dve")
            pr.wait("scalar", "dve")
            pr.emit("scalar", lambda e: e.activation(
                a_dec[:, :], asp[:, 0:IBL], AF.Exp, scale=-1.0), sem="act")
            pr.wait("vector", pr.ldsem)
            pr.wait("tensor", pr.ldsem)
            pr.wait("vector", "act")
            pr.wait("tensor", "act")

            s3 = es.enter_context(ExitStack())
            sq = pr.sb("sq_E", [P, 2 * 512], F32, es=s3)
            rowsB = pr.sb("rowsB_E", [1, 4 * T], es=s3)
            rep = pr.sb("rep_E", [P, 3 * T], es=s3)
            tmpb = pr.sb("tmpb_E", [P, T], es=s3)
            sc = dict(sq=sq, rowsB=rowsB, rep=rep, tmpb=tmpb, ones_cf=ones_cf,
                      ones_f=ones_f, normw_c=normw_c, p0=psum[0], p1=psum[1])
            emit_ln_zres(pr, xsb, znb, normw_c, sc)
            pr.barrier("E_zres_done")
            s3.close()
            u_bf = pr.sb("u_bf_E", [P, IBL * T], BF16, es=s2)
            s4 = es.enter_context(ExitStack())
            wstg1 = pr.sb("wstg1_E", [P, 2 * 4 * P], F32, es=s4)
            wcol_in = pr.sb("wcol_in_E", [P, 2 * DB * P], BF16, es=s4)
            wcol_g = pr.sb("wcol_g_E", [P, 2 * DB * P], BF16, es=s4)
            t2 = pr.sb("t2_E", [P, 512], F32, es=s4)
            t3 = pr.sb("t3_E", [P, 512], F32, es=s4)
            w1 = dict(d_inwT=dd["inwT"], d_gatewT=dd["gatewT"], inb_c=inb_c,
                      gateb_c=gateb_c)
            sc1 = dict(wstg=wstg1, wcol_in=wcol_in, wcol_g=wcol_g, t2=t2,
                       t3=t3, pu=psum[0], pg=psum[1])
            emit_l1_attn_e1(pr, znb, u_bf, w1, sc1)
            pr.barrier("E_e1_done")
            s4.close()
            wstg = pr.sb("wstg_E", [P, 2 * 4 * P], es=s2)
            wb = pr.sb("wb_E", [P, IBL * P], BF16, es=s2)
            wc = pr.sb("wc_E", [P, IBL * P], BF16, es=s2)
            wd = pr.sb("wd_E", [P, IBL * P], BF16, es=s2)
            wo = pr.sb("wo_E", [P, 1024], BF16, es=s2)
            bq = pr.sb("bq_E", [P, 1024], es=s2)
            cq = pr.sb("cq_E", [P, 1024], es=s2)
            dq = pr.sb("dq_E", [P, 1024], es=s2)
            hsq = pr.sb("hsq_E", [P, 1024], es=s2)
            a_rep = pr.sb("a_rep_E", [P, 1024], es=s2)
            onesf_t = pr.sb("onesf_t_E", [P, 1024], es=s2)
            carry = pr.sb("carry_E", [P, IBL], es=s2)
            y_bf = pr.sb("y_bf_E", [P, 1024], BF16, es=s2)
            t1b = pr.sb("t1b_E", [P, 512], BF16, es=s2)
            pr.emit("vector", lambda e: e.memset(onesf_t[:, :], 1.0), sem="dve")
            w2 = dict(d_bwT=dd["bwT"], d_cwT=dd["cwT"], d_dwT=dd["dwT"],
                      d_outwT=dd["outwT"], bb_c=bb_c, cb_c=cb_c, db_c=db_c,
                      outb_c=outb_c, a_dec=a_dec)
            sc2 = dict(wstg=wstg, wb=wb, wc=wc, wd=wd, wo=wo, bq=bq, cq=cq,
                       dq=dq, hsq=hsq, a_rep=a_rep, onesf_t=onesf_t,
                       carry=carry, y_bf=y_bf, oacc=znb, t1b=t1b,
                       pb_=psum[0], pc_=psum[1], pd_=psum[2],
                       po_=[psum[3], psum[4]])
            emit_l1_attn_e2(pr, u_bf, xsb, w2, sc2)
            endphase("E_done")
            s2.close()

        pr.scope = "phA"; pr.ldsem = "ld_A"; phase_l0_ffn("A")
        pr.scope = "phB"; pr.ldsem = "ld_B"; phase_l0_attn()
        pr.scope = "phC"; pr.ldsem = "ld_C"; phase_l0_ffn("C")
        pr.scope = "phD"; pr.ldsem = "ld_D"; phase_moe("Dx")
        pr.scope = "phE"; pr.ldsem = "ld_E"; phase_l1_attn()
        pr.scope = "phF"; pr.ldsem = "ld_F"; phase_moe("Fx")
        pr.scope = None

        pr.wait("sync", "dve")
        for b in range(DB):
            pr.emit("sync", lambda e, b=b: e.dma_start(
                out=x_out[b * P:(b + 1) * P, :],
                in_=xsb[:, b * T:(b + 1) * T]), sem="st", amt=16)
        pr.wait("sync", "st")
        pr.finalize()
    return nc


def prep_fused_inputs(inputs):
    m = {}
    m.update(prep_l0_ffn_inputs(inputs))
    m.update(prep_l0_attn_inputs(inputs))
    m.update(prep_moe_inputs(inputs))
    m.update(prep_l1_attn_inputs(inputs))
    return m


def kernel(**inputs):
    x = inputs["L0_attn_x"]
    subs = globals().get("_SUBLAYERS", "FUSED")
    xt = [np.ascontiguousarray(x[c].T.astype(np.float32))
          for c in range(N_CORES)]
    if subs == "FUSED":
        nc = build_fused_program()
        base = prep_fused_inputs(inputs)
        in_maps = []
        for c in range(N_CORES):
            m = dict(base)
            m["x_in"] = xt[c]
            in_maps.append(m)
        res = run_bass_kernel_spmd(nc, in_maps, core_ids=list(range(N_CORES)))
        out = np.stack([res.results[c]["x_out"].T for c in range(N_CORES)])
        return out.astype(np.float32)
    for s in subs:
        if s in "AC":
            nc = build_l0_ffn_program()
            base = prep_l0_ffn_inputs(inputs)
        elif s == "B":
            nc = build_l0_attn_program()
            base = prep_l0_attn_inputs(inputs)
        elif s == "E":
            nc = build_l1_attn_program()
            base = prep_l1_attn_inputs(inputs)
        elif s in "DF":
            nc = build_moe_program()
            base = prep_moe_inputs(inputs)
        else:
            raise ValueError(s)
        in_maps = []
        for c in range(N_CORES):
            m = dict(base)
            m["x_in"] = xt[c]
            in_maps.append(m)
        res = run_bass_kernel_spmd(nc, in_maps, core_ids=list(range(N_CORES)))
        xt = [np.ascontiguousarray(res.results[c]["x_out"])
              for c in range(N_CORES)]
    out = np.stack([xt[c].T for c in range(N_CORES)])
    return out.astype(np.float32)
